# version 12
# speedup vs baseline: 1.7967x; 1.4379x over previous
import numpy as np
from contextlib import ExitStack

import concourse.bass as bass
import concourse.mybir as mybir
from concourse.bass_utils import run_bass_kernel_spmd
from concourse.alu_op_type import AluOpType as OP

F32 = mybir.dt.float32
BF16 = mybir.dt.bfloat16
AF = mybir.ActivationFunctionType
AX = mybir.AxisListType.X
MAGIC = 12582912.0
NEG_INF = -1.0e30

N_CORES = 8
T = 4096
E = 1024
S = 2048
DR = 128
INNER = 4096
TG = 512
NTJ = TG // 128

ENGS = ("tensor", "vector", "scalar", "gpsimd", "sync")


class Prog:
    def __init__(self, nc, stack):
        self.nc = nc
        self.auto_sep = True
        self.junk = {}
        self.stream = {e: [] for e in ENGS}
        self.sems = {}
        self.cnt = {}
        self.memo = {e: {} for e in ENGS}
        for name in ("pe", "dve", "act", "gp", "bar") + ('dx0', 'dx1', 'dx2', 'dx3', 'dw0', 'dw1', 'du0', 'du1', 'dd0', 'dd1', 'dm', 'dr', 'dc'):
            self.sems[name] = stack.enter_context(nc.semaphore(name))
            self.cnt[name] = 0

    def wait(self, eng, sem, val=None):
        v = self.cnt[sem] if val is None else val
        if v <= 0:
            return
        if self.memo[eng].get(sem, -1) >= v:
            return
        self.memo[eng][sem] = v
        h = self.sems[sem]
        self.stream[eng].append(lambda e, h=h, v=v: e.wait_ge(h, v))

    def waitt(self, eng, tok):
        if tok is not None:
            self.wait(eng, tok[0], tok[1])

    def op(self, eng, fn, inc=None, k=1):
        if inc is None:
            self.stream[eng].append(fn)
            self._sep(eng)
            return None
        self.cnt[inc] += k
        tok = (inc, self.cnt[inc])
        h = self.sems[inc]

        def wrapped(e, fn=fn, h=h, k=k):
            fn(e).then_inc(h, k)
        self.stream[eng].append(wrapped)
        self._sep(eng)
        return tok

    def _sep(self, eng):
        if not self.auto_sep or eng not in ("vector",):
            return
        j = self.junk.get(eng)
        if j is None:
            return
        if eng == "vector":
            self.stream[eng].append(lambda v, j=j: v.tensor_scalar(
                out=j[:], in0=j[:], scalar1=0.0, scalar2=None, op0=OP.mult))
        elif eng == "scalar":
            self.stream[eng].append(lambda s_, j=j: s_.activation(
                out=j[:], in_=j[:], func=AF.Identity, scale=0.0))
        else:
            self.stream[eng].append(lambda g, j=j: g.memset(j[:], 0.0))

    def mark(self, sem):
        return (sem, self.cnt[sem])

    DMA_SEMS = ['dx0', 'dx1', 'dx2', 'dx3', 'dw0', 'dw1', 'du0', 'du1', 'dd0', 'dd1', 'dm', 'dr', 'dc']

    def barrier(self):
        for ds in self.DMA_SEMS:
            if self.cnt[ds] > 0:
                self.wait("sync", ds)
                self.wait("gpsimd", ds)
        for e in ENGS:
            self.op(e, lambda en: en.drain(), inc="bar")
        for e in ENGS:
            self.wait(e, "bar")


def ceil_div(a, b):
    return (a + b - 1) // b


def rowtiles(hbm_ap, t0, t1, width):
    return hbm_ap.rearrange("(t p) d -> p t d", p=128)[:, t0:t1, :]


def sb3(sb_ap, nt, width):
    return sb_ap.rearrange("p (t d) -> p t d", t=nt)


def build_program(stage=3, paranoid=0):
    nc = bass.Bass("TRN2", target_bir_lowering=False)

    d_in = {}
    d_in["x"] = nc.dram_tensor("x", [T, E], F32, kind="ExternalInput").ap()
    for nm, shp in [("wg", [INNER, E]), ("wu", [INNER, E]), ("wd", [E, INNER]),
                    ("wlat", [E, 768]), ("wqc", [256, 896]), ("wkc", [256, 896]),
                    ("wv", [256, E]), ("wo", [E, E]),
                    ("blat_p", [128, 6]), ("bqc_pp", [128, 7]), ("bkc_pp", [128, 7]),
                    ("bv_b", [128, E]), ("bo_b", [128, E]),
                    ("cosT", [DR, S]), ("sinT", [DR, S])]:
        d_in[nm] = nc.dram_tensor(nm, shp, F32, kind="ExternalInput").ap()
    y_out = nc.dram_tensor("y", [T, E], F32, kind="ExternalOutput").ap()
    x1 = nc.dram_tensor("x1", [T, E], F32, kind="Internal").ap()
    sg = nc.dram_tensor("sg", [16, 128, 2048], BF16, kind="Internal").ap()
    su = nc.dram_tensor("su", [16, 128, 2048], BF16, kind="Internal").ap()
    sd = nc.dram_tensor("sd", [8, 128, 4096], BF16, kind="Internal").ap()

    with ExitStack() as stack:
        EC = stack.enter_context
        P = Prog(nc, stack)

        magic = EC(nc.sbuf_tensor("magic", [128, 1], F32))
        ident = EC(nc.sbuf_tensor("ident", [128, 128], F32))
        identb = EC(nc.sbuf_tensor("identb", [128, 128], BF16))
        maskbigT = EC(nc.sbuf_tensor("maskbigT", [128, 896], BF16))
        maskTw = [maskbigT[:, 384 - a * 128: 896 - a * 128] for a in range(4)]
        ones = EC(nc.sbuf_tensor("ones", [128, 128], F32))
        wmean = EC(nc.sbuf_tensor("wmean", [128, 8], F32))
        ln127 = EC(nc.sbuf_tensor("ln127", [128, 1], F32))
        dbg = EC(nc.sbuf_tensor("dbg", [128, 16], F32))
        junkv = EC(nc.sbuf_tensor("junkv", [128, 1], F32))
        junks = EC(nc.sbuf_tensor("junks", [128, 1], F32))
        if stage == 5:
            dbg_xq = EC(nc.sbuf_tensor("dbg_xq", [128, 512], F32))
            dbg_vq = EC(nc.sbuf_tensor("dbg_vq", [128, 512], F32))
            dbg_prod = EC(nc.sbuf_tensor("dbg_prod", [128, 256], F32))
            dbg_wg = EC(nc.sbuf_tensor("dbg_wg", [128, 512], F32))
            dbg_sc = EC(nc.sbuf_tensor("dbg_sc", [128, 40], F32))

        junkg = EC(nc.sbuf_tensor("junkg", [128, 1], F32))
        P.junk = {"vector": junkv, "scalar": junks, "gpsimd": junkg}

        if paranoid == 30:
            dbg_xq = EC(nc.sbuf_tensor("dbg_xq", [128, 256], F32))
            dbg_vq = EC(nc.sbuf_tensor("dbg_vq", [128, 128], F32))
            dbg_sc = EC(nc.sbuf_tensor("dbg_sc", [128, 8], F32))

        def sepv():
            P.op("vector", lambda v: v.tensor_scalar(
                out=junkv[:], in0=junkv[:], scalar1=0.0, scalar2=None, op0=OP.mult))

        def seps():
            P.op("scalar", lambda s: s.activation(
                out=junks[:], in_=junks[:], func=AF.Identity, scale=0.0))
        PS = [EC(nc.psum_tensor(f"ps{i}", [128, 512], F32)) for i in range(8)]
        bankfree = {i: None for i in range(8)}

        P.op("gpsimd", lambda g: g.memset(magic[:], MAGIC))
        P.op("gpsimd", lambda g: g.memset(ln127[:], float(np.log(127.0))))
        P.op("gpsimd", lambda g: g.memset(junkv[:], 1.0))
        P.op("gpsimd", lambda g: g.memset(junks[:], 1.0))
        P.op("gpsimd", lambda g: g.memset(ident[:], 0.0))
        P.op("gpsimd", lambda g: g.affine_select(
            out=ident[:], in_=ident[:], compare_op=OP.not_equal, fill=1.0,
            base=0, pattern=[[-1, 128]], channel_multiplier=1))
        P.op("gpsimd", lambda g: g.memset(identb[:], 0.0))
        P.op("gpsimd", lambda g: g.affine_select(
            out=identb[:], in_=identb[:], compare_op=OP.not_equal, fill=1.0,
            base=0, pattern=[[-1, 128]], channel_multiplier=1))
        P.op("gpsimd", lambda g: g.memset(maskbigT[:], 0.0))
        P.op("gpsimd", lambda g: g.memset(maskbigT[:, 0:384], NEG_INF))
        P.op("gpsimd", lambda g: g.affine_select(
            out=maskbigT[:, 384:512], in_=maskbigT[:, 384:512],
            compare_op=OP.is_ge, fill=NEG_INF,
            base=0, pattern=[[1, 128]], channel_multiplier=-1))
        tok_consts = P.op("gpsimd", lambda g: g.memset(ones[:], 1.0), inc="gp")
        for e in ("tensor", "scalar", "vector"):
            P.waitt(e, tok_consts)

        def transpose_tile(src_ap, dst_ap, banks, src_tok, pre_evac_tok=None):
            b = banks[transpose_tile.rr % len(banks)]
            transpose_tile.rr += 1
            P.waitt("tensor", bankfree[b])
            P.waitt("tensor", src_tok)
            ptok = P.op("tensor", lambda t, b=b, s=src_ap: t.transpose(
                PS[b][:, 0:128], s, ident[:]), inc="pe")
            P.waitt("vector", ptok)
            if pre_evac_tok is not None:
                P.waitt("vector", pre_evac_tok)
            etok = P.op("vector", lambda v, b=b, d=dst_ap: v.tensor_copy(
                d, PS[b][:, 0:128]), inc="dve")
            bankfree[b] = etok
            return etok
        transpose_tile.rr = 0

        with ExitStack() as prep:
            PE_ = prep.enter_context
            wtile = [PE_(nc.sbuf_tensor(f"wt{i}", [128, 4096], F32)) for i in range(2)]
            qtmp = PE_(nc.sbuf_tensor("qtmp", [128, 4096], F32))
            tern = PE_(nc.sbuf_tensor("tern", [128, 4096], F32))
            stg_g = [PE_(nc.sbuf_tensor(f"stg{i}", [128, 2048], BF16)) for i in range(2)]
            stg_d = [PE_(nc.sbuf_tensor(f"std{i}", [128, 4096], BF16)) for i in range(2)]
            acc = PE_(nc.sbuf_tensor("acc", [128, 4], F32))
            rtmp = PE_(nc.sbuf_tensor("rtmp", [128, 8], F32))

            wt_free = [None, None]
            mats = [("g", d_in["wg"]), ("u", d_in["wu"]), ("d", d_in["wd"])]
            for mi, (nm, wap) in enumerate(mats):
                last = None
                for tix in range(8):
                    sl = tix % 2
                    P.waitt("sync", wt_free[sl])
                    if nm == "d":
                        src = wap[tix * 128:(tix + 1) * 128, :]
                        dstv = wtile[sl][:]
                    else:
                        src = rowtiles(wap, tix * 4, tix * 4 + 4, 1024)
                        dstv = sb3(wtile[sl][:], 4, 1024)
                    ltok = P.op("sync", lambda s, src=src, dstv=dstv: s.dma_start(
                        out=dstv, in_=src), inc=f"dx{sl}", k=16)
                    P.waitt("vector", ltok)
                    rt = P.op("vector", lambda v, sl=sl, tix=tix: v.tensor_reduce(
                        rtmp[:, tix:tix + 1], wtile[sl][:], axis=AX, op=OP.add,
                        apply_absolute_value=True), inc="dve")
                    wt_free[sl] = rt
                P.op("vector", lambda v, mi=mi: v.tensor_copy(
                    dbg[:, 8 * (mi % 2):8 * (mi % 2) + 8], rtmp[:, 0:8]), inc="dve")
                last = P.op("vector", lambda v, mi=mi: v.tensor_reduce(
                    acc[:, mi:mi + 1], rtmp[:, 0:8], axis=AX, op=OP.add), inc="dve")
                P.waitt("tensor", bankfree[mi])
                P.waitt("tensor", last)
                mtok = P.op("tensor", lambda t, mi=mi: t.matmul(
                    PS[mi][:, 0:1], ones[:], acc[:, mi:mi + 1], start=True, stop=True), inc="pe")
                P.waitt("vector", mtok)
                P.op("vector", lambda v, mi=mi: v.tensor_scalar(
                    out=wmean[:, mi:mi + 1], in0=PS[mi][:, 0:1], scalar1=1.0 / (INNER * E),
                    scalar2=1e-5, op0=OP.mult, op1=OP.max), inc="dve")
                mtok2 = P.mark("dve")
                P.waitt("scalar", mtok2)
                P.op("scalar", lambda s, mi=mi: s.activation(
                    out=acc[:, 3:4], in_=wmean[:, mi:mi + 1], func=AF.Ln), inc="act")
                rtok = P.op("scalar", lambda s, mi=mi: s.activation(
                    out=wmean[:, 3 + mi:4 + mi], in_=acc[:, 3:4], func=AF.Exp,
                    scale=-1.0), inc="act")
                P.waitt("vector", rtok)
                P.op("vector", lambda v, mi=mi: v.tensor_tensor(
                    out=acc[:, 3:4], in0=wmean[:, mi:mi + 1], in1=wmean[:, 3 + mi:4 + mi],
                    op=OP.mult), inc="dve")
                P.op("vector", lambda v: v.tensor_scalar(
                    out=acc[:, 3:4], in0=acc[:, 3:4], scalar1=-1.0, scalar2=2.0,
                    op0=OP.mult, op1=OP.add), inc="dve")
                etok = P.op("vector", lambda v, mi=mi: v.tensor_tensor(
                    out=wmean[:, 3 + mi:4 + mi], in0=wmean[:, 3 + mi:4 + mi], in1=acc[:, 3:4],
                    op=OP.mult), inc="dve")
                bankfree[mi] = etok
            P.op("vector", lambda v: v.tensor_copy(wmean[:, 6:7], acc[:, 0:1]), inc="dve")
            P.op("vector", lambda v: v.tensor_copy(wmean[:, 7:8], acc[:, 2:3]), inc="dve")
            means_tok = P.mark("dve")

            def quant_tile(sl, mi, width, ld_tok):
                P.waitt("scalar", ld_tok)
                P.waitt("scalar", means_tok)
                P.waitt("scalar", quant_tile.tern_free)
                atok = P.op("scalar", lambda s, sl=sl, mi=mi, w=width: s.activation(
                    out=qtmp[:, 0:w], in_=wtile[sl][:, 0:w], func=AF.Identity,
                    bias=magic[:, 0:1], scale=wmean[:, 3 + mi:4 + mi]), inc="act")
                P.waitt("vector", atok)
                P.op("vector", lambda v, w=width: v.tensor_scalar(
                    out=qtmp[:, 0:w], in0=qtmp[:, 0:w], scalar1=MAGIC, scalar2=-1.0,
                    op0=OP.subtract, op1=OP.max), inc="dve")
                return P.op("vector", lambda v, w=width: v.tensor_scalar(
                    out=tern[:, 0:w], in0=qtmp[:, 0:w], scalar1=1.0, scalar2=None,
                    op0=OP.min), inc="dve")
            quant_tile.tern_free = None

            stg_free = [None, None]
            for mi, scratch in [(0, sg), (1, su)]:
                wap = d_in["wg"] if mi == 0 else d_in["wu"]
                for j in range(16):
                    ssl = j % 2
                    for half in range(2):
                        r = 2 * j + half
                        sl = r % 2
                        P.waitt("sync", wt_free[sl])
                        src = wap[r * 128:(r + 1) * 128, :]
                        ltok = P.op("sync", lambda s, sl=sl, src=src: s.dma_start(
                            out=wtile[sl][:, 0:1024], in_=src), inc=f"dx{sl}", k=16)
                        ttok = quant_tile(sl, mi, 1024, ltok)
                        wt_free[sl] = ttok
                        for k in range(8):
                            transpose_tile(
                                tern[:, k * 128:(k + 1) * 128],
                                stg_g[ssl][:, k * 256 + half * 128: k * 256 + half * 128 + 128],
                                [6, 7], ttok, pre_evac_tok=stg_free[ssl])
                        quant_tile.tern_free = P.mark("pe")
                    lastev = P.mark("dve")
                    P.waitt("sync", lastev)
                    sttok = P.op("sync", lambda g, ssl=ssl, j=j, scratch=scratch: g.dma_start(
                        out=scratch[j], in_=stg_g[ssl][:]), inc=f"dw{ssl}", k=16)
                    stg_free[ssl] = sttok
            std_free = [None, None]
            for r in range(8):
                sl = r % 2
                ssl = r % 2
                P.waitt("sync", wt_free[sl])
                src = d_in["wd"][r * 128:(r + 1) * 128, :]
                ltok = P.op("sync", lambda s, sl=sl, src=src: s.dma_start(
                    out=wtile[sl][:], in_=src), inc=f"dx{sl}", k=16)
                ttok = quant_tile(sl, 2, 4096, ltok)
                wt_free[sl] = ttok
                for kc in range(32):
                    transpose_tile(tern[:, kc * 128:(kc + 1) * 128],
                                   stg_d[ssl][:, kc * 128:(kc + 1) * 128], [6, 7], ttok,
                                   pre_evac_tok=std_free[ssl])
                quant_tile.tern_free = P.mark("pe")
                lastev = P.mark("dve")
                P.waitt("sync", lastev)
                stok = P.op("sync", lambda g, r=r, ssl=ssl: g.dma_start(
                    out=sd[r], in_=stg_d[ssl][:]), inc=f"du{ssl}", k=16)
                std_free[ssl] = stok
            P.barrier()
            bankfree.update({i: None for i in range(8)})

        def emit_ffn(call, dbg_dump=False):
            def pbar(bit=0):
                if 0 < paranoid < 256 and paranoid not in (10, 20) and (paranoid >> bit) & 1:
                    P.barrier()
            src = d_in["x"] if call in (0, 2) else x1
            dst = x1 if call in (0, 5) else y_out
            if call == 5:
                call = 0
            with ExitStack() as fs:
                FE = fs.enter_context
                xtile = [FE(nc.sbuf_tensor(f"f{call}xt{i}", [128, 1024], F32)) for i in range(4)]
                sqtr = FE(nc.sbuf_tensor(f"f{call}sqtr", [128, 1024], BF16))
                xc = [FE(nc.sbuf_tensor(f"f{call}xc{i}", [128, 1024], F32)) for i in range(4)]
                xq = [FE(nc.sbuf_tensor(f"f{call}xq{i}", [128, 1024], F32)) for i in range(2)]
                xqT = FE(nc.sbuf_tensor(f"f{call}xqT", [128, NTJ * 8 * 128], BF16))
                prod = FE(nc.sbuf_tensor(f"f{call}prod", [128, NTJ * 4096], F32))
                qv = FE(nc.sbuf_tensor(f"f{call}qv", [128, 2048], F32))
                vq = FE(nc.sbuf_tensor(f"f{call}vq", [128, 4096], F32))
                vqT = FE(nc.sbuf_tensor(f"f{call}vqT", [128, NTJ * 32 * 128], BF16))
                wbg = [FE(nc.sbuf_tensor(f"f{call}wbg{i}", [128, 2048], BF16)) for i in range(2)]
                wbu = [FE(nc.sbuf_tensor(f"f{call}wbu{i}", [128, 2048], BF16)) for i in range(2)]
                wbd = [FE(nc.sbuf_tensor(f"f{call}wbd{i}", [128, 4096], BF16)) for i in range(2)]
                sgb = FE(nc.sbuf_tensor(f"f{call}sgb", [128, 512], F32))
                sc = FE(nc.sbuf_tensor(f"f{call}sc", [128, NTJ * 16], F32))
                vep = FE(nc.sbuf_tensor(f"f{call}vep", [128, 2 * NTJ], F32))
                qsc = FE(nc.sbuf_tensor(f"f{call}qsc", [128, 24], F32))

                xfree = [None] * 4
                wbg_free = [None, None]
                wbu_free = [None, None]
                wbd_free = [None, None]
                xc_free = [None] * 4
                qvx_free = [None, None]
                sgb_free = [None, None]
                act_guard = None
                pe_guard = None
                SSQ, SX, MEAN, NMEAN, RMX, SXQ, GSC, USC, RSTD, T0, T1, VMX, SVQ, DSC, SCR = range(15)

                def col(tj, c):
                    return sc[:, tj * 16 + c: tj * 16 + c + 1]

                tg_order = list(reversed(range(8))) if paranoid == 20 else list(range(8))
                for tg in tg_order:
                    pbar(0)
                    row0 = tg * TG
                    xq_rdy = [None] * NTJ
                    P.waitt("vector", act_guard)
                    P.waitt("vector", pe_guard)
                    for tj in range(NTJ):
                        P.waitt("sync", xfree[tj])
                        ltok = P.op("sync", lambda s, tj=tj, r=row0 + tj * 128, src=src: s.dma_start(
                            out=xtile[tj][:], in_=src[r:r + 128, :]), inc=f"dx{tj}", k=16)
                        if paranoid == 11 and tg == 0 and tj == 0:
                            P.waitt("vector", ltok)
                            P.op("vector", lambda v: v.tensor_copy(dbg_vq[:], xtile[0][:, 0:512]), inc="dve")
                        if paranoid == 11:
                            continue
                        if dbg_dump and tg == 0 and tj == 0:
                            P.waitt("vector", ltok)
                            P.op("vector", lambda v: v.tensor_copy(dbg_vq[:], xtile[0][:, 0:512]), inc="dve")
                            for _ in range(200):
                                P.op("vector", lambda v: v.tensor_scalar(
                                    out=junkv[:], in0=junkv[:], scalar1=0.0, scalar2=None, op0=OP.mult))
                            P.op("vector", lambda v: v.tensor_copy(dbg_prod[:], xtile[0][:, 0:256]), inc="dve")
                            for _ in range(1000):
                                P.op("vector", lambda v: v.tensor_scalar(
                                    out=junkv[:], in0=junkv[:], scalar1=0.0, scalar2=None, op0=OP.mult))
                            P.op("vector", lambda v: v.tensor_copy(dbg_xq[:], xtile[0][:, 0:512]), inc="dve")
                        P.waitt("scalar", ltok)
                        stok = P.op("scalar", lambda s, tj=tj: s.activation(
                            out=sqtr[:], in_=xtile[tj][:], func=AF.Square,
                            accum_out=col(tj, SSQ)), inc="act")
                        P.waitt("vector", ltok)
                        P.op("vector", lambda v, tj=tj: v.tensor_reduce(
                            col(tj, SX), xtile[tj][:], axis=AX, op=OP.add), inc="dve")
                        P.waitt("vector", stok)
                        sepv()
                        P.op("vector", lambda v, tj=tj: v.tensor_scalar(
                            out=col(tj, MEAN), in0=col(tj, SX), scalar1=1.0 / 1024, scalar2=None,
                            op0=OP.mult), inc="dve")
                        P.op("vector", lambda v, tj=tj: v.tensor_scalar(
                            out=col(tj, T0), in0=col(tj, SSQ), scalar1=1.0 / 1024, scalar2=None,
                            op0=OP.mult), inc="dve")
                        P.op("vector", lambda v, tj=tj: v.tensor_tensor(
                            out=col(tj, T1), in0=col(tj, MEAN), in1=col(tj, MEAN), op=OP.mult), inc="dve")
                        P.op("vector", lambda v, tj=tj: v.tensor_tensor(
                            out=col(tj, T0), in0=col(tj, T0), in1=col(tj, T1), op=OP.subtract), inc="dve")
                        P.op("vector", lambda v, tj=tj: v.tensor_scalar(
                            out=qsc[:, 4 + tj:5 + tj], in0=col(tj, T0), scalar1=1e-5, scalar2=None,
                            op0=OP.add), inc="dve")
                        nmtok = P.op("vector", lambda v, tj=tj: v.tensor_scalar(
                            out=col(tj, NMEAN), in0=col(tj, MEAN), scalar1=-1.0, scalar2=None,
                            op0=OP.mult), inc="dve")
                        if dbg_dump and tg == 0 and tj == 0:
                            P.op("vector", lambda v: v.tensor_copy(dbg_sc[:, 24:40], sc[:, 0:16]), inc="dve")
                            P.op("vector", lambda v: v.tensor_copy(dbg_wg[:], xtile[0][:, 0:512]), inc="dve")
                            nmtok = P.mark("dve")
                        P.waitt("scalar", nmtok)
                        P.waitt("scalar", xc_free[tj])
                        xctok = P.op("scalar", lambda s, tj=tj: s.activation(
                            out=xc[tj][:], in_=xtile[tj][:], func=AF.Identity,
                            bias=col(tj, NMEAN), scale=1.0), inc="act")
                        P.waitt("vector", xctok)
                        P.op("vector", lambda v, tj=tj: v.tensor_reduce(
                            col(tj, RMX), xc[tj][:], axis=AX, op=OP.max,
                            apply_absolute_value=True), inc="dve")
                        sepv()
                        P.op("vector", lambda v, tj=tj: v.tensor_scalar(
                            out=qsc[:, tj:tj + 1], in0=col(tj, RMX), scalar1=1e-9, scalar2=None,
                            op0=OP.max), inc="dve")
                    pbar(1)
                    a2_in = P.mark("dve")
                    P.waitt("scalar", a2_in)
                    P.op("scalar", lambda s: s.activation(
                        out=qsc[:, 8:16], in_=qsc[:, 0:8], func=AF.Ln), inc="act")
                    P.op("scalar", lambda s: s.activation(
                        out=qsc[:, 16:20], in_=qsc[:, 8:12], func=AF.Exp,
                        scale=-1.0, bias=ln127[:, 0:1]), inc="act")
                    a2tok = P.op("scalar", lambda s: s.activation(
                        out=qsc[:, 20:24], in_=qsc[:, 12:16], func=AF.Exp,
                        scale=-0.5), inc="act")
                    P.waitt("vector", a2tok)
                    for tj in range(NTJ):
                        P.op("vector", lambda v, tj=tj: v.tensor_tensor(
                            out=col(tj, T1), in0=qsc[:, tj:tj + 1], in1=qsc[:, 16 + tj:17 + tj],
                            op=OP.mult), inc="dve")
                        P.op("vector", lambda v, tj=tj: v.tensor_scalar(
                            out=col(tj, T1), in0=col(tj, T1), scalar1=-1.0 / 127, scalar2=2.0,
                            op0=OP.mult, op1=OP.add), inc="dve")
                        sxtok = P.op("vector", lambda v, tj=tj: v.tensor_tensor(
                            out=col(tj, SXQ), in0=qsc[:, 16 + tj:17 + tj], in1=col(tj, T1),
                            op=OP.mult), inc="dve")
                        P.waitt("scalar", sxtok)
                        P.waitt("scalar", qvx_free[tj % 2])
                        q1tok = P.op("scalar", lambda s, tj=tj: s.activation(
                            out=qv[:, (tj % 2) * 1024:(tj % 2) * 1024 + 1024], in_=xc[tj][:],
                            func=AF.Identity, bias=magic[:, 0:1], scale=col(tj, SXQ)), inc="act")
                        xc_free[tj] = q1tok
                        P.waitt("vector", q1tok)
                        xq_tok = P.op("vector", lambda v, tj=tj: v.tensor_scalar(
                            out=xq[tj % 2][:], in0=qv[:, (tj % 2) * 1024:(tj % 2) * 1024 + 1024],
                            scalar1=MAGIC, scalar2=None, op0=OP.subtract), inc="dve")
                        qvx_free[tj % 2] = xq_tok
                        if dbg_dump and tg == 0 and tj == 0:
                            P.op("vector", lambda v: v.tensor_copy(dbg_xq[:], xq[0][:, 0:512]), inc="dve")
                            P.op("vector", lambda v: v.tensor_copy(dbg_sc[:, 0:24], qsc[:]), inc="dve")
                            qvx_free[0] = P.mark("dve")
                        for k in range(8):
                            xq_rdy[tj] = transpose_tile(
                                xq[tj % 2][:, k * 128:(k + 1) * 128],
                                xqT[:, (tj * 8 + k) * 128:(tj * 8 + k) * 128 + 128],
                                [6, 7], xq_tok)
                    pbar(2)
                    for tj in range(NTJ):
                        P.op("vector", lambda v, tj=tj: v.tensor_tensor(
                            out=col(tj, T1), in0=qsc[:, 20 + tj:21 + tj], in1=qsc[:, 20 + tj:21 + tj],
                            op=OP.mult), inc="dve")
                        P.op("vector", lambda v, tj=tj: v.tensor_tensor(
                            out=col(tj, T1), in0=col(tj, T1), in1=qsc[:, 4 + tj:5 + tj], op=OP.mult), inc="dve")
                        P.op("vector", lambda v, tj=tj: v.tensor_scalar(
                            out=col(tj, T1), in0=col(tj, T1), scalar1=-0.5, scalar2=1.5,
                            op0=OP.mult, op1=OP.add), inc="dve")
                        P.op("vector", lambda v, tj=tj: v.tensor_tensor(
                            out=col(tj, RSTD), in0=qsc[:, 20 + tj:21 + tj], in1=col(tj, T1), op=OP.mult), inc="dve")
                        P.op("vector", lambda v, tj=tj: v.tensor_tensor(
                            out=col(tj, T0), in0=col(tj, RMX), in1=col(tj, RSTD), op=OP.mult), inc="dve")
                        P.op("vector", lambda v, tj=tj: v.tensor_scalar(
                            out=col(tj, T0), in0=col(tj, T0), scalar1=1e-5, scalar2=None,
                            op0=OP.max), inc="dve")
                        P.op("vector", lambda v, tj=tj: v.tensor_tensor(
                            out=col(tj, GSC), in0=col(tj, T0), in1=wmean[:, 0:1], op=OP.mult), inc="dve")
                        P.op("vector", lambda v, tj=tj: v.tensor_scalar(
                            out=col(tj, GSC), in0=col(tj, GSC), scalar1=1.0 / 127, scalar2=None,
                            op0=OP.mult), inc="dve")
                        P.op("vector", lambda v, tj=tj: v.tensor_tensor(
                            out=col(tj, USC), in0=col(tj, T0), in1=wmean[:, 1:2], op=OP.mult), inc="dve")
                        P.op("vector", lambda v, tj=tj: v.tensor_scalar(
                            out=col(tj, USC), in0=col(tj, USC), scalar1=1.0 / 127, scalar2=None,
                            op0=OP.mult), inc="dve")
                    scl_tok = P.mark("dve")
                    if paranoid == 10:
                        act_guard = P.mark("act")
                        pe_guard = P.mark("pe")
                        continue
                    pbar(3)
                    prod_last = [None] * NTJ
                    mmtok = None
                    for n2 in range(16):
                        gsl = n2 % 2
                        P.waitt("sync", wbg_free[gsl])
                        gtok = P.op("sync", lambda s, gsl=gsl, n2=n2: s.dma_start(
                            out=wbg[gsl][:], in_=sg[n2]), inc=f"dw{gsl}", k=16)
                        P.waitt("sync", wbu_free[gsl])
                        utok = P.op("sync", lambda s, gsl=gsl, n2=n2: s.dma_start(
                            out=wbu[gsl][:], in_=su[n2]), inc=f"du{gsl}", k=16)
                        for tj in range(NTJ):
                            bg, bu = 2 * tj, 2 * tj + 1
                            P.waitt("tensor", bankfree[bg])
                            P.waitt("tensor", bankfree[bu])
                            P.waitt("tensor", xq_rdy[tj])
                            P.waitt("tensor", gtok)
                            P.waitt("tensor", utok)
                            for k in range(8):
                                lhs = xqT[:, (tj * 8 + k) * 128:(tj * 8 + k) * 128 + 128]
                                P.op("tensor", lambda t, lhs=lhs, bg=bg, gsl=gsl, k=k: t.matmul(
                                    PS[bg][:, 0:256], lhs, wbg[gsl][:, k * 256:(k + 1) * 256],
                                    start=(k == 0), stop=(k == 7)))
                                fn = (lambda t, lhs=lhs, bu=bu, gsl=gsl, k=k: t.matmul(
                                    PS[bu][:, 0:256], lhs, wbu[gsl][:, k * 256:(k + 1) * 256],
                                    start=(k == 0), stop=(k == 7)))
                                if k == 7:
                                    mmtok = P.op("tensor", fn, inc="pe")
                                else:
                                    P.op("tensor", fn)
                            wbg_free[gsl] = mmtok
                            wbu_free[gsl] = mmtok
                            P.waitt("scalar", mmtok)
                            P.waitt("scalar", scl_tok)
                            P.waitt("scalar", sgb_free[tj % 2])
                            sltok = P.op("scalar", lambda s, bg=bg, tj=tj: s.activation(
                                out=sgb[:, (tj % 2) * 256:(tj % 2) * 256 + 256],
                                in_=PS[bg][:, 0:256], func=AF.Silu, scale=col(tj, GSC)), inc="act")
                            P.waitt("vector", sltok)
                            ptok = P.op("vector", lambda v, bu=bu, tj=tj, n2=n2: v.scalar_tensor_tensor(
                                out=prod[:, tj * 4096 + n2 * 256: tj * 4096 + n2 * 256 + 256],
                                in0=PS[bu][:, 0:256], scalar=col(tj, USC),
                                in1=sgb[:, (tj % 2) * 256:(tj % 2) * 256 + 256],
                                op0=OP.mult, op1=OP.mult), inc="dve")
                            bankfree[bg] = ptok
                            bankfree[bu] = ptok
                            sgb_free[tj % 2] = ptok
                            prod_last[tj] = ptok
                            if dbg_dump and tg == 0 and tj == 0 and n2 == 0:
                                P.op("vector", lambda v: v.tensor_copy(dbg_prod[:], prod[:, 0:256]), inc="dve")
                                prod_last[0] = P.mark("dve")
                    pbar(4)
                    vqT_rdy = [None] * NTJ
                    qvv_free = qvx_free[0]
                    for tj in range(NTJ):
                        P.waitt("vector", prod_last[tj])
                        P.op("vector", lambda v, tj=tj: v.tensor_reduce(
                            col(tj, VMX), prod[:, tj * 4096:(tj + 1) * 4096], axis=AX,
                            op=OP.max, apply_absolute_value=True), inc="dve")
                        sepv()
                        P.op("vector", lambda v, tj=tj: v.tensor_scalar(
                            out=col(tj, VMX), in0=col(tj, VMX), scalar1=1e-5, scalar2=None,
                            op0=OP.max), inc="dve")
                        vmx_tok = P.mark("dve")
                        P.waitt("scalar", vmx_tok)
                        P.op("scalar", lambda s, tj=tj: s.activation(
                            out=col(tj, SCR), in_=col(tj, VMX), func=AF.Ln), inc="act")
                        sv0 = P.op("scalar", lambda s, tj=tj: s.activation(
                            out=col(tj, T0), in_=col(tj, SCR), func=AF.Exp,
                            scale=-1.0, bias=ln127[:, 0:1]), inc="act")
                        P.waitt("vector", sv0)
                        P.op("vector", lambda v, tj=tj: v.tensor_tensor(
                            out=col(tj, T1), in0=col(tj, VMX), in1=col(tj, T0), op=OP.mult), inc="dve")
                        P.op("vector", lambda v, tj=tj: v.tensor_scalar(
                            out=col(tj, T1), in0=col(tj, T1), scalar1=-1.0 / 127, scalar2=2.0,
                            op0=OP.mult, op1=OP.add), inc="dve")
                        svqtok = P.op("vector", lambda v, tj=tj: v.tensor_tensor(
                            out=col(tj, SVQ), in0=col(tj, T0), in1=col(tj, T1), op=OP.mult), inc="dve")
                        P.op("vector", lambda v, tj=tj: v.tensor_tensor(
                            out=col(tj, DSC), in0=col(tj, VMX), in1=wmean[:, 2:3], op=OP.mult), inc="dve")
                        P.op("vector", lambda v, tj=tj: v.tensor_scalar(
                            out=col(tj, DSC), in0=col(tj, DSC), scalar1=0.5 / 127, scalar2=None,
                            op0=OP.mult), inc="dve")
                        for h2 in range(2):
                            P.waitt("scalar", svqtok)
                            P.waitt("scalar", qvv_free)
                            P.waitt("scalar", qvx_free[1])
                            q2tok = P.op("scalar", lambda s, tj=tj, h2=h2: s.activation(
                                out=qv[:], in_=prod[:, tj * 4096 + h2 * 2048: tj * 4096 + (h2 + 1) * 2048],
                                func=AF.Identity, bias=magic[:, 0:1], scale=col(tj, SVQ)), inc="act")
                            P.waitt("vector", q2tok)
                            vq_tok = P.op("vector", lambda v, h2=h2: v.tensor_scalar(
                                out=vq[:, h2 * 2048:(h2 + 1) * 2048], in0=qv[:],
                                scalar1=MAGIC, scalar2=None, op0=OP.subtract), inc="dve")
                            qvv_free = vq_tok
                            qvx_free[0] = vq_tok
                            qvx_free[1] = vq_tok

                            for k in range(16):
                                kc = h2 * 16 + k
                                vqT_rdy[tj] = transpose_tile(
                                    vq[:, kc * 128:(kc + 1) * 128],
                                    vqT[:, (tj * 32 + kc) * 128:(tj * 32 + kc) * 128 + 128],
                                    [6, 7], vq_tok)
                    pbar(5)
                    dtok = None
                    for u8 in range(8):
                        dsl = u8 % 2
                        P.waitt("sync", wbd_free[dsl])
                        sdv = sd.rearrange("r p (kc m) -> p kc r m", kc=32)
                        for kc_ in range(4):
                            src3 = sdv[:, u8 * 4 + kc_, :, :]
                            dst3 = wbd[dsl][:, kc_ * 1024:(kc_ + 1) * 1024].rearrange(
                                "p (b m) -> p b m", b=8)
                            dltok = P.op("sync", lambda s, src3=src3, dst3=dst3: s.dma_start(
                                out=dst3, in_=src3), inc=f"dd{dsl}", k=16)
                        P.waitt("tensor", dltok)
                        if u8 == 0:
                            for tj in range(NTJ):
                                P.waitt("tensor", bankfree[2 * tj])
                                P.waitt("tensor", bankfree[2 * tj + 1])
                                P.waitt("tensor", vqT_rdy[tj])
                        for kc in range(4):
                            kca = u8 * 4 + kc
                            for tj in range(NTJ):
                                lhs = vqT[:, (tj * 32 + kca) * 128:(tj * 32 + kca) * 128 + 128]
                                P.op("tensor", lambda t, lhs=lhs, tj=tj, dsl=dsl, kc=kc, kca=kca: t.matmul(
                                    PS[2 * tj][:, 0:512], lhs, wbd[dsl][:, kc * 1024:kc * 1024 + 512],
                                    start=(kca == 0), stop=(kca == 31)))
                                fn = (lambda t, lhs=lhs, tj=tj, dsl=dsl, kc=kc, kca=kca: t.matmul(
                                    PS[2 * tj + 1][:, 0:512], lhs,
                                    wbd[dsl][:, kc * 1024 + 512:kc * 1024 + 1024],
                                    start=(kca == 0), stop=(kca == 31)))
                                if kc == 3 and tj == NTJ - 1:
                                    dtok = P.op("tensor", fn, inc="pe")
                                else:
                                    P.op("tensor", fn)
                        wbd_free[dsl] = dtok
                    dfin = P.mark("pe")
                    pbar(6)
                    for tj in range(NTJ):
                        P.waitt("vector", dfin)
                        etok = None
                        for half in range(2):
                            b = 2 * tj + half
                            etok = P.op("vector", lambda v, tj=tj, b=b, half=half: v.scalar_tensor_tensor(
                                out=xtile[tj][:, half * 512:(half + 1) * 512],
                                in0=PS[b][:, 0:512], scalar=col(tj, DSC),
                                in1=xtile[tj][:, half * 512:(half + 1) * 512],
                                op0=OP.mult, op1=OP.add), inc="dve")
                            bankfree[b] = etok
                        P.waitt("sync", etok)
                        stok = P.op("sync", lambda g, tj=tj, r=row0 + tj * 128, dst=dst: g.dma_start(
                            out=dst[r:r + 128, :], in_=xtile[tj][:]), inc=f"dx{tj}", k=16)
                        xfree[tj] = stok
                    act_guard = P.mark("act")
                    pe_guard = P.mark("pe")
            P.barrier()
            bankfree.update({i: None for i in range(8)})

        def emit_attn():
            with ExitStack() as as_:
                AE = as_.enter_context
                wlat_sb = AE(nc.sbuf_tensor("wlat_sb", [128, 8 * 768], BF16))
                wqc_sb = AE(nc.sbuf_tensor("wqc_sb", [128, 2 * 896], BF16))
                wkc_sb = AE(nc.sbuf_tensor("wkc_sb", [128, 2 * 896], BF16))
                wv_sb = AE(nc.sbuf_tensor("wv_sb", [128, 2 * 1024], BF16))
                blat = AE(nc.sbuf_tensor("blat", [128, 6], F32))
                bqcp = AE(nc.sbuf_tensor("bqcp", [128, 7], F32))
                bkcp = AE(nc.sbuf_tensor("bkcp", [128, 7], F32))
                bvb = AE(nc.sbuf_tensor("bvb", [128, 1024], F32))
                bob = AE(nc.sbuf_tensor("bob", [128, 1024], F32))
                wstage = AE(nc.sbuf_tensor("wstage", [128, 1024], F32))

                def load_cast(dst_sb, hbm, nch, width):
                    last = None
                    for k in range(nch):
                        src = rowtiles(hbm, k, k + 1, width)
                        lt = P.op("sync", lambda s, src=src, w=width: s.dma_start(
                            out=sb3(wstage[:, 0:w], 1, w), in_=src), inc="dm", k=16)
                        P.waitt("vector", lt)
                        last = P.op("vector", lambda v, dst=dst_sb, k=k, w=width: v.tensor_copy(
                            dst[:, k * w:(k + 1) * w], wstage[:, 0:w]), inc="dve")
                        P.waitt("sync", last)
                    return last

                load_cast(wlat_sb, d_in["wlat"], 8, 768)
                load_cast(wqc_sb, d_in["wqc"], 2, 896)
                load_cast(wkc_sb, d_in["wkc"], 2, 896)
                load_cast(wv_sb, d_in["wv"], 2, 1024)
                for dstb, nm in [(blat, "blat_p"), (bqcp, "bqc_pp"), (bkcp, "bkc_pp"),
                                 (bvb, "bv_b"), (bob, "bo_b")]:
                    P.op("sync", lambda s, dstb=dstb, nm=nm: s.dma_start(
                        out=dstb[:], in_=d_in[nm]), inc="dm", k=16)
                wload_ld = P.mark("dm")
                wload_cast = P.mark("dve")

                def do_batch(b):
                    brow = b * S
                    with ExitStack() as bs:
                        BE = bs.enter_context
                        QT = BE(nc.sbuf_tensor(f"QT{b}", [128, 8 * 2048], BF16))
                        KT = BE(nc.sbuf_tensor(f"KT{b}", [128, 8 * 2048], BF16))
                        VB2 = BE(nc.sbuf_tensor(f"VB{b}", [128, 16 * 16 * 65], BF16))
                        VB2r = VB2.rearrange("p (k c) -> p k c", c=65)
                        P.op("gpsimd", lambda g: g.memset(VB2r[:, :, 64:65], 1.0))

                        def do_scope_a():
                          with ExitStack() as sa:
                              SA = sa.enter_context
                              xtile = [SA(nc.sbuf_tensor(f"a{b}xt{i}", [128, 1024], F32)) for i in range(4)]
                              sqtr = SA(nc.sbuf_tensor(f"a{b}sqtr", [128, 1024], BF16))
                              xn = [SA(nc.sbuf_tensor(f"a{b}xn{i}", [128, 1024], F32)) for i in range(2)]
                              xlnT = SA(nc.sbuf_tensor(f"a{b}xlnT", [128, 8 * 512], BF16))
                              cqg = SA(nc.sbuf_tensor(f"a{b}cqg", [128, 2 * 2048], BF16))
                              ckvg = SA(nc.sbuf_tensor(f"a{b}ckvg", [128, 2 * 2048], BF16))
                              cosb = SA(nc.sbuf_tensor(f"a{b}cosb", [128, 512], F32))
                              sinb = SA(nc.sbuf_tensor(f"a{b}sinb", [128, 512], F32))
                              rotb = SA(nc.sbuf_tensor(f"a{b}rotb", [128, 512], F32))
                              rawq = SA(nc.sbuf_tensor(f"a{b}rawq", [128, 512], F32))
                              rawk = SA(nc.sbuf_tensor(f"a{b}rawk", [128, 512], F32))
                              sc = SA(nc.sbuf_tensor(f"a{b}sc", [128, 64], F32))
                              vep = SA(nc.sbuf_tensor(f"a{b}vep", [128, 12], F32))

                              SSQ, SX, MEAN, NMEAN, RSTD, NMR, T0, T1 = range(8)

                              def acol(tj, c):
                                  return sc[:, tj * 8 + c: tj * 8 + c + 1]

                              lat_free = {m: None for m in range(6)}
                              lat_prev = None
                              xn_free = [None, None]
                              raw_free = [None, None]
                              rot_free = None
                              tab_free = None
                              a_act_guard = None
                              for tg in range(4):
                                  P.waitt("vector", a_act_guard)
                                  for tj in range(4):
                                      r = brow + tg * 512 + tj * 128
                                      ltok = P.op("sync", lambda s, tj=tj, r=r: s.dma_start(
                                          out=xtile[tj][:], in_=x1[r:r + 128, :]), inc=f"dx{tj}", k=16)
                                      P.waitt("scalar", ltok)
                                      stok = P.op("scalar", lambda s, tj=tj: s.activation(
                                          out=sqtr[:], in_=xtile[tj][:], func=AF.Square,
                                          accum_out=acol(tj, SSQ)), inc="act")
                                      P.waitt("vector", ltok)
                                      P.op("vector", lambda v, tj=tj: v.tensor_reduce(
                                          acol(tj, SX), xtile[tj][:], axis=AX, op=OP.add), inc="dve")
                                      P.waitt("vector", stok)
                                      sepv()
                                      P.op("vector", lambda v, tj=tj: v.tensor_scalar(
                                          out=acol(tj, MEAN), in0=acol(tj, SX), scalar1=1.0 / 1024,
                                          scalar2=None, op0=OP.mult), inc="dve")
                                      P.op("vector", lambda v, tj=tj: v.tensor_scalar(
                                          out=acol(tj, T0), in0=acol(tj, SSQ), scalar1=1.0 / 1024,
                                          scalar2=None, op0=OP.mult), inc="dve")
                                      P.op("vector", lambda v, tj=tj: v.tensor_tensor(
                                          out=acol(tj, T1), in0=acol(tj, MEAN), in1=acol(tj, MEAN),
                                          op=OP.mult), inc="dve")
                                      P.op("vector", lambda v, tj=tj: v.tensor_tensor(
                                          out=acol(tj, T0), in0=acol(tj, T0), in1=acol(tj, T1),
                                          op=OP.subtract), inc="dve")
                                      P.op("vector", lambda v, tj=tj: v.tensor_scalar(
                                          out=vep[:, tj:tj + 1], in0=acol(tj, T0), scalar1=1e-5,
                                          scalar2=None, op0=OP.add), inc="dve")
                                  sq_in = P.mark("dve")
                                  P.waitt("scalar", sq_in)
                                  P.op("scalar", lambda s: s.activation(
                                      out=vep[:, 4:8], in_=vep[:, 0:4], func=AF.Ln), inc="act")
                                  sqtok = P.op("scalar", lambda s: s.activation(
                                      out=vep[:, 8:12], in_=vep[:, 4:8], func=AF.Exp,
                                      scale=-0.5), inc="act")
                                  P.waitt("vector", sqtok)
                                  for tj in range(4):
                                      P.op("vector", lambda v, tj=tj: v.tensor_tensor(
                                          out=acol(tj, T1), in0=vep[:, 8 + tj:9 + tj], in1=vep[:, 8 + tj:9 + tj],
                                          op=OP.mult), inc="dve")
                                      P.op("vector", lambda v, tj=tj: v.tensor_tensor(
                                          out=acol(tj, T1), in0=acol(tj, T1), in1=vep[:, tj:tj + 1],
                                          op=OP.mult), inc="dve")
                                      P.op("vector", lambda v, tj=tj: v.tensor_scalar(
                                          out=acol(tj, T1), in0=acol(tj, T1), scalar1=-0.5, scalar2=1.5,
                                          op0=OP.mult, op1=OP.add), inc="dve")
                                      P.op("vector", lambda v, tj=tj: v.tensor_tensor(
                                          out=acol(tj, RSTD), in0=vep[:, 8 + tj:9 + tj], in1=acol(tj, T1),
                                          op=OP.mult), inc="dve")
                                      P.op("vector", lambda v, tj=tj: v.tensor_tensor(
                                          out=acol(tj, T1), in0=acol(tj, MEAN), in1=acol(tj, RSTD),
                                          op=OP.mult), inc="dve")
                                      nmrtok = P.op("vector", lambda v, tj=tj: v.tensor_scalar(
                                          out=acol(tj, NMR), in0=acol(tj, T1), scalar1=-1.0,
                                          scalar2=None, op0=OP.mult), inc="dve")
                                      P.waitt("scalar", nmrtok)
                                      P.waitt("scalar", xn_free[tj % 2])
                                      xntok = P.op("scalar", lambda s, tj=tj: s.activation(
                                          out=xn[tj % 2][:], in_=xtile[tj][:], func=AF.Identity,
                                          bias=acol(tj, NMR), scale=acol(tj, RSTD)), inc="act")
                                      P.waitt("vector", lat_prev)
                                      for k in range(8):
                                          xn_free[tj % 2] = transpose_tile(
                                              xn[tj % 2][:, k * 128:(k + 1) * 128],
                                              xlnT[:, k * 512 + tj * 128: k * 512 + tj * 128 + 128],
                                              [6, 7], xntok)
                                  xlnT_tok = P.mark("dve")
                                  P.waitt("tensor", xlnT_tok)
                                  P.wait("tensor", "dm", wload_ld[1])
                                  P.waitt("tensor", wload_cast)
                                  for m in range(6):
                                      P.waitt("tensor", lat_free[m])
                                  lattok = None
                                  for k in range(8):
                                      for m in range(6):
                                          lhs = wlat_sb[:, k * 768 + m * 128: k * 768 + m * 128 + 128]
                                          fn = (lambda t, lhs=lhs, m=m, k=k: t.matmul(
                                              PS[m][:, 0:512], lhs, xlnT[:, k * 512:(k + 1) * 512],
                                              start=(k == 0), stop=(k == 7)))
                                          if k == 7 and m == 5:
                                              lattok = P.op("tensor", fn, inc="pe")
                                          else:
                                              P.op("tensor", fn)
                                  lat_prev = lattok
                                  P.waitt("scalar", lattok)
                                  for m in range(4):
                                      dstb = cqg if m < 2 else ckvg
                                      evt = P.op("scalar", lambda s, m=m, dstb=dstb, tg=tg: s.activation(
                                          out=dstb[:, (m % 2) * 2048 + tg * 512:(m % 2) * 2048 + tg * 512 + 512],
                                          in_=PS[m][:, 0:512], func=AF.Gelu,
                                          bias=blat[:, m:m + 1], scale=1.0), inc="act")
                                      lat_free[m] = evt
                                  P.waitt("scalar", raw_free[0])
                                  r4 = P.op("scalar", lambda s: s.activation(
                                      out=rawq[:], in_=PS[4][:, 0:512], func=AF.Identity,
                                      bias=blat[:, 4:5], scale=1.0), inc="act")
                                  lat_free[4] = r4
                                  P.waitt("scalar", raw_free[1])
                                  r5 = P.op("scalar", lambda s: s.activation(
                                      out=rawk[:], in_=PS[5][:, 0:512], func=AF.Identity,
                                      bias=blat[:, 5:6], scale=1.0), inc="act")
                                  lat_free[5] = r5
                                  P.waitt("sync", tab_free)
                                  P.op("sync", lambda s, tg=tg: s.dma_start(
                                      out=cosb[:], in_=d_in["cosT"][:, tg * 512:(tg + 1) * 512]), inc="dc", k=16)
                                  ctok = P.op("sync", lambda s, tg=tg: s.dma_start(
                                      out=sinb[:], in_=d_in["sinT"][:, tg * 512:(tg + 1) * 512]), inc="dc", k=16)
                                  for idx, (raw, rtok, dstT) in enumerate(((rawq, r4, QT), (rawk, r5, KT))):
                                      P.waitt("scalar", rtok)
                                      P.waitt("scalar", rot_free)
                                      P.op("scalar", lambda g, raw=raw: g.dma_start(
                                          out=rotb[0:64, :], in_=raw[64:128, :]), inc="dr", k=16)
                                      shtok = P.op("scalar", lambda g, raw=raw: g.dma_start(
                                          out=rotb[64:128, :], in_=raw[0:64, :]), inc="dr", k=16)
                                      P.waitt("vector", shtok)
                                      P.op("vector", lambda v: v.tensor_scalar(
                                          out=rotb[0:64, :], in0=rotb[0:64, :], scalar1=-1.0,
                                          scalar2=None, op0=OP.mult), inc="dve")
                                      P.waitt("vector", ctok)
                                      P.op("vector", lambda v: v.tensor_tensor(
                                          out=rotb[:], in0=rotb[:], in1=sinb[:], op=OP.mult), inc="dve")
                                      P.op("vector", lambda v, raw=raw: v.tensor_tensor(
                                          out=raw[:], in0=raw[:], in1=cosb[:], op=OP.mult), inc="dve")
                                      addtok = P.op("vector", lambda v, raw=raw: v.tensor_tensor(
                                          out=raw[:], in0=raw[:], in1=rotb[:], op=OP.add), inc="dve")
                                      raw_free[idx] = addtok
                                      rot_free = addtok
                                      P.waitt("scalar", addtok)
                                      gtok = P.op("scalar", lambda s, raw=raw, dstT=dstT, tg=tg: s.activation(
                                          out=dstT[:, 7 * 2048 + tg * 512: 7 * 2048 + tg * 512 + 512],
                                          in_=raw[:], func=AF.Gelu), inc="act")
                                      tab_free = gtok
                                  a_act_guard = P.mark("act")
                                  P.wait("sync", "act", a_act_guard[1])

                              if paranoid == 30 and b == 0:
                                  cg0 = P.mark("act")
                                  P.waitt("vector", cg0)
                                  dq2 = P.op("vector", lambda v: v.tensor_copy(dbg_vq[:, 0:128], cqg[:, 0:128]), inc="dve")
                                  P.waitt("sync", dq2)
                                  P.op("sync", lambda s_: s_.dma_start(out=y_out[1664:1792, 0:128], in_=dbg_vq[:, 0:128]), inc="dm", k=16)
                              qkv_guard = P.mark("act")
                              P.waitt("tensor", qkv_guard)
                              for (wsb, bpp, dstT, srcT) in ((wqc_sb, bqcp, QT, cqg),
                                                             (wkc_sb, bkcp, KT, ckvg)):
                                  for j in range(7):
                                      for tg in range(4):
                                          bk = tg % 2
                                          P.waitt("tensor", bankfree[bk])
                                          mm = None
                                          for k in range(2):
                                              lhs = wsb[:, k * 896 + j * 128: j * 128 + k * 896 + 128]
                                              fn = (lambda t, lhs=lhs, k=k, bk=bk, tg=tg, srcT=srcT: t.matmul(
                                                  PS[bk][:, 0:512], lhs,
                                                  srcT[:, k * 2048 + tg * 512: k * 2048 + tg * 512 + 512],
                                                  start=(k == 0), stop=(k == 1)))
                                              if k == 1:
                                                  mm = P.op("tensor", fn, inc="pe")
                                              else:
                                                  P.op("tensor", fn)
                                          P.waitt("scalar", mm)
                                          ev = P.op("scalar", lambda s, bk=bk, j=j, tg=tg, bpp=bpp, dstT=dstT: s.activation(
                                              out=dstT[:, j * 2048 + tg * 512: j * 2048 + tg * 512 + 512],
                                              in_=PS[bk][:, 0:512], func=AF.Gelu,
                                              bias=bpp[:, j:j + 1], scale=1.0), inc="act")
                                          bankfree[bk] = ev
                              for tj in range(16):
                                  b0, b1 = 2, 3
                                  P.waitt("tensor", bankfree[b0])
                                  P.waitt("tensor", bankfree[b1])
                                  mm = None
                                  for k in range(2):
                                      lhs = ckvg[:, k * 2048 + tj * 128: k * 2048 + tj * 128 + 128]
                                      P.op("tensor", lambda t, lhs=lhs, k=k, b0=b0: t.matmul(
                                          PS[b0][:, 0:512], lhs, wv_sb[:, k * 1024:k * 1024 + 512],
                                          start=(k == 0), stop=(k == 1)))
                                      fn = (lambda t, lhs=lhs, k=k, b1=b1: t.matmul(
                                          PS[b1][:, 0:512], lhs, wv_sb[:, k * 1024 + 512:(k + 1) * 1024],
                                          start=(k == 0), stop=(k == 1)))
                                      if k == 1:
                                          mm = P.op("tensor", fn, inc="pe")
                                      else:
                                          P.op("tensor", fn)
                                  P.waitt("vector", mm)
                                  e0 = P.op("vector", lambda v, tj=tj, b0=b0: v.tensor_tensor(
                                      out=VB2r[:, tj * 16: tj * 16 + 8, 0:64],
                                      in0=PS[b0][:, 0:512].rearrange("p (h c) -> p h c", c=64),
                                      in1=bvb[:, 0:512].rearrange("p (h c) -> p h c", c=64),
                                      op=OP.add), inc="dve")
                                  e1 = P.op("vector", lambda v, tj=tj, b1=b1: v.tensor_tensor(
                                      out=VB2r[:, tj * 16 + 8: tj * 16 + 16, 0:64],
                                      in0=PS[b1][:, 0:512].rearrange("p (h c) -> p h c", c=64),
                                      in1=bvb[:, 512:1024].rearrange("p (h c) -> p h c", c=64),
                                      op=OP.add), inc="dve")
                                  bankfree[b0] = e0
                                  bankfree[b1] = e1

                        do_scope_a()
                        P.barrier()
                        bankfree.update({i: None for i in range(8)})

                        def do_scope_b():
                          with ExitStack() as sb:
                              SB = sb.enter_context
                              wo_sb = SB(nc.sbuf_tensor(f"wo_sb{b}", [128, 8 * 1024], BF16))
                              OT = SB(nc.sbuf_tensor(f"OT{b}", [128, 8 * 2048], BF16))
                              xtile = [SB(nc.sbuf_tensor(f"b{b}xt{i}", [128, 1024], F32)) for i in range(2)]
                              scb = SB(nc.sbuf_tensor(f"b{b}sc", [128, 16], F32))

                              wo_tok = None
                              for k in range(8):
                                  src = rowtiles(d_in["wo"], k, k + 1, 1024)
                                  lt = P.op("sync", lambda s, src=src: s.dma_start(
                                      out=sb3(wstage[:, 0:1024], 1, 1024), in_=src), inc="dm", k=16)
                                  P.waitt("vector", lt)
                                  wo_tok = P.op("vector", lambda v, k=k: v.tensor_copy(
                                      wo_sb[:, k * 1024:(k + 1) * 1024], wstage[:, 0:1024]), inc="dve")
                                  P.waitt("sync", wo_tok)

                              ebf = [SB(nc.sbuf_tensor(f"eb{b}{i}", [128, 512], BF16)) for i in range(2)]
                              Oq = [SB(nc.sbuf_tensor(f"Oq{b}{i}", [128, 64], BF16)) for i in range(2)]
                              OT6 = PS[6][:].bitcast(BF16)
                              OT7 = PS[7][:].bitcast(BF16)
                              ebf_free = [None, None]
                              Oq_free = [None, None]
                              for h in range(16):
                                  hr0 = (h % 2) * 64
                                  hch = h // 2
                                  for g4 in range(4):
                                      q0 = g4 * 512
                                      nkc = 4 * g4 + 4
                                      rhsQ = QT[hr0:hr0 + 64, hch * 2048 + q0: hch * 2048 + q0 + 512]
                                      pvtok = [None] * 4
                                      for kc in range(nkc):
                                          sb_ = kc % 2
                                          a = kc - 4 * g4
                                          P.waitt("tensor", bankfree[sb_])
                                          lhsK = KT[hr0:hr0 + 64, hch * 2048 + kc * 128: hch * 2048 + kc * 128 + 128]
                                          if a < 0:
                                              mm = P.op("tensor", lambda t, lhsK=lhsK, rhsQ=rhsQ, sb_=sb_: t.matmul(
                                                  PS[sb_][:, 0:512], lhsK, rhsQ, start=True, stop=True), inc="pe")
                                          else:
                                              P.op("tensor", lambda t, lhsK=lhsK, rhsQ=rhsQ, sb_=sb_: t.matmul(
                                                  PS[sb_][:, 0:512], lhsK, rhsQ, start=True, stop=False))
                                              mm = P.op("tensor", lambda t, sb_=sb_, a=a: t.matmul(
                                                  PS[sb_][:, 0:512], identb[:], maskTw[a][:, 0:512],
                                                  start=False, stop=True), inc="pe")
                                          P.waitt("scalar", mm)
                                          P.waitt("scalar", ebf_free[sb_])
                                          ex = P.op("scalar", lambda s, sb_=sb_: s.activation(
                                              out=ebf[sb_][:], in_=PS[sb_][:, 0:512], func=AF.Exp,
                                              scale=0.125), inc="act")
                                          bankfree[sb_] = ex
                                          for j in range(4):
                                              if j == 0:
                                                  P.waitt("tensor", ex)
                                              if kc == 0:
                                                  P.waitt("tensor", bankfree[2 + j])
                                              rhsV = VB2r[:, kc * 16 + h, :]
                                              pvtok[j] = P.op("tensor", lambda t, sb_=sb_, j=j, rhsV=rhsV, kc=kc, nkc=nkc: t.matmul(
                                                  PS[2 + j][:, 0:65], ebf[sb_][:, j * 128:(j + 1) * 128], rhsV,
                                                  start=(kc == 0), stop=(kc == nkc - 1)), inc="pe")
                                          ebf_free[sb_] = pvtok[3]
                                      for j in range(4):
                                          qt = 4 * g4 + j
                                          P.waitt("scalar", pvtok[j])
                                          P.op("scalar", lambda s, j=j: s.activation(
                                              out=scb[:, 8 + j:9 + j], in_=PS[2 + j][:, 64:65],
                                              func=AF.Ln), inc="act")
                                          li = P.op("scalar", lambda s, j=j: s.activation(
                                              out=scb[:, 12 + j:13 + j], in_=scb[:, 8 + j:9 + j],
                                              func=AF.Exp, scale=-1.0), inc="act")
                                          seps()
                                          P.waitt("vector", li)
                                          P.waitt("vector", Oq_free[j % 2])
                                          nr = P.op("vector", lambda v, j=j: v.tensor_scalar(
                                              out=Oq[j % 2][:], in0=PS[2 + j][:, 0:64],
                                              scalar1=scb[:, 12 + j:13 + j], scalar2=None,
                                              op0=OP.mult), inc="dve")
                                          bankfree[2 + j] = nr
                                          ob = 6 + (j % 2)
                                          obv = OT6 if ob == 6 else OT7
                                          P.waitt("tensor", bankfree[ob])
                                          P.waitt("tensor", nr)
                                          tt = P.op("tensor", lambda t, obv=obv, j=j, hr0=hr0: t.transpose(
                                              obv[hr0:hr0 + 64, 0:128], Oq[j % 2][:], identb[:]), inc="pe")
                                          Oq_free[j % 2] = tt
                                          if j % 2 == 0:
                                              P.waitt("scalar", tt)
                                              oev = P.op("scalar", lambda s, obv=obv, hr0=hr0, hch=hch, qt=qt: s.activation(
                                                  out=OT[hr0:hr0 + 64, hch * 2048 + qt * 128: hch * 2048 + qt * 128 + 128],
                                                  in_=obv[hr0:hr0 + 64, 0:128], func=AF.Copy), inc="act")
                                          else:
                                              P.waitt("vector", tt)
                                              oev = P.op("vector", lambda v, obv=obv, hr0=hr0, hch=hch, qt=qt: v.tensor_copy(
                                                  OT[hr0:hr0 + 64, hch * 2048 + qt * 128: hch * 2048 + qt * 128 + 128],
                                                  obv[hr0:hr0 + 64, 0:128]), inc="dve")
                                          bankfree[ob] = oev
                              o_done = P.mark("act")
                              if paranoid == 30 and b == 0:
                                  P.waitt("vector", o_done)
                                  P.waitt("vector", P.mark("dm"))
                                  do_ = P.op("vector", lambda v: v.tensor_copy(dbg_xq[:, 0:256], OT[:, 0:256]), inc="dve")
                                  P.waitt("sync", do_)
                                  P.op("sync", lambda s_: s_.dma_start(out=y_out[1792:1920, 0:256], in_=dbg_xq[:, 0:256]), inc="dm", k=16)
                                  do2 = P.op("vector", lambda v: v.tensor_copy(dbg_vq[:, 0:128], OT[:, 7 * 2048:7 * 2048 + 128]), inc="dve")
                                  P.waitt("sync", do2)
                                  P.op("sync", lambda s_: s_.dma_start(out=y_out[1920:2048, 0:128], in_=dbg_vq[:, 0:128]), inc="dm", k=16)
                              xf = [None, None]
                              for tj in range(16):
                                  r = brow + tj * 128
                                  sl = tj % 2
                                  P.waitt("sync", xf[sl])
                                  lt = P.op("sync", lambda s, sl=sl, r=r: s.dma_start(
                                      out=xtile[sl][:], in_=x1[r:r + 128, :]), inc="dm", k=16)
                                  P.waitt("vector", lt)
                                  botok = P.op("vector", lambda v, sl=sl: v.tensor_tensor(
                                      out=xtile[sl][:], in0=xtile[sl][:], in1=bob[:], op=OP.add), inc="dve")
                                  bk0, bk1 = 2 * (tj % 2), 2 * (tj % 2) + 1
                                  P.waitt("tensor", bankfree[bk0])
                                  P.waitt("tensor", bankfree[bk1])
                                  P.waitt("tensor", o_done)
                                  P.waitt("tensor", wo_tok)
                                  mm = None
                                  for k in range(8):
                                      lhs = OT[:, k * 2048 + tj * 128: k * 2048 + tj * 128 + 128]
                                      P.op("tensor", lambda t, lhs=lhs, k=k, bk0=bk0: t.matmul(
                                          PS[bk0][:, 0:512], lhs, wo_sb[:, k * 1024:k * 1024 + 512],
                                          start=(k == 0), stop=(k == 7)))
                                      fn = (lambda t, lhs=lhs, k=k, bk1=bk1: t.matmul(
                                          PS[bk1][:, 0:512], lhs, wo_sb[:, k * 1024 + 512:(k + 1) * 1024],
                                          start=(k == 0), stop=(k == 7)))
                                      if k == 7:
                                          mm = P.op("tensor", fn, inc="pe")
                                      else:
                                          P.op("tensor", fn)
                                  P.waitt("vector", mm)
                                  P.waitt("vector", botok)
                                  P.op("vector", lambda v, sl=sl, bk0=bk0: v.tensor_tensor(
                                      out=xtile[sl][:, 0:512], in0=PS[bk0][:, 0:512],
                                      in1=xtile[sl][:, 0:512], op=OP.add), inc="dve")
                                  ev = P.op("vector", lambda v, sl=sl, bk1=bk1: v.tensor_tensor(
                                      out=xtile[sl][:, 512:1024], in0=PS[bk1][:, 0:512],
                                      in1=xtile[sl][:, 512:1024], op=OP.add), inc="dve")
                                  bankfree[bk0] = ev
                                  bankfree[bk1] = ev
                                  if paranoid == 30 and b == 0 and tj == 0:
                                      P.waitt("vector", ev)
                                      P.waitt("vector", P.mark("dm"))
                                      dw_ = P.op("vector", lambda v, sl=sl: v.tensor_copy(dbg_xq[:, 0:256], xtile[sl][:, 0:256]), inc="dve")
                                      P.waitt("sync", dw_)
                                      P.op("sync", lambda s_: s_.dma_start(out=y_out[2048:2176, 0:256], in_=dbg_xq[:, 0:256]), inc="dm", k=16)
                                  P.waitt("sync", ev)
                                  stok = P.op("sync", lambda g, sl=sl, r=r: g.dma_start(
                                      out=x1[r:r + 128, :], in_=xtile[sl][:]), inc=f"dx{sl}", k=16)
                                  xf[sl] = stok

                        do_scope_b()
                        P.barrier()
                        bankfree.update({i: None for i in range(8)})


                for b in range(2):
                    do_batch(b)

        def emit_dump_prep():
            with ExitStack() as cs:
                ctb = cs.enter_context(nc.sbuf_tensor("dmpb", [128, 2048], BF16))
                ctf = cs.enter_context(nc.sbuf_tensor("dmpf", [128, 2048], F32))
                free = None
                def dump(srcap, row):
                    nonlocal free
                    lt = P.op("sync", lambda s, srcap=srcap: s.dma_start(
                        out=ctb[:], in_=srcap), inc="dm", k=16)
                    P.waitt("vector", lt)
                    P.waitt("vector", free)
                    ct2 = P.op("vector", lambda v: v.tensor_copy(ctf[:], ctb[:]), inc="dve")
                    P.waitt("gpsimd", ct2)
                    f1 = P.op("gpsimd", lambda g, row=row: g.dma_start(
                        out=y_out[row:row + 128, :], in_=ctf[:, 0:1024]), inc="dm", k=16)
                    f2 = P.op("gpsimd", lambda g, row=row: g.dma_start(
                        out=y_out[row + 128:row + 256, :], in_=ctf[:, 1024:2048]), inc="dm", k=16)
                    free = f2
                    P.wait("sync", "dm", f2[1])
                dump(sg[0], 0)
                dump(sg[15], 256)
                dump(su[0], 512)
                dump(sd[0][:, 0:2048], 768)
                dump(sd[7][:, 2048:4096], 1024)
                wv_ = P.op("gpsimd", lambda g: g.dma_start(
                    out=y_out[1280:1408, 0:8], in_=wmean[:]), inc="dm", k=16)
                wv2 = P.op("gpsimd", lambda g: g.dma_start(
                    out=y_out[1408:1536, 0:16], in_=dbg[:]), inc="dm", k=16)
            P.barrier()

        def emit_copy_out():
            with ExitStack() as cs:
                ct = cs.enter_context(nc.sbuf_tensor("cpyt", [128, 1024], F32))
                free = None
                for tj in range(32):
                    P.waitt("sync", free)
                    lt = P.op("sync", lambda s, tj=tj: s.dma_start(
                        out=ct[:], in_=x1[tj * 128:(tj + 1) * 128, :]), inc="dm", k=16)
                    P.waitt("sync", lt)
                    free = P.op("sync", lambda g, tj=tj: g.dma_start(
                        out=y_out[tj * 128:(tj + 1) * 128, :], in_=ct[:]), inc="dm", k=16)
            P.barrier()

        if stage == 9:
            with ExitStack() as ms:
                mt = ms.enter_context(nc.sbuf_tensor("rb9", [128, 2048], F32))
                mb = ms.enter_context(nc.sbuf_tensor("rb9b", [128, 2048], BF16))
                for rep in range(8):
                    lt = P.op("sync", lambda s: s.dma_start(out=mb[:], in_=sd[7][:, 0:2048]), inc="dm", k=16)
                    P.waitt("vector", lt)
                    c1 = P.op("vector", lambda v: v.tensor_copy(mt[:], mb[:]), inc="dve")
                    P.waitt("scalar", c1)
                    P.op("scalar", lambda g, rep=rep: g.dma_start(
                        out=y_out[rep * 256:rep * 256 + 128, :], in_=mt[:, 0:1024]), inc="dm", k=16)
                    st9t = P.op("scalar", lambda g, rep=rep: g.dma_start(
                        out=y_out[rep * 256 + 128:rep * 256 + 256, :], in_=mt[:, 1024:2048]), inc="dm", k=16)
                    P.wait("sync", "dm", st9t[1])
                    P.op("sync", lambda s: s.nop(cycle_cnt=50000))
            P.barrier()
            with nc.Block() as block:
                block.tensor(lambda e: [f(e) for f in P.stream["tensor"]])
                block.vector(lambda e: [f(e) for f in P.stream["vector"]])
                block.scalar(lambda e: [f(e) for f in P.stream["scalar"]])
                block.gpsimd(lambda e: [f(e) for f in P.stream["gpsimd"]])
                block.sync(lambda e: [f(e) for f in P.stream["sync"]])
            return nc
        if stage == 7:
            lvl = paranoid
            with ExitStack() as ms:
                xt4 = [ms.enter_context(nc.sbuf_tensor(f"m7x{i}", [128, 1024], F32)) for i in range(4)]
                sq7 = ms.enter_context(nc.sbuf_tensor("m7sq", [128, 1024], BF16))
                sc7 = ms.enter_context(nc.sbuf_tensor("m7sc", [128, 16], F32))
                xc7 = [ms.enter_context(nc.sbuf_tensor(f"m7c{i}", [128, 1024], F32)) for i in range(2)]
                dmp = ms.enter_context(nc.sbuf_tensor("m7d", [128, 1024], F32))
                dmp2 = ms.enter_context(nc.sbuf_tensor("m7d2", [128, 16], F32))
                wdum = ms.enter_context(nc.sbuf_tensor("m7w", [128, 4096], BF16))
                snap = None
                for tj in range(4):
                    lt = P.op("sync", lambda s, tj=tj: s.dma_start(
                        out=xt4[tj][:], in_=d_in["x"][tj * 128:(tj + 1) * 128, :]), inc="dm", k=16)
                    if lvl >= 3:
                        for w in range(4):
                            P.op("sync", lambda s, tj=tj, w=w: s.dma_start(
                                out=wdum[:, 0:2048], in_=sg[(4 * tj + w) % 16]), inc="dm", k=16)
                    P.waitt("scalar", lt)
                    stok = P.op("scalar", lambda s, tj=tj: s.activation(
                        out=sq7[:], in_=xt4[tj][:], func=AF.Square,
                        accum_out=sc7[:, tj * 4:tj * 4 + 1]), inc="act")
                    P.waitt("vector", lt)
                    P.op("vector", lambda v, tj=tj: v.tensor_reduce(
                        sc7[:, tj * 4 + 1:tj * 4 + 2], xt4[tj][:], axis=AX, op=OP.add), inc="dve")
                    if lvl >= 2:
                        P.waitt("vector", stok)
                        sepv()
                        P.op("vector", lambda v, tj=tj: v.tensor_scalar(
                            out=sc7[:, tj * 4 + 2:tj * 4 + 3], in0=sc7[:, tj * 4 + 1:tj * 4 + 2],
                            scalar1=-1.0 / 1024, scalar2=None, op0=OP.mult), inc="dve")
                        P.waitt("scalar", P.mark("dve"))
                        P.op("scalar", lambda s, tj=tj: s.activation(
                            out=xc7[tj % 2][:], in_=xt4[tj][:], func=AF.Identity,
                            bias=sc7[:, tj * 4 + 2:tj * 4 + 3], scale=1.0), inc="act")
                    if tj == 0:
                        P.op("vector", lambda v: v.tensor_copy(dmp[:], xt4[0][:]), inc="dve")
                        snap = P.op("vector", lambda v: v.tensor_copy(dmp2[:], sc7[:]), inc="dve")
                P.waitt("gpsimd", snap)
                P.wait("gpsimd", "dve")
                P.op("gpsimd", lambda g: g.dma_start(out=y_out[0:128, :], in_=dmp[:]), inc="dm", k=16)
                P.op("gpsimd", lambda g: g.dma_start(out=y_out[128:256, 0:16], in_=dmp2[:]), inc="dm", k=16)
                P.op("gpsimd", lambda g: g.dma_start(out=y_out[256:384, :], in_=xt4[0][:]), inc="dm", k=16)
                P.wait("gpsimd", "dm")
            P.barrier()
            with nc.Block() as block:
                block.tensor(lambda e: [f(e) for f in P.stream["tensor"]])
                block.vector(lambda e: [f(e) for f in P.stream["vector"]])
                block.scalar(lambda e: [f(e) for f in P.stream["scalar"]])
                block.gpsimd(lambda e: [f(e) for f in P.stream["gpsimd"]])
                block.sync(lambda e: [f(e) for f in P.stream["sync"]])
            return nc
        if stage == 6:
            with ExitStack() as ms:
                mt = ms.enter_context(nc.sbuf_tensor("minitile", [128, 1024], F32))
                mdbg = ms.enter_context(nc.sbuf_tensor("minidbg", [128, 512], F32))
                lt = P.op("sync", lambda s: s.dma_start(out=mt[:], in_=d_in["x"][0:128, :]), inc="dm", k=16)
                P.waitt("vector", lt)
                ct = P.op("vector", lambda v: v.tensor_copy(mdbg[:], mt[:, 0:512]), inc="dve")
                P.waitt("gpsimd", ct)
                st1 = P.op("gpsimd", lambda g: g.dma_start(out=y_out[0:128, 0:512], in_=mdbg[:]), inc="dm", k=16)
                st2 = P.op("gpsimd", lambda g: g.dma_start(out=y_out[128:256, :], in_=mt[:]), inc="dm", k=16)
                P.wait("gpsimd", "dm")
            P.barrier()
            with nc.Block() as block:
                block.tensor(lambda e: [f(e) for f in P.stream["tensor"]])
                block.vector(lambda e: [f(e) for f in P.stream["vector"]])
                block.scalar(lambda e: [f(e) for f in P.stream["scalar"]])
                block.gpsimd(lambda e: [f(e) for f in P.stream["gpsimd"]])
                block.sync(lambda e: [f(e) for f in P.stream["sync"]])
            return nc
        if stage == 5:
            emit_ffn(5, dbg_dump=True)
            with ExitStack() as cs:
                relx = cs.enter_context(nc.sbuf_tensor("relx", [128, 1024], F32))
                rl = P.op("sync", lambda s: s.dma_start(out=relx[:], in_=d_in["x"][0:128, :]), inc="dm", k=16)
                P.waitt("gpsimd", rl)
                P.op("gpsimd", lambda g: g.dma_start(out=y_out[640:768, :], in_=relx[:]), inc="dm", k=16)
                free = None
                for (buf, row) in [(dbg_xq, 0), (dbg_vq, 128), (dbg_prod, 256), (dbg_wg, 384), (dbg_sc, 512)]:
                    w = buf.shape[1] if hasattr(buf, 'shape') else 1024
                    f1 = P.op("gpsimd", lambda g, buf=buf, row=row: g.dma_start(
                        out=y_out[row:row + 128, 0:buf.shape[1]], in_=buf[:]), inc="dm", k=16)
                P.wait("gpsimd", "dm")
            P.barrier()
            with nc.Block() as block:
                block.tensor(lambda e: [f(e) for f in P.stream["tensor"]])
                block.vector(lambda e: [f(e) for f in P.stream["vector"]])
                block.scalar(lambda e: [f(e) for f in P.stream["scalar"]])
                block.gpsimd(lambda e: [f(e) for f in P.stream["gpsimd"]])
                block.sync(lambda e: [f(e) for f in P.stream["sync"]])
            return nc
        if stage == 0:
            emit_dump_prep()
            with nc.Block() as block:
                block.tensor(lambda e: [f(e) for f in P.stream["tensor"]])
                block.vector(lambda e: [f(e) for f in P.stream["vector"]])
                block.scalar(lambda e: [f(e) for f in P.stream["scalar"]])
                block.gpsimd(lambda e: [f(e) for f in P.stream["gpsimd"]])
                block.sync(lambda e: [f(e) for f in P.stream["sync"]])
            return nc
        emit_ffn(0 if stage > 1 else 2)
        if stage >= 2:
            emit_attn()
            if stage == 2:
                if paranoid != 30:
                    emit_copy_out()
                else:
                    P.barrier()
            else:
                emit_ffn(1)
        P.barrier()

        with nc.Block() as block:
            block.tensor(lambda e: [f(e) for f in P.stream["tensor"]])
            block.vector(lambda e: [f(e) for f in P.stream["vector"]])
            block.scalar(lambda e: [f(e) for f in P.stream["scalar"]])
            block.gpsimd(lambda e: [f(e) for f in P.stream["gpsimd"]])
            block.sync(lambda e: [f(e) for f in P.stream["sync"]])
    return nc


def kernel(L0_attn_x, L0_attn_wq, L0_attn_bq, L0_attn_wqc, L0_attn_bqc,
           L0_attn_wqr, L0_attn_bqr, L0_attn_wkv, L0_attn_bkv, L0_attn_wkr,
           L0_attn_bkr, L0_attn_wkc, L0_attn_bkc, L0_attn_wv, L0_attn_bv,
           L0_attn_wo, L0_attn_bo, L0_ffn_w_gate, L0_ffn_w_up, L0_ffn_w_down):
    f32 = lambda a: np.ascontiguousarray(np.asarray(a, dtype=np.float32))
    x = f32(L0_attn_x).reshape(16, 2048, 1024)
    wq, bq = f32(L0_attn_wq), f32(L0_attn_bq)
    wqc, bqc = f32(L0_attn_wqc), f32(L0_attn_bqc)
    wqr, bqr = f32(L0_attn_wqr), f32(L0_attn_bqr)
    wkv, bkv = f32(L0_attn_wkv), f32(L0_attn_bkv)
    wkr, bkr = f32(L0_attn_wkr), f32(L0_attn_bkr)
    wkc, bkc = f32(L0_attn_wkc), f32(L0_attn_bkc)
    wv, bv = f32(L0_attn_wv), f32(L0_attn_bv)
    wo, bo = f32(L0_attn_wo), f32(L0_attn_bo)
    wg, wu, wd = f32(L0_ffn_w_gate), f32(L0_ffn_w_up), f32(L0_ffn_w_down)

    wlat = np.ascontiguousarray(np.concatenate([wq, wkv, wqr, wkr], axis=1))
    blat_vec = np.concatenate([bq, bkv, bqr, bkr])
    blat_p = np.ascontiguousarray(blat_vec.reshape(6, 128).T)
    bqc_pp = np.ascontiguousarray(bqc.reshape(7, 128).T)
    bkc_pp = np.ascontiguousarray(bkc.reshape(7, 128).T)
    bv_b = np.ascontiguousarray(np.broadcast_to(bv, (128, 1024)))
    bo_b = np.ascontiguousarray(np.broadcast_to(bo, (128, 1024)))

    inv = 1.0 / (10000.0 ** (np.arange(0, DR, 2, dtype=np.float64) / DR))
    t = np.arange(S, dtype=np.float64)
    freqs = t[:, None] * inv[None, :]
    emb = np.concatenate([freqs, freqs], axis=1)
    cosT = np.ascontiguousarray(np.cos(emb).T.astype(np.float32))
    sinT = np.ascontiguousarray(np.sin(emb).T.astype(np.float32))

    nc = build_program()
    common = dict(wg=wg, wu=wu, wd=wd, wlat=wlat, wqc=wqc, wkc=wkc, wv=wv, wo=wo,
                  blat_p=blat_p, bqc_pp=bqc_pp, bkc_pp=bkc_pp, bv_b=bv_b, bo_b=bo_b,
                  cosT=cosT, sinT=sinT)
    in_maps = []
    for c in range(N_CORES):
        m = dict(common)
        m["x"] = np.ascontiguousarray(x[2 * c:2 * c + 2].reshape(T, E))
        in_maps.append(m)
    res = run_bass_kernel_spmd(nc, in_maps, core_ids=list(range(N_CORES)))
    out = np.empty((16, 2048, 1024), dtype=np.float32)
    for c in range(N_CORES):
        out[2 * c:2 * c + 2] = np.asarray(res.results[c]["y"]).reshape(2, 2048, 1024)
    return out


# version 13
# speedup vs baseline: 2.0058x; 1.1163x over previous
import numpy as np
from contextlib import ExitStack

import concourse.bass as bass
import concourse.mybir as mybir
from concourse.bass_utils import run_bass_kernel_spmd
from concourse.alu_op_type import AluOpType as OP

F32 = mybir.dt.float32
BF16 = mybir.dt.bfloat16
AF = mybir.ActivationFunctionType
AX = mybir.AxisListType.X
MAGIC = 12582912.0
NEG_INF = -1.0e30

N_CORES = 8
T = 4096
E = 1024
S = 2048
DR = 128
INNER = 4096
TG = 512
NTJ = TG // 128

ENGS = ("tensor", "vector", "scalar", "gpsimd", "sync")


class Prog:
    def __init__(self, nc, stack):
        self.nc = nc
        self.auto_sep = True
        self.junk = {}
        self.stream = {e: [] for e in ENGS}
        self.sems = {}
        self.cnt = {}
        self.memo = {e: {} for e in ENGS}
        for name in ("pe", "dve", "act", "gp", "bar") + ('dx0', 'dx1', 'dx2', 'dx3', 'dw0', 'dw1', 'du0', 'du1', 'dd0', 'dd1', 'dm', 'dr', 'dc'):
            self.sems[name] = stack.enter_context(nc.semaphore(name))
            self.cnt[name] = 0

    def wait(self, eng, sem, val=None):
        v = self.cnt[sem] if val is None else val
        if v <= 0:
            return
        if self.memo[eng].get(sem, -1) >= v:
            return
        self.memo[eng][sem] = v
        h = self.sems[sem]
        self.stream[eng].append(lambda e, h=h, v=v: e.wait_ge(h, v))

    def waitt(self, eng, tok):
        if tok is not None:
            self.wait(eng, tok[0], tok[1])

    def op(self, eng, fn, inc=None, k=1):
        if inc is None:
            self.stream[eng].append(fn)
            self._sep(eng)
            return None
        self.cnt[inc] += k
        tok = (inc, self.cnt[inc])
        h = self.sems[inc]

        def wrapped(e, fn=fn, h=h, k=k):
            fn(e).then_inc(h, k)
        self.stream[eng].append(wrapped)
        self._sep(eng)
        return tok

    def _sep(self, eng):
        if not self.auto_sep or eng not in ("vector",):
            return
        j = self.junk.get(eng)
        if j is None:
            return
        if eng == "vector":
            self.stream[eng].append(lambda v, j=j: v.tensor_scalar(
                out=j[:], in0=j[:], scalar1=0.0, scalar2=None, op0=OP.mult))
        elif eng == "scalar":
            self.stream[eng].append(lambda s_, j=j: s_.activation(
                out=j[:], in_=j[:], func=AF.Identity, scale=0.0))
        else:
            self.stream[eng].append(lambda g, j=j: g.memset(j[:], 0.0))

    def mark(self, sem):
        return (sem, self.cnt[sem])

    DMA_SEMS = ['dx0', 'dx1', 'dx2', 'dx3', 'dw0', 'dw1', 'du0', 'du1', 'dd0', 'dd1', 'dm', 'dr', 'dc']

    def barrier(self):
        for ds in self.DMA_SEMS:
            if self.cnt[ds] > 0:
                self.wait("sync", ds)
                self.wait("gpsimd", ds)
        for e in ENGS:
            self.op(e, lambda en: en.drain(), inc="bar")
        for e in ENGS:
            self.wait(e, "bar")


def ceil_div(a, b):
    return (a + b - 1) // b


def rowtiles(hbm_ap, t0, t1, width):
    return hbm_ap.rearrange("(t p) d -> p t d", p=128)[:, t0:t1, :]


def sb3(sb_ap, nt, width):
    return sb_ap.rearrange("p (t d) -> p t d", t=nt)


def build_program(stage=3, paranoid=0):
    nc = bass.Bass("TRN2", target_bir_lowering=False)

    d_in = {}
    d_in["x"] = nc.dram_tensor("x", [T, E], F32, kind="ExternalInput").ap()
    for nm, shp in [("wg", [INNER, E]), ("wu", [INNER, E]), ("wd", [E, INNER]),
                    ("wlat", [E, 768]), ("wqc", [256, 896]), ("wkc", [256, 896]),
                    ("wv", [256, E]), ("wo", [E, E]),
                    ("blat_p", [128, 6]), ("bqc_pp", [128, 7]), ("bkc_pp", [128, 7]),
                    ("bv_b", [128, E]), ("bo_b", [128, E]),
                    ("cosT", [DR, S]), ("sinT", [DR, S])]:
        d_in[nm] = nc.dram_tensor(nm, shp, F32, kind="ExternalInput").ap()
    y_out = nc.dram_tensor("y", [T, E], F32, kind="ExternalOutput").ap()
    x1 = nc.dram_tensor("x1", [T, E], F32, kind="Internal").ap()
    sg = nc.dram_tensor("sg", [16, 128, 2048], BF16, kind="Internal").ap()
    su = nc.dram_tensor("su", [16, 128, 2048], BF16, kind="Internal").ap()
    sd = nc.dram_tensor("sd", [8, 128, 4096], BF16, kind="Internal").ap()

    with ExitStack() as stack:
        EC = stack.enter_context
        P = Prog(nc, stack)

        magic = EC(nc.sbuf_tensor("magic", [128, 1], F32))
        ident = EC(nc.sbuf_tensor("ident", [128, 128], F32))
        identb = EC(nc.sbuf_tensor("identb", [128, 128], BF16))
        maskbigT = EC(nc.sbuf_tensor("maskbigT", [128, 896], BF16))
        maskTw = [maskbigT[:, 384 - a * 128: 896 - a * 128] for a in range(4)]
        ones = EC(nc.sbuf_tensor("ones", [128, 128], F32))
        wmean = EC(nc.sbuf_tensor("wmean", [128, 8], F32))
        ln127 = EC(nc.sbuf_tensor("ln127", [128, 1], F32))
        dbg = EC(nc.sbuf_tensor("dbg", [128, 16], F32))
        junkv = EC(nc.sbuf_tensor("junkv", [128, 1], F32))
        junks = EC(nc.sbuf_tensor("junks", [128, 1], F32))
        if stage == 5:
            dbg_xq = EC(nc.sbuf_tensor("dbg_xq", [128, 512], F32))
            dbg_vq = EC(nc.sbuf_tensor("dbg_vq", [128, 512], F32))
            dbg_prod = EC(nc.sbuf_tensor("dbg_prod", [128, 256], F32))
            dbg_wg = EC(nc.sbuf_tensor("dbg_wg", [128, 512], F32))
            dbg_sc = EC(nc.sbuf_tensor("dbg_sc", [128, 40], F32))

        junkg = EC(nc.sbuf_tensor("junkg", [128, 1], F32))
        P.junk = {"vector": junkv, "scalar": junks, "gpsimd": junkg}

        if paranoid == 30:
            dbg_xq = EC(nc.sbuf_tensor("dbg_xq", [128, 256], F32))
            dbg_vq = EC(nc.sbuf_tensor("dbg_vq", [128, 128], F32))
            dbg_sc = EC(nc.sbuf_tensor("dbg_sc", [128, 8], F32))

        def sepv():
            P.op("vector", lambda v: v.tensor_scalar(
                out=junkv[:], in0=junkv[:], scalar1=0.0, scalar2=None, op0=OP.mult))

        def seps():
            P.op("scalar", lambda s: s.activation(
                out=junks[:], in_=junks[:], func=AF.Identity, scale=0.0))
        PS = [EC(nc.psum_tensor(f"ps{i}", [128, 512], F32)) for i in range(8)]
        bankfree = {i: None for i in range(8)}

        P.op("gpsimd", lambda g: g.memset(magic[:], MAGIC))
        P.op("gpsimd", lambda g: g.memset(ln127[:], float(np.log(127.0))))
        P.op("gpsimd", lambda g: g.memset(junkv[:], 1.0))
        P.op("gpsimd", lambda g: g.memset(junks[:], 1.0))
        P.op("gpsimd", lambda g: g.memset(ident[:], 0.0))
        P.op("gpsimd", lambda g: g.affine_select(
            out=ident[:], in_=ident[:], compare_op=OP.not_equal, fill=1.0,
            base=0, pattern=[[-1, 128]], channel_multiplier=1))
        P.op("gpsimd", lambda g: g.memset(identb[:], 0.0))
        P.op("gpsimd", lambda g: g.affine_select(
            out=identb[:], in_=identb[:], compare_op=OP.not_equal, fill=1.0,
            base=0, pattern=[[-1, 128]], channel_multiplier=1))
        P.op("gpsimd", lambda g: g.memset(maskbigT[:], 0.0))
        P.op("gpsimd", lambda g: g.memset(maskbigT[:, 0:384], NEG_INF))
        P.op("gpsimd", lambda g: g.affine_select(
            out=maskbigT[:, 384:512], in_=maskbigT[:, 384:512],
            compare_op=OP.is_ge, fill=NEG_INF,
            base=0, pattern=[[1, 128]], channel_multiplier=-1))
        tok_consts = P.op("gpsimd", lambda g: g.memset(ones[:], 1.0), inc="gp")
        for e in ("tensor", "scalar", "vector"):
            P.waitt(e, tok_consts)

        def transpose_tile(src_ap, dst_ap, banks, src_tok, pre_evac_tok=None):
            b = banks[transpose_tile.rr % len(banks)]
            transpose_tile.rr += 1
            P.waitt("tensor", bankfree[b])
            P.waitt("tensor", src_tok)
            ptok = P.op("tensor", lambda t, b=b, s=src_ap: t.transpose(
                PS[b][:, 0:128], s, ident[:]), inc="pe")
            P.waitt("vector", ptok)
            if pre_evac_tok is not None:
                P.waitt("vector", pre_evac_tok)
            etok = P.op("vector", lambda v, b=b, d=dst_ap: v.tensor_copy(
                d, PS[b][:, 0:128]), inc="dve")
            bankfree[b] = etok
            return etok
        transpose_tile.rr = 0

        with ExitStack() as prep:
            PE_ = prep.enter_context
            wtile = [PE_(nc.sbuf_tensor(f"wt{i}", [128, 4096], F32)) for i in range(2)]
            qtmp = PE_(nc.sbuf_tensor("qtmp", [128, 4096], F32))
            tern = PE_(nc.sbuf_tensor("tern", [128, 4096], F32))
            stg_g = [PE_(nc.sbuf_tensor(f"stg{i}", [128, 2048], BF16)) for i in range(2)]
            stg_d = [PE_(nc.sbuf_tensor(f"std{i}", [128, 4096], BF16)) for i in range(2)]
            acc = PE_(nc.sbuf_tensor("acc", [128, 4], F32))
            rtmp = PE_(nc.sbuf_tensor("rtmp", [128, 8], F32))

            wt_free = [None, None]
            mats = [("g", d_in["wg"]), ("u", d_in["wu"]), ("d", d_in["wd"])]
            for mi, (nm, wap) in enumerate(mats):
                last = None
                for tix in range(8):
                    sl = tix % 2
                    P.waitt("sync", wt_free[sl])
                    if nm == "d":
                        src = wap[tix * 128:(tix + 1) * 128, :]
                        dstv = wtile[sl][:]
                    else:
                        src = rowtiles(wap, tix * 4, tix * 4 + 4, 1024)
                        dstv = sb3(wtile[sl][:], 4, 1024)
                    ltok = P.op("sync", lambda s, src=src, dstv=dstv: s.dma_start(
                        out=dstv, in_=src), inc=f"dx{sl}", k=16)
                    P.waitt("vector", ltok)
                    rt = P.op("vector", lambda v, sl=sl, tix=tix: v.tensor_reduce(
                        rtmp[:, tix:tix + 1], wtile[sl][:], axis=AX, op=OP.add,
                        apply_absolute_value=True), inc="dve")
                    wt_free[sl] = rt
                P.op("vector", lambda v, mi=mi: v.tensor_copy(
                    dbg[:, 8 * (mi % 2):8 * (mi % 2) + 8], rtmp[:, 0:8]), inc="dve")
                last = P.op("vector", lambda v, mi=mi: v.tensor_reduce(
                    acc[:, mi:mi + 1], rtmp[:, 0:8], axis=AX, op=OP.add), inc="dve")
                P.waitt("tensor", bankfree[mi])
                P.waitt("tensor", last)
                mtok = P.op("tensor", lambda t, mi=mi: t.matmul(
                    PS[mi][:, 0:1], ones[:], acc[:, mi:mi + 1], start=True, stop=True), inc="pe")
                P.waitt("vector", mtok)
                P.op("vector", lambda v, mi=mi: v.tensor_scalar(
                    out=wmean[:, mi:mi + 1], in0=PS[mi][:, 0:1], scalar1=1.0 / (INNER * E),
                    scalar2=1e-5, op0=OP.mult, op1=OP.max), inc="dve")
                mtok2 = P.mark("dve")
                P.waitt("scalar", mtok2)
                P.op("scalar", lambda s, mi=mi: s.activation(
                    out=acc[:, 3:4], in_=wmean[:, mi:mi + 1], func=AF.Ln), inc="act")
                rtok = P.op("scalar", lambda s, mi=mi: s.activation(
                    out=wmean[:, 3 + mi:4 + mi], in_=acc[:, 3:4], func=AF.Exp,
                    scale=-1.0), inc="act")
                P.waitt("vector", rtok)
                P.op("vector", lambda v, mi=mi: v.tensor_tensor(
                    out=acc[:, 3:4], in0=wmean[:, mi:mi + 1], in1=wmean[:, 3 + mi:4 + mi],
                    op=OP.mult), inc="dve")
                P.op("vector", lambda v: v.tensor_scalar(
                    out=acc[:, 3:4], in0=acc[:, 3:4], scalar1=-1.0, scalar2=2.0,
                    op0=OP.mult, op1=OP.add), inc="dve")
                etok = P.op("vector", lambda v, mi=mi: v.tensor_tensor(
                    out=wmean[:, 3 + mi:4 + mi], in0=wmean[:, 3 + mi:4 + mi], in1=acc[:, 3:4],
                    op=OP.mult), inc="dve")
                bankfree[mi] = etok
            P.op("vector", lambda v: v.tensor_copy(wmean[:, 6:7], acc[:, 0:1]), inc="dve")
            P.op("vector", lambda v: v.tensor_copy(wmean[:, 7:8], acc[:, 2:3]), inc="dve")
            means_tok = P.mark("dve")

            def quant_tile(sl, mi, width, ld_tok):
                P.waitt("scalar", ld_tok)
                P.waitt("scalar", means_tok)
                P.waitt("scalar", quant_tile.tern_free)
                atok = P.op("scalar", lambda s, sl=sl, mi=mi, w=width: s.activation(
                    out=qtmp[:, 0:w], in_=wtile[sl][:, 0:w], func=AF.Identity,
                    bias=magic[:, 0:1], scale=wmean[:, 3 + mi:4 + mi]), inc="act")
                P.waitt("vector", atok)
                P.op("vector", lambda v, w=width: v.tensor_scalar(
                    out=qtmp[:, 0:w], in0=qtmp[:, 0:w], scalar1=MAGIC, scalar2=-1.0,
                    op0=OP.subtract, op1=OP.max), inc="dve")
                return P.op("vector", lambda v, w=width: v.tensor_scalar(
                    out=tern[:, 0:w], in0=qtmp[:, 0:w], scalar1=1.0, scalar2=None,
                    op0=OP.min), inc="dve")
            quant_tile.tern_free = None

            stg_free = [None, None]
            for mi, scratch in [(0, sg), (1, su)]:
                wap = d_in["wg"] if mi == 0 else d_in["wu"]
                for j in range(16):
                    ssl = j % 2
                    for half in range(2):
                        r = 2 * j + half
                        sl = r % 2
                        P.waitt("sync", wt_free[sl])
                        src = wap[r * 128:(r + 1) * 128, :]
                        ltok = P.op("sync", lambda s, sl=sl, src=src: s.dma_start(
                            out=wtile[sl][:, 0:1024], in_=src), inc=f"dx{sl}", k=16)
                        ttok = quant_tile(sl, mi, 1024, ltok)
                        wt_free[sl] = ttok
                        for k in range(8):
                            transpose_tile(
                                tern[:, k * 128:(k + 1) * 128],
                                stg_g[ssl][:, k * 256 + half * 128: k * 256 + half * 128 + 128],
                                [6, 7], ttok, pre_evac_tok=stg_free[ssl])
                        quant_tile.tern_free = P.mark("pe")
                    lastev = P.mark("dve")
                    P.waitt("sync", lastev)
                    sttok = P.op("sync", lambda g, ssl=ssl, j=j, scratch=scratch: g.dma_start(
                        out=scratch[j], in_=stg_g[ssl][:]), inc=f"dw{ssl}", k=16)
                    stg_free[ssl] = sttok
            std_free = [None, None]
            for r in range(8):
                sl = r % 2
                ssl = r % 2
                P.waitt("sync", wt_free[sl])
                src = d_in["wd"][r * 128:(r + 1) * 128, :]
                ltok = P.op("sync", lambda s, sl=sl, src=src: s.dma_start(
                    out=wtile[sl][:], in_=src), inc=f"dx{sl}", k=16)
                ttok = quant_tile(sl, 2, 4096, ltok)
                wt_free[sl] = ttok
                for kc in range(32):
                    transpose_tile(tern[:, kc * 128:(kc + 1) * 128],
                                   stg_d[ssl][:, kc * 128:(kc + 1) * 128], [6, 7], ttok,
                                   pre_evac_tok=std_free[ssl])
                quant_tile.tern_free = P.mark("pe")
                lastev = P.mark("dve")
                P.waitt("sync", lastev)
                stok = P.op("sync", lambda g, r=r, ssl=ssl: g.dma_start(
                    out=sd[r], in_=stg_d[ssl][:]), inc=f"du{ssl}", k=16)
                std_free[ssl] = stok
            P.barrier()
            bankfree.update({i: None for i in range(8)})

        def emit_ffn(call, dbg_dump=False):
            def pbar(bit=0):
                if 0 < paranoid < 256 and paranoid not in (10, 20) and (paranoid >> bit) & 1:
                    P.barrier()
            src = d_in["x"] if call in (0, 2) else x1
            dst = x1 if call in (0, 5) else y_out
            if call == 5:
                call = 0
            with ExitStack() as fs:
                FE = fs.enter_context
                xtile = [FE(nc.sbuf_tensor(f"f{call}xt{i}", [128, 1024], F32)) for i in range(4)]
                sqtr = FE(nc.sbuf_tensor(f"f{call}sqtr", [128, 1024], BF16))
                xc = [FE(nc.sbuf_tensor(f"f{call}xc{i}", [128, 1024], F32)) for i in range(4)]
                xq = [FE(nc.sbuf_tensor(f"f{call}xq{i}", [128, 1024], F32)) for i in range(2)]
                xqT = FE(nc.sbuf_tensor(f"f{call}xqT", [128, NTJ * 8 * 128], BF16))
                prod = FE(nc.sbuf_tensor(f"f{call}prod", [128, NTJ * 4096], F32))
                qv = FE(nc.sbuf_tensor(f"f{call}qv", [128, 2048], F32))
                vq = FE(nc.sbuf_tensor(f"f{call}vq", [128, 4096], F32))
                vqT = FE(nc.sbuf_tensor(f"f{call}vqT", [128, NTJ * 32 * 128], BF16))
                wbg = [FE(nc.sbuf_tensor(f"f{call}wbg{i}", [128, 2048], BF16)) for i in range(2)]
                wbu = [FE(nc.sbuf_tensor(f"f{call}wbu{i}", [128, 2048], BF16)) for i in range(2)]
                wbd = [FE(nc.sbuf_tensor(f"f{call}wbd{i}", [128, 4096], BF16)) for i in range(2)]
                sgb = FE(nc.sbuf_tensor(f"f{call}sgb", [128, 512], F32))
                sc = FE(nc.sbuf_tensor(f"f{call}sc", [128, NTJ * 16], F32))
                vep = FE(nc.sbuf_tensor(f"f{call}vep", [128, 2 * NTJ], F32))
                qsc = FE(nc.sbuf_tensor(f"f{call}qsc", [128, 24], F32))

                xfree = [None] * 4
                wbg_free = [None, None]
                wbu_free = [None, None]
                wbd_free = [None, None]
                xc_free = [None] * 4
                qvx_free = [None, None]
                sgb_free = [None, None]
                act_guard = None
                pe_guard = None
                SSQ, SX, MEAN, NMEAN, RMX, SXQ, GSC, USC, RSTD, T0, T1, VMX, SVQ, DSC, SCR = range(15)

                def col(tj, c):
                    return sc[:, tj * 16 + c: tj * 16 + c + 1]

                tg_order = list(reversed(range(8))) if paranoid == 20 else list(range(8))
                for tg in tg_order:
                    pbar(0)
                    row0 = tg * TG
                    xq_rdy = [None] * NTJ
                    P.waitt("vector", act_guard)
                    P.waitt("vector", pe_guard)
                    for tj in range(NTJ):
                        P.waitt("sync", xfree[tj])
                        ltok = P.op("sync", lambda s, tj=tj, r=row0 + tj * 128, src=src: s.dma_start(
                            out=xtile[tj][:], in_=src[r:r + 128, :]), inc=f"dx{tj}", k=16)
                        if paranoid == 11 and tg == 0 and tj == 0:
                            P.waitt("vector", ltok)
                            P.op("vector", lambda v: v.tensor_copy(dbg_vq[:], xtile[0][:, 0:512]), inc="dve")
                        if paranoid == 11:
                            continue
                        if dbg_dump and tg == 0 and tj == 0:
                            P.waitt("vector", ltok)
                            P.op("vector", lambda v: v.tensor_copy(dbg_vq[:], xtile[0][:, 0:512]), inc="dve")
                            for _ in range(200):
                                P.op("vector", lambda v: v.tensor_scalar(
                                    out=junkv[:], in0=junkv[:], scalar1=0.0, scalar2=None, op0=OP.mult))
                            P.op("vector", lambda v: v.tensor_copy(dbg_prod[:], xtile[0][:, 0:256]), inc="dve")
                            for _ in range(1000):
                                P.op("vector", lambda v: v.tensor_scalar(
                                    out=junkv[:], in0=junkv[:], scalar1=0.0, scalar2=None, op0=OP.mult))
                            P.op("vector", lambda v: v.tensor_copy(dbg_xq[:], xtile[0][:, 0:512]), inc="dve")
                        P.waitt("scalar", ltok)
                        stok = P.op("scalar", lambda s, tj=tj: s.activation(
                            out=sqtr[:], in_=xtile[tj][:], func=AF.Square,
                            accum_out=col(tj, SSQ)), inc="act")
                        P.waitt("vector", ltok)
                        P.op("vector", lambda v, tj=tj: v.tensor_reduce(
                            col(tj, SX), xtile[tj][:], axis=AX, op=OP.add), inc="dve")
                        P.waitt("vector", stok)
                        sepv()
                        P.op("vector", lambda v, tj=tj: v.tensor_scalar(
                            out=col(tj, MEAN), in0=col(tj, SX), scalar1=1.0 / 1024, scalar2=None,
                            op0=OP.mult), inc="dve")
                        P.op("vector", lambda v, tj=tj: v.tensor_scalar(
                            out=col(tj, T0), in0=col(tj, SSQ), scalar1=1.0 / 1024, scalar2=None,
                            op0=OP.mult), inc="dve")
                        P.op("vector", lambda v, tj=tj: v.tensor_tensor(
                            out=col(tj, T1), in0=col(tj, MEAN), in1=col(tj, MEAN), op=OP.mult), inc="dve")
                        P.op("vector", lambda v, tj=tj: v.tensor_tensor(
                            out=col(tj, T0), in0=col(tj, T0), in1=col(tj, T1), op=OP.subtract), inc="dve")
                        P.op("vector", lambda v, tj=tj: v.tensor_scalar(
                            out=qsc[:, 4 + tj:5 + tj], in0=col(tj, T0), scalar1=1e-5, scalar2=None,
                            op0=OP.add), inc="dve")
                        nmtok = P.op("vector", lambda v, tj=tj: v.tensor_scalar(
                            out=col(tj, NMEAN), in0=col(tj, MEAN), scalar1=-1.0, scalar2=None,
                            op0=OP.mult), inc="dve")
                        if dbg_dump and tg == 0 and tj == 0:
                            P.op("vector", lambda v: v.tensor_copy(dbg_sc[:, 24:40], sc[:, 0:16]), inc="dve")
                            P.op("vector", lambda v: v.tensor_copy(dbg_wg[:], xtile[0][:, 0:512]), inc="dve")
                            nmtok = P.mark("dve")
                        P.waitt("scalar", nmtok)
                        P.waitt("scalar", xc_free[tj])
                        xctok = P.op("scalar", lambda s, tj=tj: s.activation(
                            out=xc[tj][:], in_=xtile[tj][:], func=AF.Identity,
                            bias=col(tj, NMEAN), scale=1.0), inc="act")
                        P.waitt("vector", xctok)
                        P.op("vector", lambda v, tj=tj: v.tensor_reduce(
                            col(tj, RMX), xc[tj][:], axis=AX, op=OP.max,
                            apply_absolute_value=True), inc="dve")
                        sepv()
                        P.op("vector", lambda v, tj=tj: v.tensor_scalar(
                            out=qsc[:, tj:tj + 1], in0=col(tj, RMX), scalar1=1e-9, scalar2=None,
                            op0=OP.max), inc="dve")
                    pbar(1)
                    a2_in = P.mark("dve")
                    P.waitt("scalar", a2_in)
                    P.op("scalar", lambda s: s.activation(
                        out=qsc[:, 8:16], in_=qsc[:, 0:8], func=AF.Ln), inc="act")
                    P.op("scalar", lambda s: s.activation(
                        out=qsc[:, 16:20], in_=qsc[:, 8:12], func=AF.Exp,
                        scale=-1.0, bias=ln127[:, 0:1]), inc="act")
                    a2tok = P.op("scalar", lambda s: s.activation(
                        out=qsc[:, 20:24], in_=qsc[:, 12:16], func=AF.Exp,
                        scale=-0.5), inc="act")
                    P.waitt("vector", a2tok)
                    for tj in range(NTJ):
                        P.op("vector", lambda v, tj=tj: v.tensor_tensor(
                            out=col(tj, T1), in0=qsc[:, tj:tj + 1], in1=qsc[:, 16 + tj:17 + tj],
                            op=OP.mult), inc="dve")
                        P.op("vector", lambda v, tj=tj: v.tensor_scalar(
                            out=col(tj, T1), in0=col(tj, T1), scalar1=-1.0 / 127, scalar2=2.0,
                            op0=OP.mult, op1=OP.add), inc="dve")
                        sxtok = P.op("vector", lambda v, tj=tj: v.tensor_tensor(
                            out=col(tj, SXQ), in0=qsc[:, 16 + tj:17 + tj], in1=col(tj, T1),
                            op=OP.mult), inc="dve")
                        P.waitt("scalar", sxtok)
                        P.waitt("scalar", qvx_free[tj % 2])
                        q1tok = P.op("scalar", lambda s, tj=tj: s.activation(
                            out=qv[:, (tj % 2) * 1024:(tj % 2) * 1024 + 1024], in_=xc[tj][:],
                            func=AF.Identity, bias=magic[:, 0:1], scale=col(tj, SXQ)), inc="act")
                        xc_free[tj] = q1tok
                        P.waitt("vector", q1tok)
                        xq_tok = P.op("vector", lambda v, tj=tj: v.tensor_scalar(
                            out=xq[tj % 2][:], in0=qv[:, (tj % 2) * 1024:(tj % 2) * 1024 + 1024],
                            scalar1=MAGIC, scalar2=None, op0=OP.subtract), inc="dve")
                        qvx_free[tj % 2] = xq_tok
                        if dbg_dump and tg == 0 and tj == 0:
                            P.op("vector", lambda v: v.tensor_copy(dbg_xq[:], xq[0][:, 0:512]), inc="dve")
                            P.op("vector", lambda v: v.tensor_copy(dbg_sc[:, 0:24], qsc[:]), inc="dve")
                            qvx_free[0] = P.mark("dve")
                        for k in range(8):
                            xq_rdy[tj] = transpose_tile(
                                xq[tj % 2][:, k * 128:(k + 1) * 128],
                                xqT[:, (tj * 8 + k) * 128:(tj * 8 + k) * 128 + 128],
                                [6, 7], xq_tok)
                    pbar(2)
                    for tj in range(NTJ):
                        P.op("vector", lambda v, tj=tj: v.tensor_tensor(
                            out=col(tj, T1), in0=qsc[:, 20 + tj:21 + tj], in1=qsc[:, 20 + tj:21 + tj],
                            op=OP.mult), inc="dve")
                        P.op("vector", lambda v, tj=tj: v.tensor_tensor(
                            out=col(tj, T1), in0=col(tj, T1), in1=qsc[:, 4 + tj:5 + tj], op=OP.mult), inc="dve")
                        P.op("vector", lambda v, tj=tj: v.tensor_scalar(
                            out=col(tj, T1), in0=col(tj, T1), scalar1=-0.5, scalar2=1.5,
                            op0=OP.mult, op1=OP.add), inc="dve")
                        P.op("vector", lambda v, tj=tj: v.tensor_tensor(
                            out=col(tj, RSTD), in0=qsc[:, 20 + tj:21 + tj], in1=col(tj, T1), op=OP.mult), inc="dve")
                        P.op("vector", lambda v, tj=tj: v.tensor_tensor(
                            out=col(tj, T0), in0=col(tj, RMX), in1=col(tj, RSTD), op=OP.mult), inc="dve")
                        P.op("vector", lambda v, tj=tj: v.tensor_scalar(
                            out=col(tj, T0), in0=col(tj, T0), scalar1=1e-5, scalar2=None,
                            op0=OP.max), inc="dve")
                        P.op("vector", lambda v, tj=tj: v.tensor_tensor(
                            out=col(tj, GSC), in0=col(tj, T0), in1=wmean[:, 0:1], op=OP.mult), inc="dve")
                        P.op("vector", lambda v, tj=tj: v.tensor_scalar(
                            out=col(tj, GSC), in0=col(tj, GSC), scalar1=1.0 / 127, scalar2=None,
                            op0=OP.mult), inc="dve")
                        P.op("vector", lambda v, tj=tj: v.tensor_tensor(
                            out=col(tj, USC), in0=col(tj, T0), in1=wmean[:, 1:2], op=OP.mult), inc="dve")
                        P.op("vector", lambda v, tj=tj: v.tensor_scalar(
                            out=col(tj, USC), in0=col(tj, USC), scalar1=1.0 / 127, scalar2=None,
                            op0=OP.mult), inc="dve")
                    scl_tok = P.mark("dve")
                    if paranoid == 10:
                        act_guard = P.mark("act")
                        pe_guard = P.mark("pe")
                        continue
                    pbar(3)
                    prod_last = [None] * NTJ
                    mmtok = None
                    for n2 in range(16):
                        gsl = n2 % 2
                        P.waitt("sync", wbg_free[gsl])
                        gtok = P.op("sync", lambda s, gsl=gsl, n2=n2: s.dma_start(
                            out=wbg[gsl][:], in_=sg[n2]), inc=f"dw{gsl}", k=16)
                        P.waitt("sync", wbu_free[gsl])
                        utok = P.op("sync", lambda s, gsl=gsl, n2=n2: s.dma_start(
                            out=wbu[gsl][:], in_=su[n2]), inc=f"du{gsl}", k=16)
                        for tj in range(NTJ):
                            bg, bu = 2 * tj, 2 * tj + 1
                            P.waitt("tensor", bankfree[bg])
                            P.waitt("tensor", bankfree[bu])
                            P.waitt("tensor", xq_rdy[tj])
                            P.waitt("tensor", gtok)
                            P.waitt("tensor", utok)
                            for k in range(8):
                                lhs = xqT[:, (tj * 8 + k) * 128:(tj * 8 + k) * 128 + 128]
                                P.op("tensor", lambda t, lhs=lhs, bg=bg, gsl=gsl, k=k: t.matmul(
                                    PS[bg][:, 0:256], lhs, wbg[gsl][:, k * 256:(k + 1) * 256],
                                    start=(k == 0), stop=(k == 7)))
                                fn = (lambda t, lhs=lhs, bu=bu, gsl=gsl, k=k: t.matmul(
                                    PS[bu][:, 0:256], lhs, wbu[gsl][:, k * 256:(k + 1) * 256],
                                    start=(k == 0), stop=(k == 7)))
                                if k == 7:
                                    mmtok = P.op("tensor", fn, inc="pe")
                                else:
                                    P.op("tensor", fn)
                            wbg_free[gsl] = mmtok
                            wbu_free[gsl] = mmtok
                            P.waitt("scalar", mmtok)
                            P.waitt("scalar", scl_tok)
                            P.waitt("scalar", sgb_free[tj % 2])
                            sltok = P.op("scalar", lambda s, bg=bg, tj=tj: s.activation(
                                out=sgb[:, (tj % 2) * 256:(tj % 2) * 256 + 256],
                                in_=PS[bg][:, 0:256], func=AF.Silu, scale=col(tj, GSC)), inc="act")
                            P.waitt("vector", sltok)
                            ptok = P.op("vector", lambda v, bu=bu, tj=tj, n2=n2: v.scalar_tensor_tensor(
                                out=prod[:, tj * 4096 + n2 * 256: tj * 4096 + n2 * 256 + 256],
                                in0=PS[bu][:, 0:256], scalar=col(tj, USC),
                                in1=sgb[:, (tj % 2) * 256:(tj % 2) * 256 + 256],
                                op0=OP.mult, op1=OP.mult), inc="dve")
                            bankfree[bg] = ptok
                            bankfree[bu] = ptok
                            sgb_free[tj % 2] = ptok
                            prod_last[tj] = ptok
                            if dbg_dump and tg == 0 and tj == 0 and n2 == 0:
                                P.op("vector", lambda v: v.tensor_copy(dbg_prod[:], prod[:, 0:256]), inc="dve")
                                prod_last[0] = P.mark("dve")
                    pbar(4)
                    vqT_rdy = [None] * NTJ
                    qvv_free = qvx_free[0]
                    for tj in range(NTJ):
                        P.waitt("vector", prod_last[tj])
                        P.op("vector", lambda v, tj=tj: v.tensor_reduce(
                            col(tj, VMX), prod[:, tj * 4096:(tj + 1) * 4096], axis=AX,
                            op=OP.max, apply_absolute_value=True), inc="dve")
                        sepv()
                        P.op("vector", lambda v, tj=tj: v.tensor_scalar(
                            out=col(tj, VMX), in0=col(tj, VMX), scalar1=1e-5, scalar2=None,
                            op0=OP.max), inc="dve")
                        vmx_tok = P.mark("dve")
                        P.waitt("scalar", vmx_tok)
                        P.op("scalar", lambda s, tj=tj: s.activation(
                            out=col(tj, SCR), in_=col(tj, VMX), func=AF.Ln), inc="act")
                        sv0 = P.op("scalar", lambda s, tj=tj: s.activation(
                            out=col(tj, T0), in_=col(tj, SCR), func=AF.Exp,
                            scale=-1.0, bias=ln127[:, 0:1]), inc="act")
                        P.waitt("vector", sv0)
                        P.op("vector", lambda v, tj=tj: v.tensor_tensor(
                            out=col(tj, T1), in0=col(tj, VMX), in1=col(tj, T0), op=OP.mult), inc="dve")
                        P.op("vector", lambda v, tj=tj: v.tensor_scalar(
                            out=col(tj, T1), in0=col(tj, T1), scalar1=-1.0 / 127, scalar2=2.0,
                            op0=OP.mult, op1=OP.add), inc="dve")
                        svqtok = P.op("vector", lambda v, tj=tj: v.tensor_tensor(
                            out=col(tj, SVQ), in0=col(tj, T0), in1=col(tj, T1), op=OP.mult), inc="dve")
                        P.op("vector", lambda v, tj=tj: v.tensor_tensor(
                            out=col(tj, DSC), in0=col(tj, VMX), in1=wmean[:, 2:3], op=OP.mult), inc="dve")
                        P.op("vector", lambda v, tj=tj: v.tensor_scalar(
                            out=col(tj, DSC), in0=col(tj, DSC), scalar1=0.5 / 127, scalar2=None,
                            op0=OP.mult), inc="dve")
                        for h2 in range(2):
                            P.waitt("scalar", svqtok)
                            P.waitt("scalar", qvv_free)
                            P.waitt("scalar", qvx_free[1])
                            q2tok = P.op("scalar", lambda s, tj=tj, h2=h2: s.activation(
                                out=qv[:], in_=prod[:, tj * 4096 + h2 * 2048: tj * 4096 + (h2 + 1) * 2048],
                                func=AF.Identity, bias=magic[:, 0:1], scale=col(tj, SVQ)), inc="act")
                            P.waitt("vector", q2tok)
                            vq_tok = P.op("vector", lambda v, h2=h2: v.tensor_scalar(
                                out=vq[:, h2 * 2048:(h2 + 1) * 2048], in0=qv[:],
                                scalar1=MAGIC, scalar2=None, op0=OP.subtract), inc="dve")
                            qvv_free = vq_tok
                            qvx_free[0] = vq_tok
                            qvx_free[1] = vq_tok

                            for k in range(16):
                                kc = h2 * 16 + k
                                vqT_rdy[tj] = transpose_tile(
                                    vq[:, kc * 128:(kc + 1) * 128],
                                    vqT[:, (tj * 32 + kc) * 128:(tj * 32 + kc) * 128 + 128],
                                    [6, 7], vq_tok)
                    pbar(5)
                    dtok = None
                    for u8 in range(8):
                        dsl = u8 % 2
                        P.waitt("sync", wbd_free[dsl])
                        sdv = sd.rearrange("r p (kc m) -> p kc r m", kc=32)
                        for kc_ in range(4):
                            src3 = sdv[:, u8 * 4 + kc_, :, :]
                            dst3 = wbd[dsl][:, kc_ * 1024:(kc_ + 1) * 1024].rearrange(
                                "p (b m) -> p b m", b=8)
                            dltok = P.op("sync", lambda s, src3=src3, dst3=dst3: s.dma_start(
                                out=dst3, in_=src3), inc=f"dd{dsl}", k=16)
                        P.waitt("tensor", dltok)
                        if u8 == 0:
                            for tj in range(NTJ):
                                P.waitt("tensor", bankfree[2 * tj])
                                P.waitt("tensor", bankfree[2 * tj + 1])
                                P.waitt("tensor", vqT_rdy[tj])
                        for kc in range(4):
                            kca = u8 * 4 + kc
                            for tj in range(NTJ):
                                lhs = vqT[:, (tj * 32 + kca) * 128:(tj * 32 + kca) * 128 + 128]
                                P.op("tensor", lambda t, lhs=lhs, tj=tj, dsl=dsl, kc=kc, kca=kca: t.matmul(
                                    PS[2 * tj][:, 0:512], lhs, wbd[dsl][:, kc * 1024:kc * 1024 + 512],
                                    start=(kca == 0), stop=(kca == 31)))
                                fn = (lambda t, lhs=lhs, tj=tj, dsl=dsl, kc=kc, kca=kca: t.matmul(
                                    PS[2 * tj + 1][:, 0:512], lhs,
                                    wbd[dsl][:, kc * 1024 + 512:kc * 1024 + 1024],
                                    start=(kca == 0), stop=(kca == 31)))
                                if kc == 3 and tj == NTJ - 1:
                                    dtok = P.op("tensor", fn, inc="pe")
                                else:
                                    P.op("tensor", fn)
                        wbd_free[dsl] = dtok
                    dfin = P.mark("pe")
                    pbar(6)
                    for tj in range(NTJ):
                        P.waitt("vector", dfin)
                        etok = None
                        for half in range(2):
                            b = 2 * tj + half
                            etok = P.op("vector", lambda v, tj=tj, b=b, half=half: v.scalar_tensor_tensor(
                                out=xtile[tj][:, half * 512:(half + 1) * 512],
                                in0=PS[b][:, 0:512], scalar=col(tj, DSC),
                                in1=xtile[tj][:, half * 512:(half + 1) * 512],
                                op0=OP.mult, op1=OP.add), inc="dve")
                            bankfree[b] = etok
                        P.waitt("sync", etok)
                        stok = P.op("sync", lambda g, tj=tj, r=row0 + tj * 128, dst=dst: g.dma_start(
                            out=dst[r:r + 128, :], in_=xtile[tj][:]), inc=f"dx{tj}", k=16)
                        xfree[tj] = stok
                    act_guard = P.mark("act")
                    pe_guard = P.mark("pe")
            P.barrier()
            bankfree.update({i: None for i in range(8)})

        def emit_attn():
            with ExitStack() as as_:
                AE = as_.enter_context
                wlat_sb = AE(nc.sbuf_tensor("wlat_sb", [128, 8 * 768], BF16))
                wqc_sb = AE(nc.sbuf_tensor("wqc_sb", [128, 2 * 896], BF16))
                wkc_sb = AE(nc.sbuf_tensor("wkc_sb", [128, 2 * 896], BF16))
                wv_sb = AE(nc.sbuf_tensor("wv_sb", [128, 2 * 1024], BF16))
                blat = AE(nc.sbuf_tensor("blat", [128, 6], F32))
                bqcp = AE(nc.sbuf_tensor("bqcp", [128, 7], F32))
                bkcp = AE(nc.sbuf_tensor("bkcp", [128, 7], F32))
                bvb = AE(nc.sbuf_tensor("bvb", [128, 1024], F32))
                bob = AE(nc.sbuf_tensor("bob", [128, 1024], F32))
                wstage = AE(nc.sbuf_tensor("wstage", [128, 1024], F32))

                def load_cast(dst_sb, hbm, nch, width):
                    last = None
                    for k in range(nch):
                        src = rowtiles(hbm, k, k + 1, width)
                        lt = P.op("sync", lambda s, src=src, w=width: s.dma_start(
                            out=sb3(wstage[:, 0:w], 1, w), in_=src), inc="dm", k=16)
                        P.waitt("vector", lt)
                        last = P.op("vector", lambda v, dst=dst_sb, k=k, w=width: v.tensor_copy(
                            dst[:, k * w:(k + 1) * w], wstage[:, 0:w]), inc="dve")
                        P.waitt("sync", last)
                    return last

                load_cast(wlat_sb, d_in["wlat"], 8, 768)
                load_cast(wqc_sb, d_in["wqc"], 2, 896)
                load_cast(wkc_sb, d_in["wkc"], 2, 896)
                load_cast(wv_sb, d_in["wv"], 2, 1024)
                for dstb, nm in [(blat, "blat_p"), (bqcp, "bqc_pp"), (bkcp, "bkc_pp"),
                                 (bvb, "bv_b"), (bob, "bo_b")]:
                    P.op("sync", lambda s, dstb=dstb, nm=nm: s.dma_start(
                        out=dstb[:], in_=d_in[nm]), inc="dm", k=16)
                wload_ld = P.mark("dm")
                wload_cast = P.mark("dve")

                def do_batch(b):
                    brow = b * S
                    with ExitStack() as bs:
                        BE = bs.enter_context
                        QT = BE(nc.sbuf_tensor(f"QT{b}", [128, 8 * 2048], BF16))
                        KT = BE(nc.sbuf_tensor(f"KT{b}", [128, 8 * 2048], BF16))
                        VB2 = BE(nc.sbuf_tensor(f"VB{b}", [128, 16 * 16 * 65], BF16))
                        VB2r = VB2.rearrange("p (k c) -> p k c", c=65)
                        P.op("gpsimd", lambda g: g.memset(VB2r[:, :, 64:65], 1.0))

                        def do_scope_a():
                          with ExitStack() as sa:
                              SA = sa.enter_context
                              xtile = [SA(nc.sbuf_tensor(f"a{b}xt{i}", [128, 1024], F32)) for i in range(4)]
                              sqtr = SA(nc.sbuf_tensor(f"a{b}sqtr", [128, 1024], BF16))
                              xn = [SA(nc.sbuf_tensor(f"a{b}xn{i}", [128, 1024], F32)) for i in range(2)]
                              xlnT = SA(nc.sbuf_tensor(f"a{b}xlnT", [128, 8 * 512], BF16))
                              cqg = SA(nc.sbuf_tensor(f"a{b}cqg", [128, 2 * 2048], BF16))
                              ckvg = SA(nc.sbuf_tensor(f"a{b}ckvg", [128, 2 * 2048], BF16))
                              cosb = SA(nc.sbuf_tensor(f"a{b}cosb", [128, 512], F32))
                              sinb = SA(nc.sbuf_tensor(f"a{b}sinb", [128, 512], F32))
                              rotb = SA(nc.sbuf_tensor(f"a{b}rotb", [128, 512], F32))
                              rawq = SA(nc.sbuf_tensor(f"a{b}rawq", [128, 512], F32))
                              rawk = SA(nc.sbuf_tensor(f"a{b}rawk", [128, 512], F32))
                              sc = SA(nc.sbuf_tensor(f"a{b}sc", [128, 64], F32))
                              vep = SA(nc.sbuf_tensor(f"a{b}vep", [128, 12], F32))

                              SSQ, SX, MEAN, NMEAN, RSTD, NMR, T0, T1 = range(8)

                              def acol(tj, c):
                                  return sc[:, tj * 8 + c: tj * 8 + c + 1]

                              lat_free = {m: None for m in range(6)}
                              lat_prev = None
                              xn_free = [None, None]
                              raw_free = [None, None]
                              rot_free = None
                              tab_free = None
                              a_act_guard = None
                              for tg in range(4):
                                  P.waitt("vector", a_act_guard)
                                  for tj in range(4):
                                      r = brow + tg * 512 + tj * 128
                                      ltok = P.op("sync", lambda s, tj=tj, r=r: s.dma_start(
                                          out=xtile[tj][:], in_=x1[r:r + 128, :]), inc=f"dx{tj}", k=16)
                                      P.waitt("scalar", ltok)
                                      stok = P.op("scalar", lambda s, tj=tj: s.activation(
                                          out=sqtr[:], in_=xtile[tj][:], func=AF.Square,
                                          accum_out=acol(tj, SSQ)), inc="act")
                                      P.waitt("vector", ltok)
                                      P.op("vector", lambda v, tj=tj: v.tensor_reduce(
                                          acol(tj, SX), xtile[tj][:], axis=AX, op=OP.add), inc="dve")
                                      P.waitt("vector", stok)
                                      sepv()
                                      P.op("vector", lambda v, tj=tj: v.tensor_scalar(
                                          out=acol(tj, MEAN), in0=acol(tj, SX), scalar1=1.0 / 1024,
                                          scalar2=None, op0=OP.mult), inc="dve")
                                      P.op("vector", lambda v, tj=tj: v.tensor_scalar(
                                          out=acol(tj, T0), in0=acol(tj, SSQ), scalar1=1.0 / 1024,
                                          scalar2=None, op0=OP.mult), inc="dve")
                                      P.op("vector", lambda v, tj=tj: v.tensor_tensor(
                                          out=acol(tj, T1), in0=acol(tj, MEAN), in1=acol(tj, MEAN),
                                          op=OP.mult), inc="dve")
                                      P.op("vector", lambda v, tj=tj: v.tensor_tensor(
                                          out=acol(tj, T0), in0=acol(tj, T0), in1=acol(tj, T1),
                                          op=OP.subtract), inc="dve")
                                      P.op("vector", lambda v, tj=tj: v.tensor_scalar(
                                          out=vep[:, tj:tj + 1], in0=acol(tj, T0), scalar1=1e-5,
                                          scalar2=None, op0=OP.add), inc="dve")
                                  sq_in = P.mark("dve")
                                  P.waitt("scalar", sq_in)
                                  P.op("scalar", lambda s: s.activation(
                                      out=vep[:, 4:8], in_=vep[:, 0:4], func=AF.Ln), inc="act")
                                  sqtok = P.op("scalar", lambda s: s.activation(
                                      out=vep[:, 8:12], in_=vep[:, 4:8], func=AF.Exp,
                                      scale=-0.5), inc="act")
                                  P.waitt("vector", sqtok)
                                  for tj in range(4):
                                      P.op("vector", lambda v, tj=tj: v.tensor_tensor(
                                          out=acol(tj, T1), in0=vep[:, 8 + tj:9 + tj], in1=vep[:, 8 + tj:9 + tj],
                                          op=OP.mult), inc="dve")
                                      P.op("vector", lambda v, tj=tj: v.tensor_tensor(
                                          out=acol(tj, T1), in0=acol(tj, T1), in1=vep[:, tj:tj + 1],
                                          op=OP.mult), inc="dve")
                                      P.op("vector", lambda v, tj=tj: v.tensor_scalar(
                                          out=acol(tj, T1), in0=acol(tj, T1), scalar1=-0.5, scalar2=1.5,
                                          op0=OP.mult, op1=OP.add), inc="dve")
                                      P.op("vector", lambda v, tj=tj: v.tensor_tensor(
                                          out=acol(tj, RSTD), in0=vep[:, 8 + tj:9 + tj], in1=acol(tj, T1),
                                          op=OP.mult), inc="dve")
                                      P.op("vector", lambda v, tj=tj: v.tensor_tensor(
                                          out=acol(tj, T1), in0=acol(tj, MEAN), in1=acol(tj, RSTD),
                                          op=OP.mult), inc="dve")
                                      nmrtok = P.op("vector", lambda v, tj=tj: v.tensor_scalar(
                                          out=acol(tj, NMR), in0=acol(tj, T1), scalar1=-1.0,
                                          scalar2=None, op0=OP.mult), inc="dve")
                                      P.waitt("scalar", nmrtok)
                                      P.waitt("scalar", xn_free[tj % 2])
                                      xntok = P.op("scalar", lambda s, tj=tj: s.activation(
                                          out=xn[tj % 2][:], in_=xtile[tj][:], func=AF.Identity,
                                          bias=acol(tj, NMR), scale=acol(tj, RSTD)), inc="act")
                                      P.waitt("vector", lat_prev)
                                      for k in range(8):
                                          xn_free[tj % 2] = transpose_tile(
                                              xn[tj % 2][:, k * 128:(k + 1) * 128],
                                              xlnT[:, k * 512 + tj * 128: k * 512 + tj * 128 + 128],
                                              [6, 7], xntok)
                                  xlnT_tok = P.mark("dve")
                                  P.waitt("tensor", xlnT_tok)
                                  P.wait("tensor", "dm", wload_ld[1])
                                  P.waitt("tensor", wload_cast)
                                  for m in range(6):
                                      P.waitt("tensor", lat_free[m])
                                  lattok = None
                                  for k in range(8):
                                      for m in range(6):
                                          lhs = wlat_sb[:, k * 768 + m * 128: k * 768 + m * 128 + 128]
                                          fn = (lambda t, lhs=lhs, m=m, k=k: t.matmul(
                                              PS[m][:, 0:512], lhs, xlnT[:, k * 512:(k + 1) * 512],
                                              start=(k == 0), stop=(k == 7)))
                                          if k == 7 and m == 5:
                                              lattok = P.op("tensor", fn, inc="pe")
                                          else:
                                              P.op("tensor", fn)
                                  lat_prev = lattok
                                  P.waitt("scalar", lattok)
                                  for m in range(4):
                                      dstb = cqg if m < 2 else ckvg
                                      evt = P.op("scalar", lambda s, m=m, dstb=dstb, tg=tg: s.activation(
                                          out=dstb[:, (m % 2) * 2048 + tg * 512:(m % 2) * 2048 + tg * 512 + 512],
                                          in_=PS[m][:, 0:512], func=AF.Gelu,
                                          bias=blat[:, m:m + 1], scale=1.0), inc="act")
                                      lat_free[m] = evt
                                  P.waitt("scalar", raw_free[0])
                                  r4 = P.op("scalar", lambda s: s.activation(
                                      out=rawq[:], in_=PS[4][:, 0:512], func=AF.Identity,
                                      bias=blat[:, 4:5], scale=1.0), inc="act")
                                  lat_free[4] = r4
                                  P.waitt("scalar", raw_free[1])
                                  r5 = P.op("scalar", lambda s: s.activation(
                                      out=rawk[:], in_=PS[5][:, 0:512], func=AF.Identity,
                                      bias=blat[:, 5:6], scale=1.0), inc="act")
                                  lat_free[5] = r5
                                  P.waitt("sync", tab_free)
                                  P.op("sync", lambda s, tg=tg: s.dma_start(
                                      out=cosb[:], in_=d_in["cosT"][:, tg * 512:(tg + 1) * 512]), inc="dc", k=16)
                                  ctok = P.op("sync", lambda s, tg=tg: s.dma_start(
                                      out=sinb[:], in_=d_in["sinT"][:, tg * 512:(tg + 1) * 512]), inc="dc", k=16)
                                  for idx, (raw, rtok, dstT) in enumerate(((rawq, r4, QT), (rawk, r5, KT))):
                                      P.waitt("scalar", rtok)
                                      P.waitt("scalar", rot_free)
                                      P.op("scalar", lambda g, raw=raw: g.dma_start(
                                          out=rotb[0:64, :], in_=raw[64:128, :]), inc="dr", k=16)
                                      shtok = P.op("scalar", lambda g, raw=raw: g.dma_start(
                                          out=rotb[64:128, :], in_=raw[0:64, :]), inc="dr", k=16)
                                      P.waitt("vector", shtok)
                                      P.op("vector", lambda v: v.tensor_scalar(
                                          out=rotb[0:64, :], in0=rotb[0:64, :], scalar1=-1.0,
                                          scalar2=None, op0=OP.mult), inc="dve")
                                      P.waitt("vector", ctok)
                                      P.op("vector", lambda v: v.tensor_tensor(
                                          out=rotb[:], in0=rotb[:], in1=sinb[:], op=OP.mult), inc="dve")
                                      P.op("vector", lambda v, raw=raw: v.tensor_tensor(
                                          out=raw[:], in0=raw[:], in1=cosb[:], op=OP.mult), inc="dve")
                                      addtok = P.op("vector", lambda v, raw=raw: v.tensor_tensor(
                                          out=raw[:], in0=raw[:], in1=rotb[:], op=OP.add), inc="dve")
                                      raw_free[idx] = addtok
                                      rot_free = addtok
                                      P.waitt("scalar", addtok)
                                      gtok = P.op("scalar", lambda s, raw=raw, dstT=dstT, tg=tg: s.activation(
                                          out=dstT[:, 7 * 2048 + tg * 512: 7 * 2048 + tg * 512 + 512],
                                          in_=raw[:], func=AF.Gelu), inc="act")
                                      tab_free = gtok
                                  a_act_guard = P.mark("act")
                                  P.wait("sync", "act", a_act_guard[1])

                              if paranoid == 30 and b == 0:
                                  cg0 = P.mark("act")
                                  P.waitt("vector", cg0)
                                  dq2 = P.op("vector", lambda v: v.tensor_copy(dbg_vq[:, 0:128], cqg[:, 0:128]), inc="dve")
                                  P.waitt("sync", dq2)
                                  P.op("sync", lambda s_: s_.dma_start(out=y_out[1664:1792, 0:128], in_=dbg_vq[:, 0:128]), inc="dm", k=16)
                              qkv_guard = P.mark("act")
                              P.waitt("tensor", qkv_guard)
                              for (wsb, bpp, dstT, srcT) in ((wqc_sb, bqcp, QT, cqg),
                                                             (wkc_sb, bkcp, KT, ckvg)):
                                  for j in range(7):
                                      for tg in range(4):
                                          bk = tg % 2
                                          P.waitt("tensor", bankfree[bk])
                                          mm = None
                                          for k in range(2):
                                              lhs = wsb[:, k * 896 + j * 128: j * 128 + k * 896 + 128]
                                              fn = (lambda t, lhs=lhs, k=k, bk=bk, tg=tg, srcT=srcT: t.matmul(
                                                  PS[bk][:, 0:512], lhs,
                                                  srcT[:, k * 2048 + tg * 512: k * 2048 + tg * 512 + 512],
                                                  start=(k == 0), stop=(k == 1)))
                                              if k == 1:
                                                  mm = P.op("tensor", fn, inc="pe")
                                              else:
                                                  P.op("tensor", fn)
                                          P.waitt("scalar", mm)
                                          ev = P.op("scalar", lambda s, bk=bk, j=j, tg=tg, bpp=bpp, dstT=dstT: s.activation(
                                              out=dstT[:, j * 2048 + tg * 512: j * 2048 + tg * 512 + 512],
                                              in_=PS[bk][:, 0:512], func=AF.Gelu,
                                              bias=bpp[:, j:j + 1], scale=1.0), inc="act")
                                          bankfree[bk] = ev
                              for tj in range(16):
                                  b0, b1 = 2, 3
                                  P.waitt("tensor", bankfree[b0])
                                  P.waitt("tensor", bankfree[b1])
                                  mm = None
                                  for k in range(2):
                                      lhs = ckvg[:, k * 2048 + tj * 128: k * 2048 + tj * 128 + 128]
                                      P.op("tensor", lambda t, lhs=lhs, k=k, b0=b0: t.matmul(
                                          PS[b0][:, 0:512], lhs, wv_sb[:, k * 1024:k * 1024 + 512],
                                          start=(k == 0), stop=(k == 1)))
                                      fn = (lambda t, lhs=lhs, k=k, b1=b1: t.matmul(
                                          PS[b1][:, 0:512], lhs, wv_sb[:, k * 1024 + 512:(k + 1) * 1024],
                                          start=(k == 0), stop=(k == 1)))
                                      if k == 1:
                                          mm = P.op("tensor", fn, inc="pe")
                                      else:
                                          P.op("tensor", fn)
                                  P.waitt("vector", mm)
                                  e0 = P.op("vector", lambda v, tj=tj, b0=b0: v.tensor_tensor(
                                      out=VB2r[:, tj * 16: tj * 16 + 8, 0:64],
                                      in0=PS[b0][:, 0:512].rearrange("p (h c) -> p h c", c=64),
                                      in1=bvb[:, 0:512].rearrange("p (h c) -> p h c", c=64),
                                      op=OP.add), inc="dve")
                                  e1 = P.op("vector", lambda v, tj=tj, b1=b1: v.tensor_tensor(
                                      out=VB2r[:, tj * 16 + 8: tj * 16 + 16, 0:64],
                                      in0=PS[b1][:, 0:512].rearrange("p (h c) -> p h c", c=64),
                                      in1=bvb[:, 512:1024].rearrange("p (h c) -> p h c", c=64),
                                      op=OP.add), inc="dve")
                                  bankfree[b0] = e0
                                  bankfree[b1] = e1

                        do_scope_a()
                        P.barrier()
                        bankfree.update({i: None for i in range(8)})

                        def do_scope_b():
                          with ExitStack() as sb:
                              SB = sb.enter_context
                              wo_sb = SB(nc.sbuf_tensor(f"wo_sb{b}", [128, 8 * 1024], BF16))
                              OT = SB(nc.sbuf_tensor(f"OT{b}", [128, 8 * 2048], BF16))
                              xtile = [SB(nc.sbuf_tensor(f"b{b}xt{i}", [128, 1024], F32)) for i in range(2)]
                              scb = SB(nc.sbuf_tensor(f"b{b}sc", [128, 16], F32))

                              wo_tok = None
                              for k in range(8):
                                  src = rowtiles(d_in["wo"], k, k + 1, 1024)
                                  lt = P.op("sync", lambda s, src=src: s.dma_start(
                                      out=sb3(wstage[:, 0:1024], 1, 1024), in_=src), inc="dm", k=16)
                                  P.waitt("vector", lt)
                                  wo_tok = P.op("vector", lambda v, k=k: v.tensor_copy(
                                      wo_sb[:, k * 1024:(k + 1) * 1024], wstage[:, 0:1024]), inc="dve")
                                  P.waitt("sync", wo_tok)

                              ebf = [SB(nc.sbuf_tensor(f"eb{b}{i}", [128, 512], BF16)) for i in range(2)]
                              Oq = [SB(nc.sbuf_tensor(f"Oq{b}{i}", [128, 64], BF16)) for i in range(2)]
                              OT6 = PS[6][:].bitcast(BF16)
                              OT7 = PS[7][:].bitcast(BF16)
                              ebf_free = [None, None]
                              Oq_free = [None, None]
                              for h in range(16):
                                  hr0 = (h % 2) * 64
                                  hch = h // 2
                                  for g4 in range(4):
                                      q0 = g4 * 512
                                      nkc = 4 * g4 + 4
                                      rhsQ = QT[hr0:hr0 + 64, hch * 2048 + q0: hch * 2048 + q0 + 512]
                                      pvtok = [None] * 4
                                      extok = [None] * nkc

                                      def emit_st(kc):
                                          sb_ = kc % 2
                                          a = kc - 4 * g4
                                          P.waitt("tensor", bankfree[sb_])
                                          lhsK = KT[hr0:hr0 + 64, hch * 2048 + kc * 128: hch * 2048 + kc * 128 + 128]
                                          if a < 0:
                                              mm = P.op("tensor", lambda t, lhsK=lhsK, rhsQ=rhsQ, sb_=sb_: t.matmul(
                                                  PS[sb_][:, 0:512], lhsK, rhsQ, start=True, stop=True), inc="pe")
                                          else:
                                              P.op("tensor", lambda t, lhsK=lhsK, rhsQ=rhsQ, sb_=sb_: t.matmul(
                                                  PS[sb_][:, 0:512], lhsK, rhsQ, start=True, stop=False))
                                              mm = P.op("tensor", lambda t, sb_=sb_, a=a: t.matmul(
                                                  PS[sb_][:, 0:512], identb[:], maskTw[a][:, 0:512],
                                                  start=False, stop=True), inc="pe")
                                          P.waitt("scalar", mm)
                                          P.waitt("scalar", ebf_free[sb_])
                                          ex = P.op("scalar", lambda s, sb_=sb_: s.activation(
                                              out=ebf[sb_][:], in_=PS[sb_][:, 0:512], func=AF.Exp,
                                              scale=0.125), inc="act")
                                          bankfree[sb_] = ex
                                          extok[kc] = ex

                                      def emit_pv(kc):
                                          sb_ = kc % 2
                                          for j in range(4):
                                              if j == 0:
                                                  P.waitt("tensor", extok[kc])
                                              if kc == 0:
                                                  P.waitt("tensor", bankfree[2 + j])
                                              rhsV = VB2r[:, kc * 16 + h, :]
                                              pvtok[j] = P.op("tensor", lambda t, sb_=sb_, j=j, rhsV=rhsV, kc=kc, nkc=nkc: t.matmul(
                                                  PS[2 + j][:, 0:65], ebf[sb_][:, j * 128:(j + 1) * 128], rhsV,
                                                  start=(kc == 0), stop=(kc == nkc - 1)), inc="pe")
                                          ebf_free[sb_] = pvtok[3]

                                      emit_st(0)
                                      for kc in range(1, nkc):
                                          emit_st(kc)
                                          emit_pv(kc - 1)
                                      emit_pv(nkc - 1)
                                      for j in range(4):
                                          qt = 4 * g4 + j
                                          P.waitt("scalar", pvtok[j])
                                          P.op("scalar", lambda s, j=j: s.activation(
                                              out=scb[:, 8 + j:9 + j], in_=PS[2 + j][:, 64:65],
                                              func=AF.Ln), inc="act")
                                          li = P.op("scalar", lambda s, j=j: s.activation(
                                              out=scb[:, 12 + j:13 + j], in_=scb[:, 8 + j:9 + j],
                                              func=AF.Exp, scale=-1.0), inc="act")
                                          seps()
                                          P.waitt("vector", li)
                                          P.waitt("vector", Oq_free[j % 2])
                                          nr = P.op("vector", lambda v, j=j: v.tensor_scalar(
                                              out=Oq[j % 2][:], in0=PS[2 + j][:, 0:64],
                                              scalar1=scb[:, 12 + j:13 + j], scalar2=None,
                                              op0=OP.mult), inc="dve")
                                          bankfree[2 + j] = nr
                                          ob = 6 + (j % 2)
                                          obv = OT6 if ob == 6 else OT7
                                          P.waitt("tensor", bankfree[ob])
                                          P.waitt("tensor", nr)
                                          tt = P.op("tensor", lambda t, obv=obv, j=j, hr0=hr0: t.transpose(
                                              obv[hr0:hr0 + 64, 0:128], Oq[j % 2][:], identb[:]), inc="pe")
                                          Oq_free[j % 2] = tt
                                          if j % 2 == 0:
                                              P.waitt("scalar", tt)
                                              oev = P.op("scalar", lambda s, obv=obv, hr0=hr0, hch=hch, qt=qt: s.activation(
                                                  out=OT[hr0:hr0 + 64, hch * 2048 + qt * 128: hch * 2048 + qt * 128 + 128],
                                                  in_=obv[hr0:hr0 + 64, 0:128], func=AF.Copy), inc="act")
                                          else:
                                              P.waitt("vector", tt)
                                              oev = P.op("vector", lambda v, obv=obv, hr0=hr0, hch=hch, qt=qt: v.tensor_copy(
                                                  OT[hr0:hr0 + 64, hch * 2048 + qt * 128: hch * 2048 + qt * 128 + 128],
                                                  obv[hr0:hr0 + 64, 0:128]), inc="dve")
                                          bankfree[ob] = oev
                              o_done = P.mark("act")
                              if paranoid == 30 and b == 0:
                                  P.waitt("vector", o_done)
                                  P.waitt("vector", P.mark("dm"))
                                  do_ = P.op("vector", lambda v: v.tensor_copy(dbg_xq[:, 0:256], OT[:, 0:256]), inc="dve")
                                  P.waitt("sync", do_)
                                  P.op("sync", lambda s_: s_.dma_start(out=y_out[1792:1920, 0:256], in_=dbg_xq[:, 0:256]), inc="dm", k=16)
                                  do2 = P.op("vector", lambda v: v.tensor_copy(dbg_vq[:, 0:128], OT[:, 7 * 2048:7 * 2048 + 128]), inc="dve")
                                  P.waitt("sync", do2)
                                  P.op("sync", lambda s_: s_.dma_start(out=y_out[1920:2048, 0:128], in_=dbg_vq[:, 0:128]), inc="dm", k=16)
                              xf = [None, None]
                              for tj in range(16):
                                  r = brow + tj * 128
                                  sl = tj % 2
                                  P.waitt("sync", xf[sl])
                                  lt = P.op("sync", lambda s, sl=sl, r=r: s.dma_start(
                                      out=xtile[sl][:], in_=x1[r:r + 128, :]), inc="dm", k=16)
                                  P.waitt("vector", lt)
                                  botok = P.op("vector", lambda v, sl=sl: v.tensor_tensor(
                                      out=xtile[sl][:], in0=xtile[sl][:], in1=bob[:], op=OP.add), inc="dve")
                                  bk0, bk1 = 2 * (tj % 2), 2 * (tj % 2) + 1
                                  P.waitt("tensor", bankfree[bk0])
                                  P.waitt("tensor", bankfree[bk1])
                                  P.waitt("tensor", o_done)
                                  P.waitt("tensor", wo_tok)
                                  mm = None
                                  for k in range(8):
                                      lhs = OT[:, k * 2048 + tj * 128: k * 2048 + tj * 128 + 128]
                                      P.op("tensor", lambda t, lhs=lhs, k=k, bk0=bk0: t.matmul(
                                          PS[bk0][:, 0:512], lhs, wo_sb[:, k * 1024:k * 1024 + 512],
                                          start=(k == 0), stop=(k == 7)))
                                      fn = (lambda t, lhs=lhs, k=k, bk1=bk1: t.matmul(
                                          PS[bk1][:, 0:512], lhs, wo_sb[:, k * 1024 + 512:(k + 1) * 1024],
                                          start=(k == 0), stop=(k == 7)))
                                      if k == 7:
                                          mm = P.op("tensor", fn, inc="pe")
                                      else:
                                          P.op("tensor", fn)
                                  P.waitt("vector", mm)
                                  P.waitt("vector", botok)
                                  P.op("vector", lambda v, sl=sl, bk0=bk0: v.tensor_tensor(
                                      out=xtile[sl][:, 0:512], in0=PS[bk0][:, 0:512],
                                      in1=xtile[sl][:, 0:512], op=OP.add), inc="dve")
                                  ev = P.op("vector", lambda v, sl=sl, bk1=bk1: v.tensor_tensor(
                                      out=xtile[sl][:, 512:1024], in0=PS[bk1][:, 0:512],
                                      in1=xtile[sl][:, 512:1024], op=OP.add), inc="dve")
                                  bankfree[bk0] = ev
                                  bankfree[bk1] = ev
                                  if paranoid == 30 and b == 0 and tj == 0:
                                      P.waitt("vector", ev)
                                      P.waitt("vector", P.mark("dm"))
                                      dw_ = P.op("vector", lambda v, sl=sl: v.tensor_copy(dbg_xq[:, 0:256], xtile[sl][:, 0:256]), inc="dve")
                                      P.waitt("sync", dw_)
                                      P.op("sync", lambda s_: s_.dma_start(out=y_out[2048:2176, 0:256], in_=dbg_xq[:, 0:256]), inc="dm", k=16)
                                  P.waitt("sync", ev)
                                  stok = P.op("sync", lambda g, sl=sl, r=r: g.dma_start(
                                      out=x1[r:r + 128, :], in_=xtile[sl][:]), inc=f"dx{sl}", k=16)
                                  xf[sl] = stok

                        do_scope_b()
                        P.barrier()
                        bankfree.update({i: None for i in range(8)})


                for b in range(2):
                    do_batch(b)

        def emit_dump_prep():
            with ExitStack() as cs:
                ctb = cs.enter_context(nc.sbuf_tensor("dmpb", [128, 2048], BF16))
                ctf = cs.enter_context(nc.sbuf_tensor("dmpf", [128, 2048], F32))
                free = None
                def dump(srcap, row):
                    nonlocal free
                    lt = P.op("sync", lambda s, srcap=srcap: s.dma_start(
                        out=ctb[:], in_=srcap), inc="dm", k=16)
                    P.waitt("vector", lt)
                    P.waitt("vector", free)
                    ct2 = P.op("vector", lambda v: v.tensor_copy(ctf[:], ctb[:]), inc="dve")
                    P.waitt("gpsimd", ct2)
                    f1 = P.op("gpsimd", lambda g, row=row: g.dma_start(
                        out=y_out[row:row + 128, :], in_=ctf[:, 0:1024]), inc="dm", k=16)
                    f2 = P.op("gpsimd", lambda g, row=row: g.dma_start(
                        out=y_out[row + 128:row + 256, :], in_=ctf[:, 1024:2048]), inc="dm", k=16)
                    free = f2
                    P.wait("sync", "dm", f2[1])
                dump(sg[0], 0)
                dump(sg[15], 256)
                dump(su[0], 512)
                dump(sd[0][:, 0:2048], 768)
                dump(sd[7][:, 2048:4096], 1024)
                wv_ = P.op("gpsimd", lambda g: g.dma_start(
                    out=y_out[1280:1408, 0:8], in_=wmean[:]), inc="dm", k=16)
                wv2 = P.op("gpsimd", lambda g: g.dma_start(
                    out=y_out[1408:1536, 0:16], in_=dbg[:]), inc="dm", k=16)
            P.barrier()

        def emit_copy_out():
            with ExitStack() as cs:
                ct = cs.enter_context(nc.sbuf_tensor("cpyt", [128, 1024], F32))
                free = None
                for tj in range(32):
                    P.waitt("sync", free)
                    lt = P.op("sync", lambda s, tj=tj: s.dma_start(
                        out=ct[:], in_=x1[tj * 128:(tj + 1) * 128, :]), inc="dm", k=16)
                    P.waitt("sync", lt)
                    free = P.op("sync", lambda g, tj=tj: g.dma_start(
                        out=y_out[tj * 128:(tj + 1) * 128, :], in_=ct[:]), inc="dm", k=16)
            P.barrier()

        if stage == 9:
            with ExitStack() as ms:
                mt = ms.enter_context(nc.sbuf_tensor("rb9", [128, 2048], F32))
                mb = ms.enter_context(nc.sbuf_tensor("rb9b", [128, 2048], BF16))
                for rep in range(8):
                    lt = P.op("sync", lambda s: s.dma_start(out=mb[:], in_=sd[7][:, 0:2048]), inc="dm", k=16)
                    P.waitt("vector", lt)
                    c1 = P.op("vector", lambda v: v.tensor_copy(mt[:], mb[:]), inc="dve")
                    P.waitt("scalar", c1)
                    P.op("scalar", lambda g, rep=rep: g.dma_start(
                        out=y_out[rep * 256:rep * 256 + 128, :], in_=mt[:, 0:1024]), inc="dm", k=16)
                    st9t = P.op("scalar", lambda g, rep=rep: g.dma_start(
                        out=y_out[rep * 256 + 128:rep * 256 + 256, :], in_=mt[:, 1024:2048]), inc="dm", k=16)
                    P.wait("sync", "dm", st9t[1])
                    P.op("sync", lambda s: s.nop(cycle_cnt=50000))
            P.barrier()
            with nc.Block() as block:
                block.tensor(lambda e: [f(e) for f in P.stream["tensor"]])
                block.vector(lambda e: [f(e) for f in P.stream["vector"]])
                block.scalar(lambda e: [f(e) for f in P.stream["scalar"]])
                block.gpsimd(lambda e: [f(e) for f in P.stream["gpsimd"]])
                block.sync(lambda e: [f(e) for f in P.stream["sync"]])
            return nc
        if stage == 7:
            lvl = paranoid
            with ExitStack() as ms:
                xt4 = [ms.enter_context(nc.sbuf_tensor(f"m7x{i}", [128, 1024], F32)) for i in range(4)]
                sq7 = ms.enter_context(nc.sbuf_tensor("m7sq", [128, 1024], BF16))
                sc7 = ms.enter_context(nc.sbuf_tensor("m7sc", [128, 16], F32))
                xc7 = [ms.enter_context(nc.sbuf_tensor(f"m7c{i}", [128, 1024], F32)) for i in range(2)]
                dmp = ms.enter_context(nc.sbuf_tensor("m7d", [128, 1024], F32))
                dmp2 = ms.enter_context(nc.sbuf_tensor("m7d2", [128, 16], F32))
                wdum = ms.enter_context(nc.sbuf_tensor("m7w", [128, 4096], BF16))
                snap = None
                for tj in range(4):
                    lt = P.op("sync", lambda s, tj=tj: s.dma_start(
                        out=xt4[tj][:], in_=d_in["x"][tj * 128:(tj + 1) * 128, :]), inc="dm", k=16)
                    if lvl >= 3:
                        for w in range(4):
                            P.op("sync", lambda s, tj=tj, w=w: s.dma_start(
                                out=wdum[:, 0:2048], in_=sg[(4 * tj + w) % 16]), inc="dm", k=16)
                    P.waitt("scalar", lt)
                    stok = P.op("scalar", lambda s, tj=tj: s.activation(
                        out=sq7[:], in_=xt4[tj][:], func=AF.Square,
                        accum_out=sc7[:, tj * 4:tj * 4 + 1]), inc="act")
                    P.waitt("vector", lt)
                    P.op("vector", lambda v, tj=tj: v.tensor_reduce(
                        sc7[:, tj * 4 + 1:tj * 4 + 2], xt4[tj][:], axis=AX, op=OP.add), inc="dve")
                    if lvl >= 2:
                        P.waitt("vector", stok)
                        sepv()
                        P.op("vector", lambda v, tj=tj: v.tensor_scalar(
                            out=sc7[:, tj * 4 + 2:tj * 4 + 3], in0=sc7[:, tj * 4 + 1:tj * 4 + 2],
                            scalar1=-1.0 / 1024, scalar2=None, op0=OP.mult), inc="dve")
                        P.waitt("scalar", P.mark("dve"))
                        P.op("scalar", lambda s, tj=tj: s.activation(
                            out=xc7[tj % 2][:], in_=xt4[tj][:], func=AF.Identity,
                            bias=sc7[:, tj * 4 + 2:tj * 4 + 3], scale=1.0), inc="act")
                    if tj == 0:
                        P.op("vector", lambda v: v.tensor_copy(dmp[:], xt4[0][:]), inc="dve")
                        snap = P.op("vector", lambda v: v.tensor_copy(dmp2[:], sc7[:]), inc="dve")
                P.waitt("gpsimd", snap)
                P.wait("gpsimd", "dve")
                P.op("gpsimd", lambda g: g.dma_start(out=y_out[0:128, :], in_=dmp[:]), inc="dm", k=16)
                P.op("gpsimd", lambda g: g.dma_start(out=y_out[128:256, 0:16], in_=dmp2[:]), inc="dm", k=16)
                P.op("gpsimd", lambda g: g.dma_start(out=y_out[256:384, :], in_=xt4[0][:]), inc="dm", k=16)
                P.wait("gpsimd", "dm")
            P.barrier()
            with nc.Block() as block:
                block.tensor(lambda e: [f(e) for f in P.stream["tensor"]])
                block.vector(lambda e: [f(e) for f in P.stream["vector"]])
                block.scalar(lambda e: [f(e) for f in P.stream["scalar"]])
                block.gpsimd(lambda e: [f(e) for f in P.stream["gpsimd"]])
                block.sync(lambda e: [f(e) for f in P.stream["sync"]])
            return nc
        if stage == 6:
            with ExitStack() as ms:
                mt = ms.enter_context(nc.sbuf_tensor("minitile", [128, 1024], F32))
                mdbg = ms.enter_context(nc.sbuf_tensor("minidbg", [128, 512], F32))
                lt = P.op("sync", lambda s: s.dma_start(out=mt[:], in_=d_in["x"][0:128, :]), inc="dm", k=16)
                P.waitt("vector", lt)
                ct = P.op("vector", lambda v: v.tensor_copy(mdbg[:], mt[:, 0:512]), inc="dve")
                P.waitt("gpsimd", ct)
                st1 = P.op("gpsimd", lambda g: g.dma_start(out=y_out[0:128, 0:512], in_=mdbg[:]), inc="dm", k=16)
                st2 = P.op("gpsimd", lambda g: g.dma_start(out=y_out[128:256, :], in_=mt[:]), inc="dm", k=16)
                P.wait("gpsimd", "dm")
            P.barrier()
            with nc.Block() as block:
                block.tensor(lambda e: [f(e) for f in P.stream["tensor"]])
                block.vector(lambda e: [f(e) for f in P.stream["vector"]])
                block.scalar(lambda e: [f(e) for f in P.stream["scalar"]])
                block.gpsimd(lambda e: [f(e) for f in P.stream["gpsimd"]])
                block.sync(lambda e: [f(e) for f in P.stream["sync"]])
            return nc
        if stage == 5:
            emit_ffn(5, dbg_dump=True)
            with ExitStack() as cs:
                relx = cs.enter_context(nc.sbuf_tensor("relx", [128, 1024], F32))
                rl = P.op("sync", lambda s: s.dma_start(out=relx[:], in_=d_in["x"][0:128, :]), inc="dm", k=16)
                P.waitt("gpsimd", rl)
                P.op("gpsimd", lambda g: g.dma_start(out=y_out[640:768, :], in_=relx[:]), inc="dm", k=16)
                free = None
                for (buf, row) in [(dbg_xq, 0), (dbg_vq, 128), (dbg_prod, 256), (dbg_wg, 384), (dbg_sc, 512)]:
                    w = buf.shape[1] if hasattr(buf, 'shape') else 1024
                    f1 = P.op("gpsimd", lambda g, buf=buf, row=row: g.dma_start(
                        out=y_out[row:row + 128, 0:buf.shape[1]], in_=buf[:]), inc="dm", k=16)
                P.wait("gpsimd", "dm")
            P.barrier()
            with nc.Block() as block:
                block.tensor(lambda e: [f(e) for f in P.stream["tensor"]])
                block.vector(lambda e: [f(e) for f in P.stream["vector"]])
                block.scalar(lambda e: [f(e) for f in P.stream["scalar"]])
                block.gpsimd(lambda e: [f(e) for f in P.stream["gpsimd"]])
                block.sync(lambda e: [f(e) for f in P.stream["sync"]])
            return nc
        if stage == 0:
            emit_dump_prep()
            with nc.Block() as block:
                block.tensor(lambda e: [f(e) for f in P.stream["tensor"]])
                block.vector(lambda e: [f(e) for f in P.stream["vector"]])
                block.scalar(lambda e: [f(e) for f in P.stream["scalar"]])
                block.gpsimd(lambda e: [f(e) for f in P.stream["gpsimd"]])
                block.sync(lambda e: [f(e) for f in P.stream["sync"]])
            return nc
        emit_ffn(0 if stage > 1 else 2)
        if stage >= 2:
            emit_attn()
            if stage == 2:
                if paranoid != 30:
                    emit_copy_out()
                else:
                    P.barrier()
            else:
                emit_ffn(1)
        P.barrier()

        with nc.Block() as block:
            block.tensor(lambda e: [f(e) for f in P.stream["tensor"]])
            block.vector(lambda e: [f(e) for f in P.stream["vector"]])
            block.scalar(lambda e: [f(e) for f in P.stream["scalar"]])
            block.gpsimd(lambda e: [f(e) for f in P.stream["gpsimd"]])
            block.sync(lambda e: [f(e) for f in P.stream["sync"]])
    return nc


def kernel(L0_attn_x, L0_attn_wq, L0_attn_bq, L0_attn_wqc, L0_attn_bqc,
           L0_attn_wqr, L0_attn_bqr, L0_attn_wkv, L0_attn_bkv, L0_attn_wkr,
           L0_attn_bkr, L0_attn_wkc, L0_attn_bkc, L0_attn_wv, L0_attn_bv,
           L0_attn_wo, L0_attn_bo, L0_ffn_w_gate, L0_ffn_w_up, L0_ffn_w_down):
    f32 = lambda a: np.ascontiguousarray(np.asarray(a, dtype=np.float32))
    x = f32(L0_attn_x).reshape(16, 2048, 1024)
    wq, bq = f32(L0_attn_wq), f32(L0_attn_bq)
    wqc, bqc = f32(L0_attn_wqc), f32(L0_attn_bqc)
    wqr, bqr = f32(L0_attn_wqr), f32(L0_attn_bqr)
    wkv, bkv = f32(L0_attn_wkv), f32(L0_attn_bkv)
    wkr, bkr = f32(L0_attn_wkr), f32(L0_attn_bkr)
    wkc, bkc = f32(L0_attn_wkc), f32(L0_attn_bkc)
    wv, bv = f32(L0_attn_wv), f32(L0_attn_bv)
    wo, bo = f32(L0_attn_wo), f32(L0_attn_bo)
    wg, wu, wd = f32(L0_ffn_w_gate), f32(L0_ffn_w_up), f32(L0_ffn_w_down)

    wlat = np.ascontiguousarray(np.concatenate([wq, wkv, wqr, wkr], axis=1))
    blat_vec = np.concatenate([bq, bkv, bqr, bkr])
    blat_p = np.ascontiguousarray(blat_vec.reshape(6, 128).T)
    bqc_pp = np.ascontiguousarray(bqc.reshape(7, 128).T)
    bkc_pp = np.ascontiguousarray(bkc.reshape(7, 128).T)
    bv_b = np.ascontiguousarray(np.broadcast_to(bv, (128, 1024)))
    bo_b = np.ascontiguousarray(np.broadcast_to(bo, (128, 1024)))

    inv = 1.0 / (10000.0 ** (np.arange(0, DR, 2, dtype=np.float64) / DR))
    t = np.arange(S, dtype=np.float64)
    freqs = t[:, None] * inv[None, :]
    emb = np.concatenate([freqs, freqs], axis=1)
    cosT = np.ascontiguousarray(np.cos(emb).T.astype(np.float32))
    sinT = np.ascontiguousarray(np.sin(emb).T.astype(np.float32))

    nc = build_program()
    common = dict(wg=wg, wu=wu, wd=wd, wlat=wlat, wqc=wqc, wkc=wkc, wv=wv, wo=wo,
                  blat_p=blat_p, bqc_pp=bqc_pp, bkc_pp=bkc_pp, bv_b=bv_b, bo_b=bo_b,
                  cosT=cosT, sinT=sinT)
    in_maps = []
    for c in range(N_CORES):
        m = dict(common)
        m["x"] = np.ascontiguousarray(x[2 * c:2 * c + 2].reshape(T, E))
        in_maps.append(m)
    res = run_bass_kernel_spmd(nc, in_maps, core_ids=list(range(N_CORES)))
    out = np.empty((16, 2048, 1024), dtype=np.float32)
    for c in range(N_CORES):
        out[2 * c:2 * c + 2] = np.asarray(res.results[c]["y"]).reshape(2, 2048, 1024)
    return out


# version 15
# speedup vs baseline: 2.1013x; 1.0476x over previous
import numpy as np
from contextlib import ExitStack

import concourse.bass as bass
import concourse.mybir as mybir
from concourse.bass_utils import run_bass_kernel_spmd
from concourse.alu_op_type import AluOpType as OP

F32 = mybir.dt.float32
BF16 = mybir.dt.bfloat16
AF = mybir.ActivationFunctionType
AX = mybir.AxisListType.X
MAGIC = 12582912.0
NEG_INF = -1.0e30

N_CORES = 8
T = 4096
E = 1024
S = 2048
DR = 128
INNER = 4096
TG = 512
NTJ = TG // 128

ENGS = ("tensor", "vector", "scalar", "gpsimd", "sync")


class Prog:
    def __init__(self, nc, stack):
        self.nc = nc
        self.auto_sep = True
        self.junk = {}
        self.stream = {e: [] for e in ENGS}
        self.sems = {}
        self.cnt = {}
        self.memo = {e: {} for e in ENGS}
        for name in ("pe", "dve", "act", "gp", "bar") + ('dx0', 'dx1', 'dx2', 'dx3', 'dw0', 'dw1', 'du0', 'du1', 'dd0', 'dd1', 'dm', 'dr', 'dc'):
            self.sems[name] = stack.enter_context(nc.semaphore(name))
            self.cnt[name] = 0

    def wait(self, eng, sem, val=None):
        v = self.cnt[sem] if val is None else val
        if v <= 0:
            return
        if self.memo[eng].get(sem, -1) >= v:
            return
        self.memo[eng][sem] = v
        h = self.sems[sem]
        self.stream[eng].append(lambda e, h=h, v=v: e.wait_ge(h, v))

    def waitt(self, eng, tok):
        if tok is not None:
            self.wait(eng, tok[0], tok[1])

    def op(self, eng, fn, inc=None, k=1):
        if inc is None:
            self.stream[eng].append(fn)
            self._sep(eng)
            return None
        self.cnt[inc] += k
        tok = (inc, self.cnt[inc])
        h = self.sems[inc]

        def wrapped(e, fn=fn, h=h, k=k):
            fn(e).then_inc(h, k)
        self.stream[eng].append(wrapped)
        self._sep(eng)
        return tok

    def _sep(self, eng):
        if not self.auto_sep or eng not in ("vector",):
            return
        j = self.junk.get(eng)
        if j is None:
            return
        if eng == "vector":
            self.stream[eng].append(lambda v, j=j: v.tensor_scalar(
                out=j[:], in0=j[:], scalar1=0.0, scalar2=None, op0=OP.mult))
        elif eng == "scalar":
            self.stream[eng].append(lambda s_, j=j: s_.activation(
                out=j[:], in_=j[:], func=AF.Identity, scale=0.0))
        else:
            self.stream[eng].append(lambda g, j=j: g.memset(j[:], 0.0))

    def mark(self, sem):
        return (sem, self.cnt[sem])

    DMA_SEMS = ['dx0', 'dx1', 'dx2', 'dx3', 'dw0', 'dw1', 'du0', 'du1', 'dd0', 'dd1', 'dm', 'dr', 'dc']

    def barrier(self):
        for ds in self.DMA_SEMS:
            if self.cnt[ds] > 0:
                self.wait("sync", ds)
                self.wait("gpsimd", ds)
        for e in ENGS:
            self.op(e, lambda en: en.drain(), inc="bar")
        for e in ENGS:
            self.wait(e, "bar")


def ceil_div(a, b):
    return (a + b - 1) // b


def rowtiles(hbm_ap, t0, t1, width):
    return hbm_ap.rearrange("(t p) d -> p t d", p=128)[:, t0:t1, :]


def sb3(sb_ap, nt, width):
    return sb_ap.rearrange("p (t d) -> p t d", t=nt)


def build_program(stage=3, paranoid=0):
    nc = bass.Bass("TRN2", target_bir_lowering=False)

    d_in = {}
    d_in["x"] = nc.dram_tensor("x", [T, E], F32, kind="ExternalInput").ap()
    for nm, shp in [("wg", [INNER, E]), ("wu", [INNER, E]), ("wd", [E, INNER]),
                    ("wlat", [E, 768]), ("wqc", [256, 896]), ("wkc", [256, 896]),
                    ("wv", [256, E]), ("wo", [E, E]),
                    ("blat_p", [128, 6]), ("bqc_pp", [128, 7]), ("bkc_pp", [128, 7]),
                    ("bv_b", [128, E]), ("bo_b", [128, E]),
                    ("cosT", [DR, S]), ("sinT", [DR, S])]:
        d_in[nm] = nc.dram_tensor(nm, shp, F32, kind="ExternalInput").ap()
    y_out = nc.dram_tensor("y", [T, E], F32, kind="ExternalOutput").ap()
    x1 = nc.dram_tensor("x1", [T, E], F32, kind="Internal").ap()
    sg = nc.dram_tensor("sg", [16, 128, 2048], BF16, kind="Internal").ap()
    su = nc.dram_tensor("su", [16, 128, 2048], BF16, kind="Internal").ap()
    sd = nc.dram_tensor("sd", [8, 128, 4096], BF16, kind="Internal").ap()

    with ExitStack() as stack:
        EC = stack.enter_context
        P = Prog(nc, stack)

        magic = EC(nc.sbuf_tensor("magic", [128, 1], F32))
        ident = EC(nc.sbuf_tensor("ident", [128, 128], F32))
        identb = EC(nc.sbuf_tensor("identb", [128, 128], BF16))
        maskbigT = EC(nc.sbuf_tensor("maskbigT", [128, 896], BF16))
        maskTw = [maskbigT[:, 384 - a * 128: 896 - a * 128] for a in range(4)]
        ones = EC(nc.sbuf_tensor("ones", [128, 128], F32))
        wmean = EC(nc.sbuf_tensor("wmean", [128, 8], F32))
        ln127 = EC(nc.sbuf_tensor("ln127", [128, 1], F32))
        dbg = EC(nc.sbuf_tensor("dbg", [128, 16], F32))
        junkv = EC(nc.sbuf_tensor("junkv", [128, 1], F32))
        junks = EC(nc.sbuf_tensor("junks", [128, 1], F32))
        if stage == 5:
            dbg_xq = EC(nc.sbuf_tensor("dbg_xq", [128, 512], F32))
            dbg_vq = EC(nc.sbuf_tensor("dbg_vq", [128, 512], F32))
            dbg_prod = EC(nc.sbuf_tensor("dbg_prod", [128, 256], F32))
            dbg_wg = EC(nc.sbuf_tensor("dbg_wg", [128, 512], F32))
            dbg_sc = EC(nc.sbuf_tensor("dbg_sc", [128, 40], F32))

        junkg = EC(nc.sbuf_tensor("junkg", [128, 1], F32))
        P.junk = {"vector": junkv, "scalar": junks, "gpsimd": junkg}

        if paranoid == 30:
            dbg_xq = EC(nc.sbuf_tensor("dbg_xq", [128, 256], F32))
            dbg_vq = EC(nc.sbuf_tensor("dbg_vq", [128, 128], F32))
            dbg_sc = EC(nc.sbuf_tensor("dbg_sc", [128, 8], F32))

        def sepv():
            P.op("vector", lambda v: v.tensor_scalar(
                out=junkv[:], in0=junkv[:], scalar1=0.0, scalar2=None, op0=OP.mult))

        def seps():
            P.op("scalar", lambda s: s.activation(
                out=junks[:], in_=junks[:], func=AF.Identity, scale=0.0))
        PS = [EC(nc.psum_tensor(f"ps{i}", [128, 512], F32)) for i in range(8)]
        bankfree = {i: None for i in range(8)}

        P.op("gpsimd", lambda g: g.memset(magic[:], MAGIC))
        P.op("gpsimd", lambda g: g.memset(ln127[:], float(np.log(127.0))))
        P.op("gpsimd", lambda g: g.memset(junkv[:], 1.0))
        P.op("gpsimd", lambda g: g.memset(junks[:], 1.0))
        P.op("gpsimd", lambda g: g.memset(ident[:], 0.0))
        P.op("gpsimd", lambda g: g.affine_select(
            out=ident[:], in_=ident[:], compare_op=OP.not_equal, fill=1.0,
            base=0, pattern=[[-1, 128]], channel_multiplier=1))
        P.op("gpsimd", lambda g: g.memset(identb[:], 0.0))
        P.op("gpsimd", lambda g: g.affine_select(
            out=identb[:], in_=identb[:], compare_op=OP.not_equal, fill=1.0,
            base=0, pattern=[[-1, 128]], channel_multiplier=1))
        P.op("gpsimd", lambda g: g.memset(maskbigT[:], 0.0))
        P.op("gpsimd", lambda g: g.memset(maskbigT[:, 0:384], NEG_INF))
        P.op("gpsimd", lambda g: g.affine_select(
            out=maskbigT[:, 384:512], in_=maskbigT[:, 384:512],
            compare_op=OP.is_ge, fill=NEG_INF,
            base=0, pattern=[[1, 128]], channel_multiplier=-1))
        tok_consts = P.op("gpsimd", lambda g: g.memset(ones[:], 1.0), inc="gp")
        for e in ("tensor", "scalar", "vector"):
            P.waitt(e, tok_consts)

        def transpose_tile(src_ap, dst_ap, banks, src_tok, pre_evac_tok=None):
            b = banks[transpose_tile.rr % len(banks)]
            transpose_tile.rr += 1
            P.waitt("tensor", bankfree[b])
            P.waitt("tensor", src_tok)
            if src_ap.dtype == BF16:
                pview = PS[b][:].bitcast(BF16)[:, 0:128]
                idt = identb
            else:
                pview = PS[b][:, 0:128]
                idt = ident
            ptok = P.op("tensor", lambda t, pview=pview, s=src_ap, idt=idt: t.transpose(
                pview, s, idt[:]), inc="pe")
            P.waitt("vector", ptok)
            if pre_evac_tok is not None:
                P.waitt("vector", pre_evac_tok)
            etok = P.op("vector", lambda v, pview=pview, d=dst_ap: v.tensor_copy(
                d, pview), inc="dve")
            bankfree[b] = etok
            return etok
        transpose_tile.rr = 0

        with ExitStack() as prep:
            PE_ = prep.enter_context
            wtile = [PE_(nc.sbuf_tensor(f"wt{i}", [128, 4096], F32)) for i in range(2)]
            qtmp = PE_(nc.sbuf_tensor("qtmp", [128, 4096], F32))
            tern = PE_(nc.sbuf_tensor("tern", [128, 4096], BF16))
            stg_g = [PE_(nc.sbuf_tensor(f"stg{i}", [128, 2048], BF16)) for i in range(2)]
            stg_d = [PE_(nc.sbuf_tensor(f"std{i}", [128, 4096], BF16)) for i in range(2)]
            acc = PE_(nc.sbuf_tensor("acc", [128, 4], F32))
            rtmp = PE_(nc.sbuf_tensor("rtmp", [128, 8], F32))

            wt_free = [None, None]
            mats = [("g", d_in["wg"]), ("u", d_in["wu"]), ("d", d_in["wd"])]
            for mi, (nm, wap) in enumerate(mats):
                last = None
                for tix in range(8):
                    sl = tix % 2
                    P.waitt("sync", wt_free[sl])
                    if nm == "d":
                        src = wap[tix * 128:(tix + 1) * 128, :]
                        dstv = wtile[sl][:]
                    else:
                        src = rowtiles(wap, tix * 4, tix * 4 + 4, 1024)
                        dstv = sb3(wtile[sl][:], 4, 1024)
                    ltok = P.op("sync", lambda s, src=src, dstv=dstv: s.dma_start(
                        out=dstv, in_=src), inc=f"dx{sl}", k=16)
                    P.waitt("vector", ltok)
                    rt = P.op("vector", lambda v, sl=sl, tix=tix: v.tensor_reduce(
                        rtmp[:, tix:tix + 1], wtile[sl][:], axis=AX, op=OP.add,
                        apply_absolute_value=True), inc="dve")
                    wt_free[sl] = rt
                P.op("vector", lambda v, mi=mi: v.tensor_copy(
                    dbg[:, 8 * (mi % 2):8 * (mi % 2) + 8], rtmp[:, 0:8]), inc="dve")
                last = P.op("vector", lambda v, mi=mi: v.tensor_reduce(
                    acc[:, mi:mi + 1], rtmp[:, 0:8], axis=AX, op=OP.add), inc="dve")
                P.waitt("tensor", bankfree[mi])
                P.waitt("tensor", last)
                mtok = P.op("tensor", lambda t, mi=mi: t.matmul(
                    PS[mi][:, 0:1], ones[:], acc[:, mi:mi + 1], start=True, stop=True), inc="pe")
                P.waitt("vector", mtok)
                P.op("vector", lambda v, mi=mi: v.tensor_scalar(
                    out=wmean[:, mi:mi + 1], in0=PS[mi][:, 0:1], scalar1=1.0 / (INNER * E),
                    scalar2=1e-5, op0=OP.mult, op1=OP.max), inc="dve")
                mtok2 = P.mark("dve")
                P.waitt("scalar", mtok2)
                P.op("scalar", lambda s, mi=mi: s.activation(
                    out=acc[:, 3:4], in_=wmean[:, mi:mi + 1], func=AF.Ln), inc="act")
                rtok = P.op("scalar", lambda s, mi=mi: s.activation(
                    out=wmean[:, 3 + mi:4 + mi], in_=acc[:, 3:4], func=AF.Exp,
                    scale=-1.0), inc="act")
                P.waitt("vector", rtok)
                P.op("vector", lambda v, mi=mi: v.tensor_tensor(
                    out=acc[:, 3:4], in0=wmean[:, mi:mi + 1], in1=wmean[:, 3 + mi:4 + mi],
                    op=OP.mult), inc="dve")
                P.op("vector", lambda v: v.tensor_scalar(
                    out=acc[:, 3:4], in0=acc[:, 3:4], scalar1=-1.0, scalar2=2.0,
                    op0=OP.mult, op1=OP.add), inc="dve")
                etok = P.op("vector", lambda v, mi=mi: v.tensor_tensor(
                    out=wmean[:, 3 + mi:4 + mi], in0=wmean[:, 3 + mi:4 + mi], in1=acc[:, 3:4],
                    op=OP.mult), inc="dve")
                bankfree[mi] = etok
            P.op("vector", lambda v: v.tensor_copy(wmean[:, 6:7], acc[:, 0:1]), inc="dve")
            P.op("vector", lambda v: v.tensor_copy(wmean[:, 7:8], acc[:, 2:3]), inc="dve")
            means_tok = P.mark("dve")

            def quant_tile(sl, mi, width, ld_tok):
                P.waitt("scalar", ld_tok)
                P.waitt("scalar", means_tok)
                P.waitt("scalar", quant_tile.tern_free)
                atok = P.op("scalar", lambda s, sl=sl, mi=mi, w=width: s.activation(
                    out=qtmp[:, 0:w], in_=wtile[sl][:, 0:w], func=AF.Identity,
                    bias=magic[:, 0:1], scale=wmean[:, 3 + mi:4 + mi]), inc="act")
                P.waitt("vector", atok)
                P.op("vector", lambda v, w=width: v.tensor_scalar(
                    out=qtmp[:, 0:w], in0=qtmp[:, 0:w], scalar1=MAGIC, scalar2=-1.0,
                    op0=OP.subtract, op1=OP.max), inc="dve")
                return P.op("vector", lambda v, w=width: v.tensor_scalar(
                    out=tern[:, 0:w], in0=qtmp[:, 0:w], scalar1=1.0, scalar2=None,
                    op0=OP.min), inc="dve")
            quant_tile.tern_free = None

            stg_free = [None, None]
            for mi, scratch in [(0, sg), (1, su)]:
                wap = d_in["wg"] if mi == 0 else d_in["wu"]
                for j in range(16):
                    ssl = j % 2
                    for half in range(2):
                        r = 2 * j + half
                        sl = r % 2
                        P.waitt("sync", wt_free[sl])
                        src = wap[r * 128:(r + 1) * 128, :]
                        ltok = P.op("sync", lambda s, sl=sl, src=src: s.dma_start(
                            out=wtile[sl][:, 0:1024], in_=src), inc=f"dx{sl}", k=16)
                        ttok = quant_tile(sl, mi, 1024, ltok)
                        wt_free[sl] = ttok
                        for k in range(8):
                            transpose_tile(
                                tern[:, k * 128:(k + 1) * 128],
                                stg_g[ssl][:, k * 256 + half * 128: k * 256 + half * 128 + 128],
                                [6, 7], ttok, pre_evac_tok=stg_free[ssl])
                        quant_tile.tern_free = P.mark("pe")
                    lastev = P.mark("dve")
                    P.waitt("sync", lastev)
                    sttok = P.op("sync", lambda g, ssl=ssl, j=j, scratch=scratch: g.dma_start(
                        out=scratch[j], in_=stg_g[ssl][:]), inc=f"dw{ssl}", k=16)
                    stg_free[ssl] = sttok
            std_free = [None, None]
            for r in range(8):
                sl = r % 2
                ssl = r % 2
                P.waitt("sync", wt_free[sl])
                src = d_in["wd"][r * 128:(r + 1) * 128, :]
                ltok = P.op("sync", lambda s, sl=sl, src=src: s.dma_start(
                    out=wtile[sl][:], in_=src), inc=f"dx{sl}", k=16)
                ttok = quant_tile(sl, 2, 4096, ltok)
                wt_free[sl] = ttok
                for kc in range(32):
                    transpose_tile(tern[:, kc * 128:(kc + 1) * 128],
                                   stg_d[ssl][:, kc * 128:(kc + 1) * 128], [6, 7], ttok,
                                   pre_evac_tok=std_free[ssl])
                quant_tile.tern_free = P.mark("pe")
                lastev = P.mark("dve")
                P.waitt("sync", lastev)
                stok = P.op("sync", lambda g, r=r, ssl=ssl: g.dma_start(
                    out=sd[r], in_=stg_d[ssl][:]), inc=f"du{ssl}", k=16)
                std_free[ssl] = stok
            P.barrier()
            bankfree.update({i: None for i in range(8)})

        def emit_ffn(call, dbg_dump=False):
            def pbar(bit=0):
                if 0 < paranoid < 256 and paranoid not in (10, 20) and (paranoid >> bit) & 1:
                    P.barrier()
            src = d_in["x"] if call in (0, 2) else x1
            dst = x1 if call in (0, 5) else y_out
            if call == 5:
                call = 0
            with ExitStack() as fs:
                FE = fs.enter_context
                xtile = [FE(nc.sbuf_tensor(f"f{call}xt{i}", [128, 1024], F32)) for i in range(4)]
                sqtr = FE(nc.sbuf_tensor(f"f{call}sqtr", [128, 1024], BF16))
                xc = [FE(nc.sbuf_tensor(f"f{call}xc{i}", [128, 1024], F32)) for i in range(4)]
                xq = [FE(nc.sbuf_tensor(f"f{call}xq{i}", [128, 1024], BF16)) for i in range(2)]
                xqT = FE(nc.sbuf_tensor(f"f{call}xqT", [128, NTJ * 8 * 128], BF16))
                prod = FE(nc.sbuf_tensor(f"f{call}prod", [128, NTJ * 4096], F32))
                qv = FE(nc.sbuf_tensor(f"f{call}qv", [128, 2048], F32))
                vq = FE(nc.sbuf_tensor(f"f{call}vq", [128, 4096], BF16))
                vqT = FE(nc.sbuf_tensor(f"f{call}vqT", [128, NTJ * 32 * 128], BF16))
                wbg = [FE(nc.sbuf_tensor(f"f{call}wbg{i}", [128, 2048], BF16)) for i in range(2)]
                wbu = [FE(nc.sbuf_tensor(f"f{call}wbu{i}", [128, 2048], BF16)) for i in range(2)]
                wbd = [FE(nc.sbuf_tensor(f"f{call}wbd{i}", [128, 4096], BF16)) for i in range(2)]
                sgb = FE(nc.sbuf_tensor(f"f{call}sgb", [128, 512], F32))
                sc = FE(nc.sbuf_tensor(f"f{call}sc", [128, NTJ * 16], F32))
                vep = FE(nc.sbuf_tensor(f"f{call}vep", [128, 2 * NTJ], F32))
                qsc = FE(nc.sbuf_tensor(f"f{call}qsc", [128, 24], F32))

                xfree = [None] * 4
                wbg_free = [None, None]
                wbu_free = [None, None]
                wbd_free = [None, None]
                xc_free = [None] * 4
                qvx_free = [None, None]
                sgb_free = [None, None]
                act_guard = None
                pe_guard = None
                SSQ, SX, MEAN, NMEAN, RMX, SXQ, GSC, USC, RSTD, T0, T1, VMX, SVQ, DSC, SCR = range(15)

                def col(tj, c):
                    return sc[:, tj * 16 + c: tj * 16 + c + 1]

                tg_order = list(reversed(range(8))) if paranoid == 20 else list(range(8))
                for tg in tg_order:
                    pbar(0)
                    row0 = tg * TG
                    xq_rdy = [None] * NTJ
                    P.waitt("vector", act_guard)
                    P.waitt("vector", pe_guard)
                    for tj in range(NTJ):
                        P.waitt("sync", xfree[tj])
                        ltok = P.op("sync", lambda s, tj=tj, r=row0 + tj * 128, src=src: s.dma_start(
                            out=xtile[tj][:], in_=src[r:r + 128, :]), inc=f"dx{tj}", k=16)
                        if paranoid == 11 and tg == 0 and tj == 0:
                            P.waitt("vector", ltok)
                            P.op("vector", lambda v: v.tensor_copy(dbg_vq[:], xtile[0][:, 0:512]), inc="dve")
                        if paranoid == 11:
                            continue
                        if dbg_dump and tg == 0 and tj == 0:
                            P.waitt("vector", ltok)
                            P.op("vector", lambda v: v.tensor_copy(dbg_vq[:], xtile[0][:, 0:512]), inc="dve")
                            for _ in range(200):
                                P.op("vector", lambda v: v.tensor_scalar(
                                    out=junkv[:], in0=junkv[:], scalar1=0.0, scalar2=None, op0=OP.mult))
                            P.op("vector", lambda v: v.tensor_copy(dbg_prod[:], xtile[0][:, 0:256]), inc="dve")
                            for _ in range(1000):
                                P.op("vector", lambda v: v.tensor_scalar(
                                    out=junkv[:], in0=junkv[:], scalar1=0.0, scalar2=None, op0=OP.mult))
                            P.op("vector", lambda v: v.tensor_copy(dbg_xq[:], xtile[0][:, 0:512]), inc="dve")
                        P.waitt("scalar", ltok)
                        stok = P.op("scalar", lambda s, tj=tj: s.activation(
                            out=sqtr[:], in_=xtile[tj][:], func=AF.Square,
                            accum_out=col(tj, SSQ)), inc="act")
                        P.waitt("vector", ltok)
                        P.op("vector", lambda v, tj=tj: v.tensor_reduce(
                            col(tj, SX), xtile[tj][:], axis=AX, op=OP.add), inc="dve")
                        P.waitt("vector", stok)
                        sepv()
                        P.op("vector", lambda v, tj=tj: v.tensor_scalar(
                            out=col(tj, MEAN), in0=col(tj, SX), scalar1=1.0 / 1024, scalar2=None,
                            op0=OP.mult), inc="dve")
                        P.op("vector", lambda v, tj=tj: v.tensor_scalar(
                            out=col(tj, T0), in0=col(tj, SSQ), scalar1=1.0 / 1024, scalar2=None,
                            op0=OP.mult), inc="dve")
                        P.op("vector", lambda v, tj=tj: v.tensor_tensor(
                            out=col(tj, T1), in0=col(tj, MEAN), in1=col(tj, MEAN), op=OP.mult), inc="dve")
                        P.op("vector", lambda v, tj=tj: v.tensor_tensor(
                            out=col(tj, T0), in0=col(tj, T0), in1=col(tj, T1), op=OP.subtract), inc="dve")
                        P.op("vector", lambda v, tj=tj: v.tensor_scalar(
                            out=qsc[:, 4 + tj:5 + tj], in0=col(tj, T0), scalar1=1e-5, scalar2=None,
                            op0=OP.add), inc="dve")
                        nmtok = P.op("vector", lambda v, tj=tj: v.tensor_scalar(
                            out=col(tj, NMEAN), in0=col(tj, MEAN), scalar1=-1.0, scalar2=None,
                            op0=OP.mult), inc="dve")
                        if dbg_dump and tg == 0 and tj == 0:
                            P.op("vector", lambda v: v.tensor_copy(dbg_sc[:, 24:40], sc[:, 0:16]), inc="dve")
                            P.op("vector", lambda v: v.tensor_copy(dbg_wg[:], xtile[0][:, 0:512]), inc="dve")
                            nmtok = P.mark("dve")
                        P.waitt("scalar", nmtok)
                        P.waitt("scalar", xc_free[tj])
                        xctok = P.op("scalar", lambda s, tj=tj: s.activation(
                            out=xc[tj][:], in_=xtile[tj][:], func=AF.Identity,
                            bias=col(tj, NMEAN), scale=1.0), inc="act")
                        P.waitt("vector", xctok)
                        P.op("vector", lambda v, tj=tj: v.tensor_reduce(
                            col(tj, RMX), xc[tj][:], axis=AX, op=OP.max,
                            apply_absolute_value=True), inc="dve")
                        sepv()
                        P.op("vector", lambda v, tj=tj: v.tensor_scalar(
                            out=qsc[:, tj:tj + 1], in0=col(tj, RMX), scalar1=1e-9, scalar2=None,
                            op0=OP.max), inc="dve")
                    pbar(1)
                    a2_in = P.mark("dve")
                    P.waitt("scalar", a2_in)
                    P.op("scalar", lambda s: s.activation(
                        out=qsc[:, 8:16], in_=qsc[:, 0:8], func=AF.Ln), inc="act")
                    P.op("scalar", lambda s: s.activation(
                        out=qsc[:, 16:20], in_=qsc[:, 8:12], func=AF.Exp,
                        scale=-1.0, bias=ln127[:, 0:1]), inc="act")
                    a2tok = P.op("scalar", lambda s: s.activation(
                        out=qsc[:, 20:24], in_=qsc[:, 12:16], func=AF.Exp,
                        scale=-0.5), inc="act")
                    P.waitt("vector", a2tok)
                    for tj in range(NTJ):
                        P.op("vector", lambda v, tj=tj: v.tensor_tensor(
                            out=col(tj, T1), in0=qsc[:, tj:tj + 1], in1=qsc[:, 16 + tj:17 + tj],
                            op=OP.mult), inc="dve")
                        P.op("vector", lambda v, tj=tj: v.tensor_scalar(
                            out=col(tj, T1), in0=col(tj, T1), scalar1=-1.0 / 127, scalar2=2.0,
                            op0=OP.mult, op1=OP.add), inc="dve")
                        sxtok = P.op("vector", lambda v, tj=tj: v.tensor_tensor(
                            out=col(tj, SXQ), in0=qsc[:, 16 + tj:17 + tj], in1=col(tj, T1),
                            op=OP.mult), inc="dve")
                        P.waitt("scalar", sxtok)
                        P.waitt("scalar", qvx_free[tj % 2])
                        q1tok = P.op("scalar", lambda s, tj=tj: s.activation(
                            out=qv[:, (tj % 2) * 1024:(tj % 2) * 1024 + 1024], in_=xc[tj][:],
                            func=AF.Identity, bias=magic[:, 0:1], scale=col(tj, SXQ)), inc="act")
                        xc_free[tj] = q1tok
                        P.waitt("vector", q1tok)
                        xq_tok = P.op("vector", lambda v, tj=tj: v.tensor_scalar(
                            out=xq[tj % 2][:], in0=qv[:, (tj % 2) * 1024:(tj % 2) * 1024 + 1024],
                            scalar1=MAGIC, scalar2=None, op0=OP.subtract), inc="dve")
                        qvx_free[tj % 2] = xq_tok
                        if dbg_dump and tg == 0 and tj == 0:
                            P.op("vector", lambda v: v.tensor_copy(dbg_xq[:], xq[0][:, 0:512]), inc="dve")
                            P.op("vector", lambda v: v.tensor_copy(dbg_sc[:, 0:24], qsc[:]), inc="dve")
                            qvx_free[0] = P.mark("dve")
                        for k in range(8):
                            xq_rdy[tj] = transpose_tile(
                                xq[tj % 2][:, k * 128:(k + 1) * 128],
                                xqT[:, (tj * 8 + k) * 128:(tj * 8 + k) * 128 + 128],
                                [6, 7], xq_tok)
                    pbar(2)
                    for tj in range(NTJ):
                        P.op("vector", lambda v, tj=tj: v.tensor_tensor(
                            out=col(tj, T1), in0=qsc[:, 20 + tj:21 + tj], in1=qsc[:, 20 + tj:21 + tj],
                            op=OP.mult), inc="dve")
                        P.op("vector", lambda v, tj=tj: v.tensor_tensor(
                            out=col(tj, T1), in0=col(tj, T1), in1=qsc[:, 4 + tj:5 + tj], op=OP.mult), inc="dve")
                        P.op("vector", lambda v, tj=tj: v.tensor_scalar(
                            out=col(tj, T1), in0=col(tj, T1), scalar1=-0.5, scalar2=1.5,
                            op0=OP.mult, op1=OP.add), inc="dve")
                        P.op("vector", lambda v, tj=tj: v.tensor_tensor(
                            out=col(tj, RSTD), in0=qsc[:, 20 + tj:21 + tj], in1=col(tj, T1), op=OP.mult), inc="dve")
                        P.op("vector", lambda v, tj=tj: v.tensor_tensor(
                            out=col(tj, T0), in0=col(tj, RMX), in1=col(tj, RSTD), op=OP.mult), inc="dve")
                        P.op("vector", lambda v, tj=tj: v.tensor_scalar(
                            out=col(tj, T0), in0=col(tj, T0), scalar1=1e-5, scalar2=None,
                            op0=OP.max), inc="dve")
                        P.op("vector", lambda v, tj=tj: v.tensor_tensor(
                            out=col(tj, GSC), in0=col(tj, T0), in1=wmean[:, 0:1], op=OP.mult), inc="dve")
                        P.op("vector", lambda v, tj=tj: v.tensor_scalar(
                            out=col(tj, GSC), in0=col(tj, GSC), scalar1=1.0 / 127, scalar2=None,
                            op0=OP.mult), inc="dve")
                        P.op("vector", lambda v, tj=tj: v.tensor_tensor(
                            out=col(tj, USC), in0=col(tj, T0), in1=wmean[:, 1:2], op=OP.mult), inc="dve")
                        P.op("vector", lambda v, tj=tj: v.tensor_scalar(
                            out=col(tj, USC), in0=col(tj, USC), scalar1=1.0 / 127, scalar2=None,
                            op0=OP.mult), inc="dve")
                    scl_tok = P.mark("dve")
                    if paranoid == 10:
                        act_guard = P.mark("act")
                        pe_guard = P.mark("pe")
                        continue
                    pbar(3)
                    prod_last = [None] * NTJ
                    mmtok = None
                    for n2 in range(16):
                        gsl = n2 % 2
                        P.waitt("sync", wbg_free[gsl])
                        gtok = P.op("sync", lambda s, gsl=gsl, n2=n2: s.dma_start(
                            out=wbg[gsl][:], in_=sg[n2]), inc=f"dw{gsl}", k=16)
                        P.waitt("sync", wbu_free[gsl])
                        utok = P.op("sync", lambda s, gsl=gsl, n2=n2: s.dma_start(
                            out=wbu[gsl][:], in_=su[n2]), inc=f"du{gsl}", k=16)
                        for tj in range(NTJ):
                            bg, bu = 2 * tj, 2 * tj + 1
                            P.waitt("tensor", bankfree[bg])
                            P.waitt("tensor", bankfree[bu])
                            P.waitt("tensor", xq_rdy[tj])
                            P.waitt("tensor", gtok)
                            P.waitt("tensor", utok)
                            for k in range(8):
                                lhs = xqT[:, (tj * 8 + k) * 128:(tj * 8 + k) * 128 + 128]
                                P.op("tensor", lambda t, lhs=lhs, bg=bg, gsl=gsl, k=k: t.matmul(
                                    PS[bg][:, 0:256], lhs, wbg[gsl][:, k * 256:(k + 1) * 256],
                                    start=(k == 0), stop=(k == 7)))
                                fn = (lambda t, lhs=lhs, bu=bu, gsl=gsl, k=k: t.matmul(
                                    PS[bu][:, 0:256], lhs, wbu[gsl][:, k * 256:(k + 1) * 256],
                                    start=(k == 0), stop=(k == 7)))
                                if k == 7:
                                    mmtok = P.op("tensor", fn, inc="pe")
                                else:
                                    P.op("tensor", fn)
                            wbg_free[gsl] = mmtok
                            wbu_free[gsl] = mmtok
                            P.waitt("scalar", mmtok)
                            P.waitt("scalar", scl_tok)
                            P.waitt("scalar", sgb_free[tj % 2])
                            sltok = P.op("scalar", lambda s, bg=bg, tj=tj: s.activation(
                                out=sgb[:, (tj % 2) * 256:(tj % 2) * 256 + 256],
                                in_=PS[bg][:, 0:256], func=AF.Silu, scale=col(tj, GSC)), inc="act")
                            P.waitt("vector", sltok)
                            ptok = P.op("vector", lambda v, bu=bu, tj=tj, n2=n2: v.scalar_tensor_tensor(
                                out=prod[:, tj * 4096 + n2 * 256: tj * 4096 + n2 * 256 + 256],
                                in0=PS[bu][:, 0:256], scalar=col(tj, USC),
                                in1=sgb[:, (tj % 2) * 256:(tj % 2) * 256 + 256],
                                op0=OP.mult, op1=OP.mult), inc="dve")
                            bankfree[bg] = ptok
                            bankfree[bu] = ptok
                            sgb_free[tj % 2] = ptok
                            prod_last[tj] = ptok
                            if dbg_dump and tg == 0 and tj == 0 and n2 == 0:
                                P.op("vector", lambda v: v.tensor_copy(dbg_prod[:], prod[:, 0:256]), inc="dve")
                                prod_last[0] = P.mark("dve")
                    pbar(4)
                    vqT_rdy = [None] * NTJ
                    qvv_free = qvx_free[0]
                    for tj in range(NTJ):
                        P.waitt("vector", prod_last[tj])
                        P.op("vector", lambda v, tj=tj: v.tensor_reduce(
                            col(tj, VMX), prod[:, tj * 4096:(tj + 1) * 4096], axis=AX,
                            op=OP.max, apply_absolute_value=True), inc="dve")
                        sepv()
                        P.op("vector", lambda v, tj=tj: v.tensor_scalar(
                            out=col(tj, VMX), in0=col(tj, VMX), scalar1=1e-5, scalar2=None,
                            op0=OP.max), inc="dve")
                        vmx_tok = P.mark("dve")
                        P.waitt("scalar", vmx_tok)
                        P.op("scalar", lambda s, tj=tj: s.activation(
                            out=col(tj, SCR), in_=col(tj, VMX), func=AF.Ln), inc="act")
                        sv0 = P.op("scalar", lambda s, tj=tj: s.activation(
                            out=col(tj, T0), in_=col(tj, SCR), func=AF.Exp,
                            scale=-1.0, bias=ln127[:, 0:1]), inc="act")
                        P.waitt("vector", sv0)
                        P.op("vector", lambda v, tj=tj: v.tensor_tensor(
                            out=col(tj, T1), in0=col(tj, VMX), in1=col(tj, T0), op=OP.mult), inc="dve")
                        P.op("vector", lambda v, tj=tj: v.tensor_scalar(
                            out=col(tj, T1), in0=col(tj, T1), scalar1=-1.0 / 127, scalar2=2.0,
                            op0=OP.mult, op1=OP.add), inc="dve")
                        svqtok = P.op("vector", lambda v, tj=tj: v.tensor_tensor(
                            out=col(tj, SVQ), in0=col(tj, T0), in1=col(tj, T1), op=OP.mult), inc="dve")
                        P.op("vector", lambda v, tj=tj: v.tensor_tensor(
                            out=col(tj, DSC), in0=col(tj, VMX), in1=wmean[:, 2:3], op=OP.mult), inc="dve")
                        P.op("vector", lambda v, tj=tj: v.tensor_scalar(
                            out=col(tj, DSC), in0=col(tj, DSC), scalar1=0.5 / 127, scalar2=None,
                            op0=OP.mult), inc="dve")
                        for h2 in range(2):
                            P.waitt("scalar", svqtok)
                            P.waitt("scalar", qvv_free)
                            P.waitt("scalar", qvx_free[1])
                            q2tok = P.op("scalar", lambda s, tj=tj, h2=h2: s.activation(
                                out=qv[:], in_=prod[:, tj * 4096 + h2 * 2048: tj * 4096 + (h2 + 1) * 2048],
                                func=AF.Identity, bias=magic[:, 0:1], scale=col(tj, SVQ)), inc="act")
                            P.waitt("vector", q2tok)
                            vq_tok = P.op("vector", lambda v, h2=h2: v.tensor_scalar(
                                out=vq[:, h2 * 2048:(h2 + 1) * 2048], in0=qv[:],
                                scalar1=MAGIC, scalar2=None, op0=OP.subtract), inc="dve")
                            qvv_free = vq_tok
                            qvx_free[0] = vq_tok
                            qvx_free[1] = vq_tok

                            for k in range(16):
                                kc = h2 * 16 + k
                                vqT_rdy[tj] = transpose_tile(
                                    vq[:, kc * 128:(kc + 1) * 128],
                                    vqT[:, (tj * 32 + kc) * 128:(tj * 32 + kc) * 128 + 128],
                                    [6, 7], vq_tok)
                    pbar(5)
                    dtok = None
                    for u8 in range(8):
                        dsl = u8 % 2
                        P.waitt("sync", wbd_free[dsl])
                        sdv = sd.rearrange("r p (kc m) -> p kc r m", kc=32)
                        for kc_ in range(4):
                            src3 = sdv[:, u8 * 4 + kc_, :, :]
                            dst3 = wbd[dsl][:, kc_ * 1024:(kc_ + 1) * 1024].rearrange(
                                "p (b m) -> p b m", b=8)
                            dltok = P.op("sync", lambda s, src3=src3, dst3=dst3: s.dma_start(
                                out=dst3, in_=src3), inc=f"dd{dsl}", k=16)
                        P.waitt("tensor", dltok)
                        if u8 == 0:
                            for tj in range(NTJ):
                                P.waitt("tensor", bankfree[2 * tj])
                                P.waitt("tensor", bankfree[2 * tj + 1])
                                P.waitt("tensor", vqT_rdy[tj])
                        for kc in range(4):
                            kca = u8 * 4 + kc
                            for tj in range(NTJ):
                                lhs = vqT[:, (tj * 32 + kca) * 128:(tj * 32 + kca) * 128 + 128]
                                P.op("tensor", lambda t, lhs=lhs, tj=tj, dsl=dsl, kc=kc, kca=kca: t.matmul(
                                    PS[2 * tj][:, 0:512], lhs, wbd[dsl][:, kc * 1024:kc * 1024 + 512],
                                    start=(kca == 0), stop=(kca == 31)))
                                fn = (lambda t, lhs=lhs, tj=tj, dsl=dsl, kc=kc, kca=kca: t.matmul(
                                    PS[2 * tj + 1][:, 0:512], lhs,
                                    wbd[dsl][:, kc * 1024 + 512:kc * 1024 + 1024],
                                    start=(kca == 0), stop=(kca == 31)))
                                if kc == 3 and tj == NTJ - 1:
                                    dtok = P.op("tensor", fn, inc="pe")
                                else:
                                    P.op("tensor", fn)
                        wbd_free[dsl] = dtok
                    dfin = P.mark("pe")
                    pbar(6)
                    for tj in range(NTJ):
                        P.waitt("vector", dfin)
                        etok = None
                        for half in range(2):
                            b = 2 * tj + half
                            etok = P.op("vector", lambda v, tj=tj, b=b, half=half: v.scalar_tensor_tensor(
                                out=xtile[tj][:, half * 512:(half + 1) * 512],
                                in0=PS[b][:, 0:512], scalar=col(tj, DSC),
                                in1=xtile[tj][:, half * 512:(half + 1) * 512],
                                op0=OP.mult, op1=OP.add), inc="dve")
                            bankfree[b] = etok
                        P.waitt("sync", etok)
                        stok = P.op("sync", lambda g, tj=tj, r=row0 + tj * 128, dst=dst: g.dma_start(
                            out=dst[r:r + 128, :], in_=xtile[tj][:]), inc=f"dx{tj}", k=16)
                        xfree[tj] = stok
                    act_guard = P.mark("act")
                    pe_guard = P.mark("pe")
            P.barrier()
            bankfree.update({i: None for i in range(8)})

        def emit_attn():
            with ExitStack() as as_:
                AE = as_.enter_context
                wlat_sb = AE(nc.sbuf_tensor("wlat_sb", [128, 8 * 768], BF16))
                wqc_sb = AE(nc.sbuf_tensor("wqc_sb", [128, 2 * 896], BF16))
                wkc_sb = AE(nc.sbuf_tensor("wkc_sb", [128, 2 * 896], BF16))
                wv_sb = AE(nc.sbuf_tensor("wv_sb", [128, 2 * 1024], BF16))
                blat = AE(nc.sbuf_tensor("blat", [128, 6], F32))
                bqcp = AE(nc.sbuf_tensor("bqcp", [128, 7], F32))
                bkcp = AE(nc.sbuf_tensor("bkcp", [128, 7], F32))
                bvb = AE(nc.sbuf_tensor("bvb", [128, 1024], F32))
                bob = AE(nc.sbuf_tensor("bob", [128, 1024], F32))
                wstage = AE(nc.sbuf_tensor("wstage", [128, 1024], F32))

                def load_cast(dst_sb, hbm, nch, width):
                    last = None
                    for k in range(nch):
                        src = rowtiles(hbm, k, k + 1, width)
                        lt = P.op("sync", lambda s, src=src, w=width: s.dma_start(
                            out=sb3(wstage[:, 0:w], 1, w), in_=src), inc="dm", k=16)
                        P.waitt("vector", lt)
                        last = P.op("vector", lambda v, dst=dst_sb, k=k, w=width: v.tensor_copy(
                            dst[:, k * w:(k + 1) * w], wstage[:, 0:w]), inc="dve")
                        P.waitt("sync", last)
                    return last

                load_cast(wlat_sb, d_in["wlat"], 8, 768)
                load_cast(wqc_sb, d_in["wqc"], 2, 896)
                load_cast(wkc_sb, d_in["wkc"], 2, 896)
                load_cast(wv_sb, d_in["wv"], 2, 1024)
                for dstb, nm in [(blat, "blat_p"), (bqcp, "bqc_pp"), (bkcp, "bkc_pp"),
                                 (bvb, "bv_b"), (bob, "bo_b")]:
                    P.op("sync", lambda s, dstb=dstb, nm=nm: s.dma_start(
                        out=dstb[:], in_=d_in[nm]), inc="dm", k=16)
                wload_ld = P.mark("dm")
                wload_cast = P.mark("dve")

                def do_batch(b):
                    brow = b * S
                    with ExitStack() as bs:
                        BE = bs.enter_context
                        QT = BE(nc.sbuf_tensor(f"QT{b}", [128, 8 * 2048], BF16))
                        KT = BE(nc.sbuf_tensor(f"KT{b}", [128, 8 * 2048], BF16))
                        VB2 = BE(nc.sbuf_tensor(f"VB{b}", [128, 16 * 16 * 65], BF16))
                        VB2r = VB2.rearrange("p (k c) -> p k c", c=65)
                        P.op("gpsimd", lambda g: g.memset(VB2r[:, :, 64:65], 1.0))

                        def do_scope_a():
                          with ExitStack() as sa:
                              SA = sa.enter_context
                              xtile = [SA(nc.sbuf_tensor(f"a{b}xt{i}", [128, 1024], F32)) for i in range(4)]
                              sqtr = SA(nc.sbuf_tensor(f"a{b}sqtr", [128, 1024], BF16))
                              xn = [SA(nc.sbuf_tensor(f"a{b}xn{i}", [128, 1024], F32)) for i in range(2)]
                              xlnT = SA(nc.sbuf_tensor(f"a{b}xlnT", [128, 8 * 512], BF16))
                              cqg = SA(nc.sbuf_tensor(f"a{b}cqg", [128, 2 * 2048], BF16))
                              ckvg = SA(nc.sbuf_tensor(f"a{b}ckvg", [128, 2 * 2048], BF16))
                              cosb = SA(nc.sbuf_tensor(f"a{b}cosb", [128, 512], F32))
                              sinb = SA(nc.sbuf_tensor(f"a{b}sinb", [128, 512], F32))
                              rotb = SA(nc.sbuf_tensor(f"a{b}rotb", [128, 512], F32))
                              rawq = SA(nc.sbuf_tensor(f"a{b}rawq", [128, 512], F32))
                              rawk = SA(nc.sbuf_tensor(f"a{b}rawk", [128, 512], F32))
                              sc = SA(nc.sbuf_tensor(f"a{b}sc", [128, 64], F32))
                              vep = SA(nc.sbuf_tensor(f"a{b}vep", [128, 12], F32))

                              SSQ, SX, MEAN, NMEAN, RSTD, NMR, T0, T1 = range(8)

                              def acol(tj, c):
                                  return sc[:, tj * 8 + c: tj * 8 + c + 1]

                              lat_free = {m: None for m in range(6)}
                              lat_prev = None
                              xn_free = [None, None]
                              raw_free = [None, None]
                              rot_free = None
                              tab_free = None
                              a_act_guard = None
                              for tg in range(4):
                                  P.waitt("vector", a_act_guard)
                                  for tj in range(4):
                                      r = brow + tg * 512 + tj * 128
                                      ltok = P.op("sync", lambda s, tj=tj, r=r: s.dma_start(
                                          out=xtile[tj][:], in_=x1[r:r + 128, :]), inc=f"dx{tj}", k=16)
                                      P.waitt("scalar", ltok)
                                      stok = P.op("scalar", lambda s, tj=tj: s.activation(
                                          out=sqtr[:], in_=xtile[tj][:], func=AF.Square,
                                          accum_out=acol(tj, SSQ)), inc="act")
                                      P.waitt("vector", ltok)
                                      P.op("vector", lambda v, tj=tj: v.tensor_reduce(
                                          acol(tj, SX), xtile[tj][:], axis=AX, op=OP.add), inc="dve")
                                      P.waitt("vector", stok)
                                      sepv()
                                      P.op("vector", lambda v, tj=tj: v.tensor_scalar(
                                          out=acol(tj, MEAN), in0=acol(tj, SX), scalar1=1.0 / 1024,
                                          scalar2=None, op0=OP.mult), inc="dve")
                                      P.op("vector", lambda v, tj=tj: v.tensor_scalar(
                                          out=acol(tj, T0), in0=acol(tj, SSQ), scalar1=1.0 / 1024,
                                          scalar2=None, op0=OP.mult), inc="dve")
                                      P.op("vector", lambda v, tj=tj: v.tensor_tensor(
                                          out=acol(tj, T1), in0=acol(tj, MEAN), in1=acol(tj, MEAN),
                                          op=OP.mult), inc="dve")
                                      P.op("vector", lambda v, tj=tj: v.tensor_tensor(
                                          out=acol(tj, T0), in0=acol(tj, T0), in1=acol(tj, T1),
                                          op=OP.subtract), inc="dve")
                                      P.op("vector", lambda v, tj=tj: v.tensor_scalar(
                                          out=vep[:, tj:tj + 1], in0=acol(tj, T0), scalar1=1e-5,
                                          scalar2=None, op0=OP.add), inc="dve")
                                  sq_in = P.mark("dve")
                                  P.waitt("scalar", sq_in)
                                  P.op("scalar", lambda s: s.activation(
                                      out=vep[:, 4:8], in_=vep[:, 0:4], func=AF.Ln), inc="act")
                                  sqtok = P.op("scalar", lambda s: s.activation(
                                      out=vep[:, 8:12], in_=vep[:, 4:8], func=AF.Exp,
                                      scale=-0.5), inc="act")
                                  P.waitt("vector", sqtok)
                                  for tj in range(4):
                                      P.op("vector", lambda v, tj=tj: v.tensor_tensor(
                                          out=acol(tj, T1), in0=vep[:, 8 + tj:9 + tj], in1=vep[:, 8 + tj:9 + tj],
                                          op=OP.mult), inc="dve")
                                      P.op("vector", lambda v, tj=tj: v.tensor_tensor(
                                          out=acol(tj, T1), in0=acol(tj, T1), in1=vep[:, tj:tj + 1],
                                          op=OP.mult), inc="dve")
                                      P.op("vector", lambda v, tj=tj: v.tensor_scalar(
                                          out=acol(tj, T1), in0=acol(tj, T1), scalar1=-0.5, scalar2=1.5,
                                          op0=OP.mult, op1=OP.add), inc="dve")
                                      P.op("vector", lambda v, tj=tj: v.tensor_tensor(
                                          out=acol(tj, RSTD), in0=vep[:, 8 + tj:9 + tj], in1=acol(tj, T1),
                                          op=OP.mult), inc="dve")
                                      P.op("vector", lambda v, tj=tj: v.tensor_tensor(
                                          out=acol(tj, T1), in0=acol(tj, MEAN), in1=acol(tj, RSTD),
                                          op=OP.mult), inc="dve")
                                      nmrtok = P.op("vector", lambda v, tj=tj: v.tensor_scalar(
                                          out=acol(tj, NMR), in0=acol(tj, T1), scalar1=-1.0,
                                          scalar2=None, op0=OP.mult), inc="dve")
                                      P.waitt("scalar", nmrtok)
                                      P.waitt("scalar", xn_free[tj % 2])
                                      xntok = P.op("scalar", lambda s, tj=tj: s.activation(
                                          out=xn[tj % 2][:], in_=xtile[tj][:], func=AF.Identity,
                                          bias=acol(tj, NMR), scale=acol(tj, RSTD)), inc="act")
                                      P.waitt("vector", lat_prev)
                                      for k in range(8):
                                          xn_free[tj % 2] = transpose_tile(
                                              xn[tj % 2][:, k * 128:(k + 1) * 128],
                                              xlnT[:, k * 512 + tj * 128: k * 512 + tj * 128 + 128],
                                              [6, 7], xntok)
                                  xlnT_tok = P.mark("dve")
                                  P.waitt("tensor", xlnT_tok)
                                  P.wait("tensor", "dm", wload_ld[1])
                                  P.waitt("tensor", wload_cast)
                                  for m in range(6):
                                      P.waitt("tensor", lat_free[m])
                                  lattok = None
                                  for k in range(8):
                                      for m in range(6):
                                          lhs = wlat_sb[:, k * 768 + m * 128: k * 768 + m * 128 + 128]
                                          fn = (lambda t, lhs=lhs, m=m, k=k: t.matmul(
                                              PS[m][:, 0:512], lhs, xlnT[:, k * 512:(k + 1) * 512],
                                              start=(k == 0), stop=(k == 7)))
                                          if k == 7 and m == 5:
                                              lattok = P.op("tensor", fn, inc="pe")
                                          else:
                                              P.op("tensor", fn)
                                  lat_prev = lattok
                                  P.waitt("scalar", lattok)
                                  for m in range(4):
                                      dstb = cqg if m < 2 else ckvg
                                      evt = P.op("scalar", lambda s, m=m, dstb=dstb, tg=tg: s.activation(
                                          out=dstb[:, (m % 2) * 2048 + tg * 512:(m % 2) * 2048 + tg * 512 + 512],
                                          in_=PS[m][:, 0:512], func=AF.Gelu,
                                          bias=blat[:, m:m + 1], scale=1.0), inc="act")
                                      lat_free[m] = evt
                                  P.waitt("scalar", raw_free[0])
                                  r4 = P.op("scalar", lambda s: s.activation(
                                      out=rawq[:], in_=PS[4][:, 0:512], func=AF.Identity,
                                      bias=blat[:, 4:5], scale=1.0), inc="act")
                                  lat_free[4] = r4
                                  P.waitt("scalar", raw_free[1])
                                  r5 = P.op("scalar", lambda s: s.activation(
                                      out=rawk[:], in_=PS[5][:, 0:512], func=AF.Identity,
                                      bias=blat[:, 5:6], scale=1.0), inc="act")
                                  lat_free[5] = r5
                                  P.waitt("sync", tab_free)
                                  P.op("sync", lambda s, tg=tg: s.dma_start(
                                      out=cosb[:], in_=d_in["cosT"][:, tg * 512:(tg + 1) * 512]), inc="dc", k=16)
                                  ctok = P.op("sync", lambda s, tg=tg: s.dma_start(
                                      out=sinb[:], in_=d_in["sinT"][:, tg * 512:(tg + 1) * 512]), inc="dc", k=16)
                                  for idx, (raw, rtok, dstT) in enumerate(((rawq, r4, QT), (rawk, r5, KT))):
                                      P.waitt("scalar", rtok)
                                      P.waitt("scalar", rot_free)
                                      P.op("scalar", lambda g, raw=raw: g.dma_start(
                                          out=rotb[0:64, :], in_=raw[64:128, :]), inc="dr", k=16)
                                      shtok = P.op("scalar", lambda g, raw=raw: g.dma_start(
                                          out=rotb[64:128, :], in_=raw[0:64, :]), inc="dr", k=16)
                                      P.waitt("vector", shtok)
                                      P.op("vector", lambda v: v.tensor_scalar(
                                          out=rotb[0:64, :], in0=rotb[0:64, :], scalar1=-1.0,
                                          scalar2=None, op0=OP.mult), inc="dve")
                                      P.waitt("vector", ctok)
                                      P.op("vector", lambda v: v.tensor_tensor(
                                          out=rotb[:], in0=rotb[:], in1=sinb[:], op=OP.mult), inc="dve")
                                      P.op("vector", lambda v, raw=raw: v.tensor_tensor(
                                          out=raw[:], in0=raw[:], in1=cosb[:], op=OP.mult), inc="dve")
                                      addtok = P.op("vector", lambda v, raw=raw: v.tensor_tensor(
                                          out=raw[:], in0=raw[:], in1=rotb[:], op=OP.add), inc="dve")
                                      raw_free[idx] = addtok
                                      rot_free = addtok
                                      P.waitt("scalar", addtok)
                                      gtok = P.op("scalar", lambda s, raw=raw, dstT=dstT, tg=tg: s.activation(
                                          out=dstT[:, 7 * 2048 + tg * 512: 7 * 2048 + tg * 512 + 512],
                                          in_=raw[:], func=AF.Gelu), inc="act")
                                      tab_free = gtok
                                  a_act_guard = P.mark("act")
                                  P.wait("sync", "act", a_act_guard[1])

                              if paranoid == 30 and b == 0:
                                  cg0 = P.mark("act")
                                  P.waitt("vector", cg0)
                                  dq2 = P.op("vector", lambda v: v.tensor_copy(dbg_vq[:, 0:128], cqg[:, 0:128]), inc="dve")
                                  P.waitt("sync", dq2)
                                  P.op("sync", lambda s_: s_.dma_start(out=y_out[1664:1792, 0:128], in_=dbg_vq[:, 0:128]), inc="dm", k=16)
                              qkv_guard = P.mark("act")
                              P.waitt("tensor", qkv_guard)
                              for (wsb, bpp, dstT, srcT) in ((wqc_sb, bqcp, QT, cqg),
                                                             (wkc_sb, bkcp, KT, ckvg)):
                                  for j in range(7):
                                      for tg in range(4):
                                          bk = tg % 2
                                          P.waitt("tensor", bankfree[bk])
                                          mm = None
                                          for k in range(2):
                                              lhs = wsb[:, k * 896 + j * 128: j * 128 + k * 896 + 128]
                                              fn = (lambda t, lhs=lhs, k=k, bk=bk, tg=tg, srcT=srcT: t.matmul(
                                                  PS[bk][:, 0:512], lhs,
                                                  srcT[:, k * 2048 + tg * 512: k * 2048 + tg * 512 + 512],
                                                  start=(k == 0), stop=(k == 1)))
                                              if k == 1:
                                                  mm = P.op("tensor", fn, inc="pe")
                                              else:
                                                  P.op("tensor", fn)
                                          P.waitt("scalar", mm)
                                          ev = P.op("scalar", lambda s, bk=bk, j=j, tg=tg, bpp=bpp, dstT=dstT: s.activation(
                                              out=dstT[:, j * 2048 + tg * 512: j * 2048 + tg * 512 + 512],
                                              in_=PS[bk][:, 0:512], func=AF.Gelu,
                                              bias=bpp[:, j:j + 1], scale=1.0), inc="act")
                                          bankfree[bk] = ev
                              for tj in range(16):
                                  b0, b1 = 2, 3
                                  P.waitt("tensor", bankfree[b0])
                                  P.waitt("tensor", bankfree[b1])
                                  mm = None
                                  for k in range(2):
                                      lhs = ckvg[:, k * 2048 + tj * 128: k * 2048 + tj * 128 + 128]
                                      P.op("tensor", lambda t, lhs=lhs, k=k, b0=b0: t.matmul(
                                          PS[b0][:, 0:512], lhs, wv_sb[:, k * 1024:k * 1024 + 512],
                                          start=(k == 0), stop=(k == 1)))
                                      fn = (lambda t, lhs=lhs, k=k, b1=b1: t.matmul(
                                          PS[b1][:, 0:512], lhs, wv_sb[:, k * 1024 + 512:(k + 1) * 1024],
                                          start=(k == 0), stop=(k == 1)))
                                      if k == 1:
                                          mm = P.op("tensor", fn, inc="pe")
                                      else:
                                          P.op("tensor", fn)
                                  P.waitt("vector", mm)
                                  e0 = P.op("vector", lambda v, tj=tj, b0=b0: v.tensor_tensor(
                                      out=VB2r[:, tj * 16: tj * 16 + 8, 0:64],
                                      in0=PS[b0][:, 0:512].rearrange("p (h c) -> p h c", c=64),
                                      in1=bvb[:, 0:512].rearrange("p (h c) -> p h c", c=64),
                                      op=OP.add), inc="dve")
                                  e1 = P.op("vector", lambda v, tj=tj, b1=b1: v.tensor_tensor(
                                      out=VB2r[:, tj * 16 + 8: tj * 16 + 16, 0:64],
                                      in0=PS[b1][:, 0:512].rearrange("p (h c) -> p h c", c=64),
                                      in1=bvb[:, 512:1024].rearrange("p (h c) -> p h c", c=64),
                                      op=OP.add), inc="dve")
                                  bankfree[b0] = e0
                                  bankfree[b1] = e1

                        do_scope_a()
                        P.barrier()
                        bankfree.update({i: None for i in range(8)})

                        def do_scope_b():
                          with ExitStack() as sb:
                              SB = sb.enter_context
                              wo_sb = SB(nc.sbuf_tensor(f"wo_sb{b}", [128, 8 * 1024], BF16))
                              OT = SB(nc.sbuf_tensor(f"OT{b}", [128, 8 * 2048], BF16))
                              xtile = [SB(nc.sbuf_tensor(f"b{b}xt{i}", [128, 1024], F32)) for i in range(2)]
                              scb = SB(nc.sbuf_tensor(f"b{b}sc", [128, 16], F32))

                              wo_tok = None
                              for k in range(8):
                                  src = rowtiles(d_in["wo"], k, k + 1, 1024)
                                  lt = P.op("sync", lambda s, src=src: s.dma_start(
                                      out=sb3(wstage[:, 0:1024], 1, 1024), in_=src), inc="dm", k=16)
                                  P.waitt("vector", lt)
                                  wo_tok = P.op("vector", lambda v, k=k: v.tensor_copy(
                                      wo_sb[:, k * 1024:(k + 1) * 1024], wstage[:, 0:1024]), inc="dve")
                                  P.waitt("sync", wo_tok)

                              ebf = [SB(nc.sbuf_tensor(f"eb{b}{i}", [128, 512], BF16)) for i in range(2)]
                              Oq = [SB(nc.sbuf_tensor(f"Oq{b}{i}", [128, 64], BF16)) for i in range(2)]
                              OT6 = PS[6][:].bitcast(BF16)
                              OT7 = PS[7][:].bitcast(BF16)
                              ebf_free = [None, None]
                              Oq_free = [None, None]
                              for h in range(16):
                                  hr0 = (h % 2) * 64
                                  hch = h // 2
                                  for g4 in range(4):
                                      q0 = g4 * 512
                                      nkc = 4 * g4 + 4
                                      rhsQ = QT[hr0:hr0 + 64, hch * 2048 + q0: hch * 2048 + q0 + 512]
                                      pvtok = [None] * 4
                                      extok = [None] * nkc

                                      def emit_st(kc):
                                          sb_ = kc % 2
                                          a = kc - 4 * g4
                                          P.waitt("tensor", bankfree[sb_])
                                          lhsK = KT[hr0:hr0 + 64, hch * 2048 + kc * 128: hch * 2048 + kc * 128 + 128]
                                          if a < 0:
                                              mm = P.op("tensor", lambda t, lhsK=lhsK, rhsQ=rhsQ, sb_=sb_: t.matmul(
                                                  PS[sb_][:, 0:512], lhsK, rhsQ, start=True, stop=True), inc="pe")
                                          else:
                                              P.op("tensor", lambda t, lhsK=lhsK, rhsQ=rhsQ, sb_=sb_: t.matmul(
                                                  PS[sb_][:, 0:512], lhsK, rhsQ, start=True, stop=False))
                                              mm = P.op("tensor", lambda t, sb_=sb_, a=a: t.matmul(
                                                  PS[sb_][:, 0:512], identb[:], maskTw[a][:, 0:512],
                                                  start=False, stop=True), inc="pe")
                                          P.waitt("scalar", mm)
                                          P.waitt("scalar", ebf_free[sb_])
                                          ex = P.op("scalar", lambda s, sb_=sb_: s.activation(
                                              out=ebf[sb_][:], in_=PS[sb_][:, 0:512], func=AF.Exp,
                                              scale=0.125), inc="act")
                                          bankfree[sb_] = ex
                                          extok[kc] = ex

                                      def emit_pv(kc):
                                          sb_ = kc % 2
                                          for j in range(4):
                                              if j == 0:
                                                  P.waitt("tensor", extok[kc])
                                              if kc == 0:
                                                  P.waitt("tensor", bankfree[2 + j])
                                              rhsV = VB2r[:, kc * 16 + h, :]
                                              pvtok[j] = P.op("tensor", lambda t, sb_=sb_, j=j, rhsV=rhsV, kc=kc, nkc=nkc: t.matmul(
                                                  PS[2 + j][:, 0:65], ebf[sb_][:, j * 128:(j + 1) * 128], rhsV,
                                                  start=(kc == 0), stop=(kc == nkc - 1)), inc="pe")
                                          ebf_free[sb_] = pvtok[3]

                                      emit_st(0)
                                      for kc in range(1, nkc):
                                          emit_st(kc)
                                          emit_pv(kc - 1)
                                      emit_pv(nkc - 1)
                                      for j in range(4):
                                          qt = 4 * g4 + j
                                          P.waitt("scalar", pvtok[j])
                                          P.op("scalar", lambda s, j=j: s.activation(
                                              out=scb[:, 8 + j:9 + j], in_=PS[2 + j][:, 64:65],
                                              func=AF.Ln), inc="act")
                                          li = P.op("scalar", lambda s, j=j: s.activation(
                                              out=scb[:, 12 + j:13 + j], in_=scb[:, 8 + j:9 + j],
                                              func=AF.Exp, scale=-1.0), inc="act")
                                          seps()
                                          P.waitt("vector", li)
                                          P.waitt("vector", Oq_free[j % 2])
                                          nr = P.op("vector", lambda v, j=j: v.tensor_scalar(
                                              out=Oq[j % 2][:], in0=PS[2 + j][:, 0:64],
                                              scalar1=scb[:, 12 + j:13 + j], scalar2=None,
                                              op0=OP.mult), inc="dve")
                                          bankfree[2 + j] = nr
                                          ob = 6 + (j % 2)
                                          obv = OT6 if ob == 6 else OT7
                                          P.waitt("tensor", bankfree[ob])
                                          P.waitt("tensor", nr)
                                          tt = P.op("tensor", lambda t, obv=obv, j=j, hr0=hr0: t.transpose(
                                              obv[hr0:hr0 + 64, 0:128], Oq[j % 2][:], identb[:]), inc="pe")
                                          Oq_free[j % 2] = tt
                                          if j % 2 == 0:
                                              P.waitt("scalar", tt)
                                              oev = P.op("scalar", lambda s, obv=obv, hr0=hr0, hch=hch, qt=qt: s.activation(
                                                  out=OT[hr0:hr0 + 64, hch * 2048 + qt * 128: hch * 2048 + qt * 128 + 128],
                                                  in_=obv[hr0:hr0 + 64, 0:128], func=AF.Copy), inc="act")
                                          else:
                                              P.waitt("vector", tt)
                                              oev = P.op("vector", lambda v, obv=obv, hr0=hr0, hch=hch, qt=qt: v.tensor_copy(
                                                  OT[hr0:hr0 + 64, hch * 2048 + qt * 128: hch * 2048 + qt * 128 + 128],
                                                  obv[hr0:hr0 + 64, 0:128]), inc="dve")
                                          bankfree[ob] = oev
                              o_done = P.mark("act")
                              if paranoid == 30 and b == 0:
                                  P.waitt("vector", o_done)
                                  P.waitt("vector", P.mark("dm"))
                                  do_ = P.op("vector", lambda v: v.tensor_copy(dbg_xq[:, 0:256], OT[:, 0:256]), inc="dve")
                                  P.waitt("sync", do_)
                                  P.op("sync", lambda s_: s_.dma_start(out=y_out[1792:1920, 0:256], in_=dbg_xq[:, 0:256]), inc="dm", k=16)
                                  do2 = P.op("vector", lambda v: v.tensor_copy(dbg_vq[:, 0:128], OT[:, 7 * 2048:7 * 2048 + 128]), inc="dve")
                                  P.waitt("sync", do2)
                                  P.op("sync", lambda s_: s_.dma_start(out=y_out[1920:2048, 0:128], in_=dbg_vq[:, 0:128]), inc="dm", k=16)
                              xf = [None, None]
                              for tj in range(16):
                                  r = brow + tj * 128
                                  sl = tj % 2
                                  P.waitt("sync", xf[sl])
                                  lt = P.op("sync", lambda s, sl=sl, r=r: s.dma_start(
                                      out=xtile[sl][:], in_=x1[r:r + 128, :]), inc="dm", k=16)
                                  P.waitt("vector", lt)
                                  botok = P.op("vector", lambda v, sl=sl: v.tensor_tensor(
                                      out=xtile[sl][:], in0=xtile[sl][:], in1=bob[:], op=OP.add), inc="dve")
                                  bk0, bk1 = 2 * (tj % 2), 2 * (tj % 2) + 1
                                  P.waitt("tensor", bankfree[bk0])
                                  P.waitt("tensor", bankfree[bk1])
                                  P.waitt("tensor", o_done)
                                  P.waitt("tensor", wo_tok)
                                  mm = None
                                  for k in range(8):
                                      lhs = OT[:, k * 2048 + tj * 128: k * 2048 + tj * 128 + 128]
                                      P.op("tensor", lambda t, lhs=lhs, k=k, bk0=bk0: t.matmul(
                                          PS[bk0][:, 0:512], lhs, wo_sb[:, k * 1024:k * 1024 + 512],
                                          start=(k == 0), stop=(k == 7)))
                                      fn = (lambda t, lhs=lhs, k=k, bk1=bk1: t.matmul(
                                          PS[bk1][:, 0:512], lhs, wo_sb[:, k * 1024 + 512:(k + 1) * 1024],
                                          start=(k == 0), stop=(k == 7)))
                                      if k == 7:
                                          mm = P.op("tensor", fn, inc="pe")
                                      else:
                                          P.op("tensor", fn)
                                  P.waitt("vector", mm)
                                  P.waitt("vector", botok)
                                  P.op("vector", lambda v, sl=sl, bk0=bk0: v.tensor_tensor(
                                      out=xtile[sl][:, 0:512], in0=PS[bk0][:, 0:512],
                                      in1=xtile[sl][:, 0:512], op=OP.add), inc="dve")
                                  ev = P.op("vector", lambda v, sl=sl, bk1=bk1: v.tensor_tensor(
                                      out=xtile[sl][:, 512:1024], in0=PS[bk1][:, 0:512],
                                      in1=xtile[sl][:, 512:1024], op=OP.add), inc="dve")
                                  bankfree[bk0] = ev
                                  bankfree[bk1] = ev
                                  if paranoid == 30 and b == 0 and tj == 0:
                                      P.waitt("vector", ev)
                                      P.waitt("vector", P.mark("dm"))
                                      dw_ = P.op("vector", lambda v, sl=sl: v.tensor_copy(dbg_xq[:, 0:256], xtile[sl][:, 0:256]), inc="dve")
                                      P.waitt("sync", dw_)
                                      P.op("sync", lambda s_: s_.dma_start(out=y_out[2048:2176, 0:256], in_=dbg_xq[:, 0:256]), inc="dm", k=16)
                                  P.waitt("sync", ev)
                                  stok = P.op("sync", lambda g, sl=sl, r=r: g.dma_start(
                                      out=x1[r:r + 128, :], in_=xtile[sl][:]), inc=f"dx{sl}", k=16)
                                  xf[sl] = stok

                        do_scope_b()
                        P.barrier()
                        bankfree.update({i: None for i in range(8)})


                for b in range(2):
                    do_batch(b)

        def emit_dump_prep():
            with ExitStack() as cs:
                ctb = cs.enter_context(nc.sbuf_tensor("dmpb", [128, 2048], BF16))
                ctf = cs.enter_context(nc.sbuf_tensor("dmpf", [128, 2048], F32))
                free = None
                def dump(srcap, row):
                    nonlocal free
                    lt = P.op("sync", lambda s, srcap=srcap: s.dma_start(
                        out=ctb[:], in_=srcap), inc="dm", k=16)
                    P.waitt("vector", lt)
                    P.waitt("vector", free)
                    ct2 = P.op("vector", lambda v: v.tensor_copy(ctf[:], ctb[:]), inc="dve")
                    P.waitt("gpsimd", ct2)
                    f1 = P.op("gpsimd", lambda g, row=row: g.dma_start(
                        out=y_out[row:row + 128, :], in_=ctf[:, 0:1024]), inc="dm", k=16)
                    f2 = P.op("gpsimd", lambda g, row=row: g.dma_start(
                        out=y_out[row + 128:row + 256, :], in_=ctf[:, 1024:2048]), inc="dm", k=16)
                    free = f2
                    P.wait("sync", "dm", f2[1])
                dump(sg[0], 0)
                dump(sg[15], 256)
                dump(su[0], 512)
                dump(sd[0][:, 0:2048], 768)
                dump(sd[7][:, 2048:4096], 1024)
                wv_ = P.op("gpsimd", lambda g: g.dma_start(
                    out=y_out[1280:1408, 0:8], in_=wmean[:]), inc="dm", k=16)
                wv2 = P.op("gpsimd", lambda g: g.dma_start(
                    out=y_out[1408:1536, 0:16], in_=dbg[:]), inc="dm", k=16)
            P.barrier()

        def emit_copy_out():
            with ExitStack() as cs:
                ct = cs.enter_context(nc.sbuf_tensor("cpyt", [128, 1024], F32))
                free = None
                for tj in range(32):
                    P.waitt("sync", free)
                    lt = P.op("sync", lambda s, tj=tj: s.dma_start(
                        out=ct[:], in_=x1[tj * 128:(tj + 1) * 128, :]), inc="dm", k=16)
                    P.waitt("sync", lt)
                    free = P.op("sync", lambda g, tj=tj: g.dma_start(
                        out=y_out[tj * 128:(tj + 1) * 128, :], in_=ct[:]), inc="dm", k=16)
            P.barrier()

        if stage == 9:
            with ExitStack() as ms:
                mt = ms.enter_context(nc.sbuf_tensor("rb9", [128, 2048], F32))
                mb = ms.enter_context(nc.sbuf_tensor("rb9b", [128, 2048], BF16))
                for rep in range(8):
                    lt = P.op("sync", lambda s: s.dma_start(out=mb[:], in_=sd[7][:, 0:2048]), inc="dm", k=16)
                    P.waitt("vector", lt)
                    c1 = P.op("vector", lambda v: v.tensor_copy(mt[:], mb[:]), inc="dve")
                    P.waitt("scalar", c1)
                    P.op("scalar", lambda g, rep=rep: g.dma_start(
                        out=y_out[rep * 256:rep * 256 + 128, :], in_=mt[:, 0:1024]), inc="dm", k=16)
                    st9t = P.op("scalar", lambda g, rep=rep: g.dma_start(
                        out=y_out[rep * 256 + 128:rep * 256 + 256, :], in_=mt[:, 1024:2048]), inc="dm", k=16)
                    P.wait("sync", "dm", st9t[1])
                    P.op("sync", lambda s: s.nop(cycle_cnt=50000))
            P.barrier()
            with nc.Block() as block:
                block.tensor(lambda e: [f(e) for f in P.stream["tensor"]])
                block.vector(lambda e: [f(e) for f in P.stream["vector"]])
                block.scalar(lambda e: [f(e) for f in P.stream["scalar"]])
                block.gpsimd(lambda e: [f(e) for f in P.stream["gpsimd"]])
                block.sync(lambda e: [f(e) for f in P.stream["sync"]])
            return nc
        if stage == 7:
            lvl = paranoid
            with ExitStack() as ms:
                xt4 = [ms.enter_context(nc.sbuf_tensor(f"m7x{i}", [128, 1024], F32)) for i in range(4)]
                sq7 = ms.enter_context(nc.sbuf_tensor("m7sq", [128, 1024], BF16))
                sc7 = ms.enter_context(nc.sbuf_tensor("m7sc", [128, 16], F32))
                xc7 = [ms.enter_context(nc.sbuf_tensor(f"m7c{i}", [128, 1024], F32)) for i in range(2)]
                dmp = ms.enter_context(nc.sbuf_tensor("m7d", [128, 1024], F32))
                dmp2 = ms.enter_context(nc.sbuf_tensor("m7d2", [128, 16], F32))
                wdum = ms.enter_context(nc.sbuf_tensor("m7w", [128, 4096], BF16))
                snap = None
                for tj in range(4):
                    lt = P.op("sync", lambda s, tj=tj: s.dma_start(
                        out=xt4[tj][:], in_=d_in["x"][tj * 128:(tj + 1) * 128, :]), inc="dm", k=16)
                    if lvl >= 3:
                        for w in range(4):
                            P.op("sync", lambda s, tj=tj, w=w: s.dma_start(
                                out=wdum[:, 0:2048], in_=sg[(4 * tj + w) % 16]), inc="dm", k=16)
                    P.waitt("scalar", lt)
                    stok = P.op("scalar", lambda s, tj=tj: s.activation(
                        out=sq7[:], in_=xt4[tj][:], func=AF.Square,
                        accum_out=sc7[:, tj * 4:tj * 4 + 1]), inc="act")
                    P.waitt("vector", lt)
                    P.op("vector", lambda v, tj=tj: v.tensor_reduce(
                        sc7[:, tj * 4 + 1:tj * 4 + 2], xt4[tj][:], axis=AX, op=OP.add), inc="dve")
                    if lvl >= 2:
                        P.waitt("vector", stok)
                        sepv()
                        P.op("vector", lambda v, tj=tj: v.tensor_scalar(
                            out=sc7[:, tj * 4 + 2:tj * 4 + 3], in0=sc7[:, tj * 4 + 1:tj * 4 + 2],
                            scalar1=-1.0 / 1024, scalar2=None, op0=OP.mult), inc="dve")
                        P.waitt("scalar", P.mark("dve"))
                        P.op("scalar", lambda s, tj=tj: s.activation(
                            out=xc7[tj % 2][:], in_=xt4[tj][:], func=AF.Identity,
                            bias=sc7[:, tj * 4 + 2:tj * 4 + 3], scale=1.0), inc="act")
                    if tj == 0:
                        P.op("vector", lambda v: v.tensor_copy(dmp[:], xt4[0][:]), inc="dve")
                        snap = P.op("vector", lambda v: v.tensor_copy(dmp2[:], sc7[:]), inc="dve")
                P.waitt("gpsimd", snap)
                P.wait("gpsimd", "dve")
                P.op("gpsimd", lambda g: g.dma_start(out=y_out[0:128, :], in_=dmp[:]), inc="dm", k=16)
                P.op("gpsimd", lambda g: g.dma_start(out=y_out[128:256, 0:16], in_=dmp2[:]), inc="dm", k=16)
                P.op("gpsimd", lambda g: g.dma_start(out=y_out[256:384, :], in_=xt4[0][:]), inc="dm", k=16)
                P.wait("gpsimd", "dm")
            P.barrier()
            with nc.Block() as block:
                block.tensor(lambda e: [f(e) for f in P.stream["tensor"]])
                block.vector(lambda e: [f(e) for f in P.stream["vector"]])
                block.scalar(lambda e: [f(e) for f in P.stream["scalar"]])
                block.gpsimd(lambda e: [f(e) for f in P.stream["gpsimd"]])
                block.sync(lambda e: [f(e) for f in P.stream["sync"]])
            return nc
        if stage == 6:
            with ExitStack() as ms:
                mt = ms.enter_context(nc.sbuf_tensor("minitile", [128, 1024], F32))
                mdbg = ms.enter_context(nc.sbuf_tensor("minidbg", [128, 512], F32))
                lt = P.op("sync", lambda s: s.dma_start(out=mt[:], in_=d_in["x"][0:128, :]), inc="dm", k=16)
                P.waitt("vector", lt)
                ct = P.op("vector", lambda v: v.tensor_copy(mdbg[:], mt[:, 0:512]), inc="dve")
                P.waitt("gpsimd", ct)
                st1 = P.op("gpsimd", lambda g: g.dma_start(out=y_out[0:128, 0:512], in_=mdbg[:]), inc="dm", k=16)
                st2 = P.op("gpsimd", lambda g: g.dma_start(out=y_out[128:256, :], in_=mt[:]), inc="dm", k=16)
                P.wait("gpsimd", "dm")
            P.barrier()
            with nc.Block() as block:
                block.tensor(lambda e: [f(e) for f in P.stream["tensor"]])
                block.vector(lambda e: [f(e) for f in P.stream["vector"]])
                block.scalar(lambda e: [f(e) for f in P.stream["scalar"]])
                block.gpsimd(lambda e: [f(e) for f in P.stream["gpsimd"]])
                block.sync(lambda e: [f(e) for f in P.stream["sync"]])
            return nc
        if stage == 5:
            emit_ffn(5, dbg_dump=True)
            with ExitStack() as cs:
                relx = cs.enter_context(nc.sbuf_tensor("relx", [128, 1024], F32))
                rl = P.op("sync", lambda s: s.dma_start(out=relx[:], in_=d_in["x"][0:128, :]), inc="dm", k=16)
                P.waitt("gpsimd", rl)
                P.op("gpsimd", lambda g: g.dma_start(out=y_out[640:768, :], in_=relx[:]), inc="dm", k=16)
                free = None
                for (buf, row) in [(dbg_xq, 0), (dbg_vq, 128), (dbg_prod, 256), (dbg_wg, 384), (dbg_sc, 512)]:
                    w = buf.shape[1] if hasattr(buf, 'shape') else 1024
                    f1 = P.op("gpsimd", lambda g, buf=buf, row=row: g.dma_start(
                        out=y_out[row:row + 128, 0:buf.shape[1]], in_=buf[:]), inc="dm", k=16)
                P.wait("gpsimd", "dm")
            P.barrier()
            with nc.Block() as block:
                block.tensor(lambda e: [f(e) for f in P.stream["tensor"]])
                block.vector(lambda e: [f(e) for f in P.stream["vector"]])
                block.scalar(lambda e: [f(e) for f in P.stream["scalar"]])
                block.gpsimd(lambda e: [f(e) for f in P.stream["gpsimd"]])
                block.sync(lambda e: [f(e) for f in P.stream["sync"]])
            return nc
        if stage == 0:
            emit_dump_prep()
            with nc.Block() as block:
                block.tensor(lambda e: [f(e) for f in P.stream["tensor"]])
                block.vector(lambda e: [f(e) for f in P.stream["vector"]])
                block.scalar(lambda e: [f(e) for f in P.stream["scalar"]])
                block.gpsimd(lambda e: [f(e) for f in P.stream["gpsimd"]])
                block.sync(lambda e: [f(e) for f in P.stream["sync"]])
            return nc
        emit_ffn(0 if stage > 1 else 2)
        if stage >= 2:
            emit_attn()
            if stage == 2:
                if paranoid != 30:
                    emit_copy_out()
                else:
                    P.barrier()
            else:
                emit_ffn(1)
        P.barrier()

        with nc.Block() as block:
            block.tensor(lambda e: [f(e) for f in P.stream["tensor"]])
            block.vector(lambda e: [f(e) for f in P.stream["vector"]])
            block.scalar(lambda e: [f(e) for f in P.stream["scalar"]])
            block.gpsimd(lambda e: [f(e) for f in P.stream["gpsimd"]])
            block.sync(lambda e: [f(e) for f in P.stream["sync"]])
    return nc


def kernel(L0_attn_x, L0_attn_wq, L0_attn_bq, L0_attn_wqc, L0_attn_bqc,
           L0_attn_wqr, L0_attn_bqr, L0_attn_wkv, L0_attn_bkv, L0_attn_wkr,
           L0_attn_bkr, L0_attn_wkc, L0_attn_bkc, L0_attn_wv, L0_attn_bv,
           L0_attn_wo, L0_attn_bo, L0_ffn_w_gate, L0_ffn_w_up, L0_ffn_w_down):
    f32 = lambda a: np.ascontiguousarray(np.asarray(a, dtype=np.float32))
    x = f32(L0_attn_x).reshape(16, 2048, 1024)
    wq, bq = f32(L0_attn_wq), f32(L0_attn_bq)
    wqc, bqc = f32(L0_attn_wqc), f32(L0_attn_bqc)
    wqr, bqr = f32(L0_attn_wqr), f32(L0_attn_bqr)
    wkv, bkv = f32(L0_attn_wkv), f32(L0_attn_bkv)
    wkr, bkr = f32(L0_attn_wkr), f32(L0_attn_bkr)
    wkc, bkc = f32(L0_attn_wkc), f32(L0_attn_bkc)
    wv, bv = f32(L0_attn_wv), f32(L0_attn_bv)
    wo, bo = f32(L0_attn_wo), f32(L0_attn_bo)
    wg, wu, wd = f32(L0_ffn_w_gate), f32(L0_ffn_w_up), f32(L0_ffn_w_down)

    wlat = np.ascontiguousarray(np.concatenate([wq, wkv, wqr, wkr], axis=1))
    blat_vec = np.concatenate([bq, bkv, bqr, bkr])
    blat_p = np.ascontiguousarray(blat_vec.reshape(6, 128).T)
    bqc_pp = np.ascontiguousarray(bqc.reshape(7, 128).T)
    bkc_pp = np.ascontiguousarray(bkc.reshape(7, 128).T)
    bv_b = np.ascontiguousarray(np.broadcast_to(bv, (128, 1024)))
    bo_b = np.ascontiguousarray(np.broadcast_to(bo, (128, 1024)))

    inv = 1.0 / (10000.0 ** (np.arange(0, DR, 2, dtype=np.float64) / DR))
    t = np.arange(S, dtype=np.float64)
    freqs = t[:, None] * inv[None, :]
    emb = np.concatenate([freqs, freqs], axis=1)
    cosT = np.ascontiguousarray(np.cos(emb).T.astype(np.float32))
    sinT = np.ascontiguousarray(np.sin(emb).T.astype(np.float32))

    nc = build_program()
    common = dict(wg=wg, wu=wu, wd=wd, wlat=wlat, wqc=wqc, wkc=wkc, wv=wv, wo=wo,
                  blat_p=blat_p, bqc_pp=bqc_pp, bkc_pp=bkc_pp, bv_b=bv_b, bo_b=bo_b,
                  cosT=cosT, sinT=sinT)
    in_maps = []
    for c in range(N_CORES):
        m = dict(common)
        m["x"] = np.ascontiguousarray(x[2 * c:2 * c + 2].reshape(T, E))
        in_maps.append(m)
    res = run_bass_kernel_spmd(nc, in_maps, core_ids=list(range(N_CORES)))
    out = np.empty((16, 2048, 1024), dtype=np.float32)
    for c in range(N_CORES):
        out[2 * c:2 * c + 2] = np.asarray(res.results[c]["y"]).reshape(2, 2048, 1024)
    return out


# version 16 (final)
# speedup vs baseline: 2.1240x; 1.0108x over previous
import numpy as np
from contextlib import ExitStack

import concourse.bass as bass
import concourse.mybir as mybir
from concourse.bass_utils import run_bass_kernel_spmd
from concourse.alu_op_type import AluOpType as OP

F32 = mybir.dt.float32
BF16 = mybir.dt.bfloat16
AF = mybir.ActivationFunctionType
AX = mybir.AxisListType.X
MAGIC = 12582912.0
NEG_INF = -1.0e30

N_CORES = 8
T = 4096
E = 1024
S = 2048
DR = 128
INNER = 4096
TG = 512
NTJ = TG // 128

ENGS = ("tensor", "vector", "scalar", "gpsimd", "sync")


class Prog:
    def __init__(self, nc, stack):
        self.nc = nc
        self.auto_sep = True
        self.junk = {}
        self.stream = {e: [] for e in ENGS}
        self.sems = {}
        self.cnt = {}
        self.memo = {e: {} for e in ENGS}
        for name in ("pe", "dve", "act", "gp", "bar") + ('dx0', 'dx1', 'dx2', 'dx3', 'dw0', 'dw1', 'du0', 'du1', 'dd0', 'dd1', 'dm', 'dr', 'dc'):
            self.sems[name] = stack.enter_context(nc.semaphore(name))
            self.cnt[name] = 0

    def wait(self, eng, sem, val=None):
        v = self.cnt[sem] if val is None else val
        if v <= 0:
            return
        if self.memo[eng].get(sem, -1) >= v:
            return
        self.memo[eng][sem] = v
        h = self.sems[sem]
        self.stream[eng].append(lambda e, h=h, v=v: e.wait_ge(h, v))

    def waitt(self, eng, tok):
        if tok is not None:
            self.wait(eng, tok[0], tok[1])

    def op(self, eng, fn, inc=None, k=1):
        if inc is None:
            self.stream[eng].append(fn)
            self._sep(eng)
            return None
        self.cnt[inc] += k
        tok = (inc, self.cnt[inc])
        h = self.sems[inc]

        def wrapped(e, fn=fn, h=h, k=k):
            fn(e).then_inc(h, k)
        self.stream[eng].append(wrapped)
        self._sep(eng)
        return tok

    def _sep(self, eng):
        if not self.auto_sep or eng not in ("vector",):
            return
        j = self.junk.get(eng)
        if j is None:
            return
        if eng == "vector":
            self.stream[eng].append(lambda v, j=j: v.tensor_scalar(
                out=j[:], in0=j[:], scalar1=0.0, scalar2=None, op0=OP.mult))
        elif eng == "scalar":
            self.stream[eng].append(lambda s_, j=j: s_.activation(
                out=j[:], in_=j[:], func=AF.Identity, scale=0.0))
        else:
            self.stream[eng].append(lambda g, j=j: g.memset(j[:], 0.0))

    def mark(self, sem):
        return (sem, self.cnt[sem])

    DMA_SEMS = ['dx0', 'dx1', 'dx2', 'dx3', 'dw0', 'dw1', 'du0', 'du1', 'dd0', 'dd1', 'dm', 'dr', 'dc']

    def barrier(self):
        for ds in self.DMA_SEMS:
            if self.cnt[ds] > 0:
                self.wait("sync", ds)
                self.wait("gpsimd", ds)
        for e in ENGS:
            self.op(e, lambda en: en.drain(), inc="bar")
        for e in ENGS:
            self.wait(e, "bar")


def ceil_div(a, b):
    return (a + b - 1) // b


def rowtiles(hbm_ap, t0, t1, width):
    return hbm_ap.rearrange("(t p) d -> p t d", p=128)[:, t0:t1, :]


def sb3(sb_ap, nt, width):
    return sb_ap.rearrange("p (t d) -> p t d", t=nt)


def build_program(stage=3, paranoid=0):
    nc = bass.Bass("TRN2", target_bir_lowering=False)

    d_in = {}
    d_in["x"] = nc.dram_tensor("x", [T, E], F32, kind="ExternalInput").ap()
    for nm, shp in [("wg", [INNER, E]), ("wu", [INNER, E]), ("wd", [E, INNER]),
                    ("wlat", [E, 768]), ("wqc", [256, 896]), ("wkc", [256, 896]),
                    ("wv", [256, E]), ("wo", [E, E]),
                    ("blat_p", [128, 6]), ("bqc_pp", [128, 7]), ("bkc_pp", [128, 7]),
                    ("bv_b", [128, E]), ("bo_b", [128, E]),
                    ("cosT", [DR, S]), ("sinT", [DR, S])]:
        d_in[nm] = nc.dram_tensor(nm, shp, F32, kind="ExternalInput").ap()
    y_out = nc.dram_tensor("y", [T, E], F32, kind="ExternalOutput").ap()
    x1 = nc.dram_tensor("x1", [T, E], F32, kind="Internal").ap()
    sg = nc.dram_tensor("sg", [16, 128, 2048], BF16, kind="Internal").ap()
    su = nc.dram_tensor("su", [16, 128, 2048], BF16, kind="Internal").ap()
    sd = nc.dram_tensor("sd", [8, 128, 4096], BF16, kind="Internal").ap()

    with ExitStack() as stack:
        EC = stack.enter_context
        P = Prog(nc, stack)

        magic = EC(nc.sbuf_tensor("magic", [128, 1], F32))
        ident = EC(nc.sbuf_tensor("ident", [128, 128], F32))
        identb = EC(nc.sbuf_tensor("identb", [128, 128], BF16))
        maskbigT = EC(nc.sbuf_tensor("maskbigT", [128, 896], BF16))
        maskTw = [maskbigT[:, 384 - a * 128: 896 - a * 128] for a in range(4)]
        ones = EC(nc.sbuf_tensor("ones", [128, 128], F32))
        wmean = EC(nc.sbuf_tensor("wmean", [128, 8], F32))
        ln127 = EC(nc.sbuf_tensor("ln127", [128, 1], F32))
        dbg = EC(nc.sbuf_tensor("dbg", [128, 16], F32))
        junkv = EC(nc.sbuf_tensor("junkv", [128, 1], F32))
        junks = EC(nc.sbuf_tensor("junks", [128, 1], F32))
        if stage == 5:
            dbg_xq = EC(nc.sbuf_tensor("dbg_xq", [128, 512], F32))
            dbg_vq = EC(nc.sbuf_tensor("dbg_vq", [128, 512], F32))
            dbg_prod = EC(nc.sbuf_tensor("dbg_prod", [128, 256], F32))
            dbg_wg = EC(nc.sbuf_tensor("dbg_wg", [128, 512], F32))
            dbg_sc = EC(nc.sbuf_tensor("dbg_sc", [128, 40], F32))

        junkg = EC(nc.sbuf_tensor("junkg", [128, 1], F32))
        P.junk = {"vector": junkv, "scalar": junks, "gpsimd": junkg}

        if paranoid == 30:
            dbg_xq = EC(nc.sbuf_tensor("dbg_xq", [128, 256], F32))
            dbg_vq = EC(nc.sbuf_tensor("dbg_vq", [128, 128], F32))
            dbg_sc = EC(nc.sbuf_tensor("dbg_sc", [128, 8], F32))

        def sepv():
            P.op("vector", lambda v: v.tensor_scalar(
                out=junkv[:], in0=junkv[:], scalar1=0.0, scalar2=None, op0=OP.mult))

        def seps():
            P.op("scalar", lambda s: s.activation(
                out=junks[:], in_=junks[:], func=AF.Identity, scale=0.0))
        PS = [EC(nc.psum_tensor(f"ps{i}", [128, 512], F32)) for i in range(8)]
        bankfree = {i: None for i in range(8)}

        P.op("gpsimd", lambda g: g.memset(magic[:], MAGIC))
        P.op("gpsimd", lambda g: g.memset(ln127[:], float(np.log(127.0))))
        P.op("gpsimd", lambda g: g.memset(junkv[:], 1.0))
        P.op("gpsimd", lambda g: g.memset(junks[:], 1.0))
        P.op("gpsimd", lambda g: g.memset(ident[:], 0.0))
        P.op("gpsimd", lambda g: g.affine_select(
            out=ident[:], in_=ident[:], compare_op=OP.not_equal, fill=1.0,
            base=0, pattern=[[-1, 128]], channel_multiplier=1))
        P.op("gpsimd", lambda g: g.memset(identb[:], 0.0))
        P.op("gpsimd", lambda g: g.affine_select(
            out=identb[:], in_=identb[:], compare_op=OP.not_equal, fill=1.0,
            base=0, pattern=[[-1, 128]], channel_multiplier=1))
        P.op("gpsimd", lambda g: g.memset(maskbigT[:], 0.0))
        P.op("gpsimd", lambda g: g.memset(maskbigT[:, 0:384], NEG_INF))
        P.op("gpsimd", lambda g: g.affine_select(
            out=maskbigT[:, 384:512], in_=maskbigT[:, 384:512],
            compare_op=OP.is_ge, fill=NEG_INF,
            base=0, pattern=[[1, 128]], channel_multiplier=-1))
        tok_consts = P.op("gpsimd", lambda g: g.memset(ones[:], 1.0), inc="gp")
        for e in ("tensor", "scalar", "vector"):
            P.waitt(e, tok_consts)

        def transpose_tile(src_ap, dst_ap, banks, src_tok, pre_evac_tok=None):
            b = banks[transpose_tile.rr % len(banks)]
            transpose_tile.rr += 1
            P.waitt("tensor", bankfree[b])
            P.waitt("tensor", src_tok)
            if src_ap.dtype == BF16:
                pview = PS[b][:].bitcast(BF16)[:, 0:128]
                idt = identb
            else:
                pview = PS[b][:, 0:128]
                idt = ident
            ptok = P.op("tensor", lambda t, pview=pview, s=src_ap, idt=idt: t.transpose(
                pview, s, idt[:]), inc="pe")
            P.waitt("vector", ptok)
            if pre_evac_tok is not None:
                P.waitt("vector", pre_evac_tok)
            etok = P.op("vector", lambda v, pview=pview, d=dst_ap: v.tensor_copy(
                d, pview), inc="dve")
            bankfree[b] = etok
            return etok
        transpose_tile.rr = 0

        with ExitStack() as prep:
            PE_ = prep.enter_context
            wtile = [PE_(nc.sbuf_tensor(f"wt{i}", [128, 4096], F32)) for i in range(2)]
            qtmp = PE_(nc.sbuf_tensor("qtmp", [128, 4096], F32))
            tern = PE_(nc.sbuf_tensor("tern", [128, 4096], BF16))
            stg_g = [PE_(nc.sbuf_tensor(f"stg{i}", [128, 2048], BF16)) for i in range(2)]
            stg_d = [PE_(nc.sbuf_tensor(f"std{i}", [128, 4096], BF16)) for i in range(2)]
            acc = PE_(nc.sbuf_tensor("acc", [128, 4], F32))
            rtmp = PE_(nc.sbuf_tensor("rtmp", [128, 8], F32))

            wt_free = [None, None]
            mats = [("g", d_in["wg"]), ("u", d_in["wu"]), ("d", d_in["wd"])]
            for mi, (nm, wap) in enumerate(mats):
                last = None
                for tix in range(8):
                    sl = tix % 2
                    P.waitt("sync", wt_free[sl])
                    if nm == "d":
                        src = wap[tix * 128:(tix + 1) * 128, :]
                        dstv = wtile[sl][:]
                    else:
                        src = rowtiles(wap, tix * 4, tix * 4 + 4, 1024)
                        dstv = sb3(wtile[sl][:], 4, 1024)
                    ltok = P.op("sync", lambda s, src=src, dstv=dstv: s.dma_start(
                        out=dstv, in_=src), inc=f"dx{sl}", k=16)
                    P.waitt("vector", ltok)
                    rt = P.op("vector", lambda v, sl=sl, tix=tix: v.tensor_reduce(
                        rtmp[:, tix:tix + 1], wtile[sl][:], axis=AX, op=OP.add,
                        apply_absolute_value=True), inc="dve")
                    wt_free[sl] = rt
                P.op("vector", lambda v, mi=mi: v.tensor_copy(
                    dbg[:, 8 * (mi % 2):8 * (mi % 2) + 8], rtmp[:, 0:8]), inc="dve")
                last = P.op("vector", lambda v, mi=mi: v.tensor_reduce(
                    acc[:, mi:mi + 1], rtmp[:, 0:8], axis=AX, op=OP.add), inc="dve")
                P.waitt("tensor", bankfree[mi])
                P.waitt("tensor", last)
                mtok = P.op("tensor", lambda t, mi=mi: t.matmul(
                    PS[mi][:, 0:1], ones[:], acc[:, mi:mi + 1], start=True, stop=True), inc="pe")
                P.waitt("vector", mtok)
                P.op("vector", lambda v, mi=mi: v.tensor_scalar(
                    out=wmean[:, mi:mi + 1], in0=PS[mi][:, 0:1], scalar1=1.0 / (INNER * E),
                    scalar2=1e-5, op0=OP.mult, op1=OP.max), inc="dve")
                mtok2 = P.mark("dve")
                P.waitt("scalar", mtok2)
                P.op("scalar", lambda s, mi=mi: s.activation(
                    out=acc[:, 3:4], in_=wmean[:, mi:mi + 1], func=AF.Ln), inc="act")
                rtok = P.op("scalar", lambda s, mi=mi: s.activation(
                    out=wmean[:, 3 + mi:4 + mi], in_=acc[:, 3:4], func=AF.Exp,
                    scale=-1.0), inc="act")
                P.waitt("vector", rtok)
                P.op("vector", lambda v, mi=mi: v.tensor_tensor(
                    out=acc[:, 3:4], in0=wmean[:, mi:mi + 1], in1=wmean[:, 3 + mi:4 + mi],
                    op=OP.mult), inc="dve")
                P.op("vector", lambda v: v.tensor_scalar(
                    out=acc[:, 3:4], in0=acc[:, 3:4], scalar1=-1.0, scalar2=2.0,
                    op0=OP.mult, op1=OP.add), inc="dve")
                etok = P.op("vector", lambda v, mi=mi: v.tensor_tensor(
                    out=wmean[:, 3 + mi:4 + mi], in0=wmean[:, 3 + mi:4 + mi], in1=acc[:, 3:4],
                    op=OP.mult), inc="dve")
                bankfree[mi] = etok
            P.op("vector", lambda v: v.tensor_copy(wmean[:, 6:7], acc[:, 0:1]), inc="dve")
            P.op("vector", lambda v: v.tensor_copy(wmean[:, 7:8], acc[:, 2:3]), inc="dve")
            means_tok = P.mark("dve")

            def quant_tile(sl, mi, width, ld_tok):
                P.waitt("scalar", ld_tok)
                P.waitt("scalar", means_tok)
                P.waitt("scalar", quant_tile.tern_free)
                atok = P.op("scalar", lambda s, sl=sl, mi=mi, w=width: s.activation(
                    out=qtmp[:, 0:w], in_=wtile[sl][:, 0:w], func=AF.Identity,
                    bias=magic[:, 0:1], scale=wmean[:, 3 + mi:4 + mi]), inc="act")
                P.waitt("vector", atok)
                P.op("vector", lambda v, w=width: v.tensor_scalar(
                    out=qtmp[:, 0:w], in0=qtmp[:, 0:w], scalar1=MAGIC, scalar2=-1.0,
                    op0=OP.subtract, op1=OP.max), inc="dve")
                return P.op("vector", lambda v, w=width: v.tensor_scalar(
                    out=tern[:, 0:w], in0=qtmp[:, 0:w], scalar1=1.0, scalar2=None,
                    op0=OP.min), inc="dve")
            quant_tile.tern_free = None

            stg_free = [None, None]
            for mi, scratch in [(0, sg), (1, su)]:
                wap = d_in["wg"] if mi == 0 else d_in["wu"]
                for j in range(16):
                    ssl = j % 2
                    for half in range(2):
                        r = 2 * j + half
                        sl = r % 2
                        P.waitt("sync", wt_free[sl])
                        src = wap[r * 128:(r + 1) * 128, :]
                        ltok = P.op("sync", lambda s, sl=sl, src=src: s.dma_start(
                            out=wtile[sl][:, 0:1024], in_=src), inc=f"dx{sl}", k=16)
                        ttok = quant_tile(sl, mi, 1024, ltok)
                        wt_free[sl] = ttok
                        for k in range(8):
                            transpose_tile(
                                tern[:, k * 128:(k + 1) * 128],
                                stg_g[ssl][:, k * 256 + half * 128: k * 256 + half * 128 + 128],
                                [6, 7], ttok, pre_evac_tok=stg_free[ssl])
                        quant_tile.tern_free = P.mark("pe")
                    lastev = P.mark("dve")
                    P.waitt("sync", lastev)
                    sttok = P.op("sync", lambda g, ssl=ssl, j=j, scratch=scratch: g.dma_start(
                        out=scratch[j], in_=stg_g[ssl][:]), inc=f"dw{ssl}", k=16)
                    stg_free[ssl] = sttok
            std_free = [None, None]
            for r in range(8):
                sl = r % 2
                ssl = r % 2
                P.waitt("sync", wt_free[sl])
                src = d_in["wd"][r * 128:(r + 1) * 128, :]
                ltok = P.op("sync", lambda s, sl=sl, src=src: s.dma_start(
                    out=wtile[sl][:], in_=src), inc=f"dx{sl}", k=16)
                ttok = quant_tile(sl, 2, 4096, ltok)
                wt_free[sl] = ttok
                for kc in range(32):
                    transpose_tile(tern[:, kc * 128:(kc + 1) * 128],
                                   stg_d[ssl][:, kc * 128:(kc + 1) * 128], [6, 7], ttok,
                                   pre_evac_tok=std_free[ssl])
                quant_tile.tern_free = P.mark("pe")
                lastev = P.mark("dve")
                P.waitt("sync", lastev)
                stok = P.op("sync", lambda g, r=r, ssl=ssl: g.dma_start(
                    out=sd[r], in_=stg_d[ssl][:]), inc=f"du{ssl}", k=16)
                std_free[ssl] = stok
            P.barrier()
            bankfree.update({i: None for i in range(8)})

        def emit_ffn(call, dbg_dump=False):
            def pbar(bit=0):
                if 0 < paranoid < 256 and paranoid not in (10, 20) and (paranoid >> bit) & 1:
                    P.barrier()
            src = d_in["x"] if call in (0, 2) else x1
            dst = x1 if call in (0, 5) else y_out
            if call == 5:
                call = 0
            with ExitStack() as fs:
                FE = fs.enter_context
                xtile = [FE(nc.sbuf_tensor(f"f{call}xt{i}", [128, 1024], F32)) for i in range(4)]
                sqtr = FE(nc.sbuf_tensor(f"f{call}sqtr", [128, 1024], BF16))
                xc = [FE(nc.sbuf_tensor(f"f{call}xc{i}", [128, 1024], F32)) for i in range(4)]
                xq = [FE(nc.sbuf_tensor(f"f{call}xq{i}", [128, 1024], BF16)) for i in range(2)]
                xqT = FE(nc.sbuf_tensor(f"f{call}xqT", [128, NTJ * 8 * 128], BF16))
                prod = FE(nc.sbuf_tensor(f"f{call}prod", [128, NTJ * 4096], F32))
                qv = FE(nc.sbuf_tensor(f"f{call}qv", [128, 2048], F32))
                vq = FE(nc.sbuf_tensor(f"f{call}vq", [128, 4096], BF16))
                vqT = FE(nc.sbuf_tensor(f"f{call}vqT", [128, NTJ * 32 * 128], BF16))
                wbg = [FE(nc.sbuf_tensor(f"f{call}wbg{i}", [128, 2048], BF16)) for i in range(2)]
                wbu = [FE(nc.sbuf_tensor(f"f{call}wbu{i}", [128, 2048], BF16)) for i in range(2)]
                wbd = [FE(nc.sbuf_tensor(f"f{call}wbd{i}", [128, 4096], BF16)) for i in range(2)]
                sgb = FE(nc.sbuf_tensor(f"f{call}sgb", [128, 512], F32))
                sc = FE(nc.sbuf_tensor(f"f{call}sc", [128, NTJ * 16], F32))
                vep = FE(nc.sbuf_tensor(f"f{call}vep", [128, 2 * NTJ], F32))
                qsc = FE(nc.sbuf_tensor(f"f{call}qsc", [128, 24], F32))

                xfree = [None] * 4
                wbg_free = [None, None]
                wbu_free = [None, None]
                wbd_free = [None, None]
                xc_free = [None] * 4
                qvx_free = [None, None]
                sgb_free = [None, None]
                act_guard = None
                pe_guard = None
                SSQ, SX, MEAN, NMEAN, RMX, SXQ, GSC, USC, RSTD, T0, T1, VMX, SVQ, DSC, SCR = range(15)

                def col(tj, c):
                    return sc[:, tj * 16 + c: tj * 16 + c + 1]

                tg_order = list(reversed(range(8))) if paranoid == 20 else list(range(8))
                for tg in tg_order:
                    pbar(0)
                    row0 = tg * TG
                    xq_rdy = [None] * NTJ
                    P.waitt("vector", act_guard)
                    P.waitt("vector", pe_guard)
                    for tj in range(NTJ):
                        P.waitt("sync", xfree[tj])
                        ltok = P.op("sync", lambda s, tj=tj, r=row0 + tj * 128, src=src: s.dma_start(
                            out=xtile[tj][:], in_=src[r:r + 128, :]), inc=f"dx{tj}", k=16)
                        if paranoid == 11 and tg == 0 and tj == 0:
                            P.waitt("vector", ltok)
                            P.op("vector", lambda v: v.tensor_copy(dbg_vq[:], xtile[0][:, 0:512]), inc="dve")
                        if paranoid == 11:
                            continue
                        if dbg_dump and tg == 0 and tj == 0:
                            P.waitt("vector", ltok)
                            P.op("vector", lambda v: v.tensor_copy(dbg_vq[:], xtile[0][:, 0:512]), inc="dve")
                            for _ in range(200):
                                P.op("vector", lambda v: v.tensor_scalar(
                                    out=junkv[:], in0=junkv[:], scalar1=0.0, scalar2=None, op0=OP.mult))
                            P.op("vector", lambda v: v.tensor_copy(dbg_prod[:], xtile[0][:, 0:256]), inc="dve")
                            for _ in range(1000):
                                P.op("vector", lambda v: v.tensor_scalar(
                                    out=junkv[:], in0=junkv[:], scalar1=0.0, scalar2=None, op0=OP.mult))
                            P.op("vector", lambda v: v.tensor_copy(dbg_xq[:], xtile[0][:, 0:512]), inc="dve")
                        P.waitt("scalar", ltok)
                        stok = P.op("scalar", lambda s, tj=tj: s.activation(
                            out=sqtr[:], in_=xtile[tj][:], func=AF.Square,
                            accum_out=col(tj, SSQ)), inc="act")
                        P.waitt("vector", ltok)
                        P.op("vector", lambda v, tj=tj: v.tensor_reduce(
                            col(tj, SX), xtile[tj][:], axis=AX, op=OP.add), inc="dve")
                        P.waitt("vector", stok)
                        sepv()
                        P.op("vector", lambda v, tj=tj: v.tensor_scalar(
                            out=col(tj, MEAN), in0=col(tj, SX), scalar1=1.0 / 1024, scalar2=None,
                            op0=OP.mult), inc="dve")
                        P.op("vector", lambda v, tj=tj: v.tensor_scalar(
                            out=col(tj, T0), in0=col(tj, SSQ), scalar1=1.0 / 1024, scalar2=None,
                            op0=OP.mult), inc="dve")
                        P.op("vector", lambda v, tj=tj: v.tensor_tensor(
                            out=col(tj, T1), in0=col(tj, MEAN), in1=col(tj, MEAN), op=OP.mult), inc="dve")
                        P.op("vector", lambda v, tj=tj: v.tensor_tensor(
                            out=col(tj, T0), in0=col(tj, T0), in1=col(tj, T1), op=OP.subtract), inc="dve")
                        P.op("vector", lambda v, tj=tj: v.tensor_scalar(
                            out=qsc[:, 4 + tj:5 + tj], in0=col(tj, T0), scalar1=1e-5, scalar2=None,
                            op0=OP.add), inc="dve")
                        nmtok = P.op("vector", lambda v, tj=tj: v.tensor_scalar(
                            out=col(tj, NMEAN), in0=col(tj, MEAN), scalar1=-1.0, scalar2=None,
                            op0=OP.mult), inc="dve")
                        if dbg_dump and tg == 0 and tj == 0:
                            P.op("vector", lambda v: v.tensor_copy(dbg_sc[:, 24:40], sc[:, 0:16]), inc="dve")
                            P.op("vector", lambda v: v.tensor_copy(dbg_wg[:], xtile[0][:, 0:512]), inc="dve")
                            nmtok = P.mark("dve")
                        P.waitt("scalar", nmtok)
                        P.waitt("scalar", xc_free[tj])
                        xctok = P.op("scalar", lambda s, tj=tj: s.activation(
                            out=xc[tj][:], in_=xtile[tj][:], func=AF.Identity,
                            bias=col(tj, NMEAN), scale=1.0), inc="act")
                        P.waitt("vector", xctok)
                        P.op("vector", lambda v, tj=tj: v.tensor_reduce(
                            col(tj, RMX), xc[tj][:], axis=AX, op=OP.max,
                            apply_absolute_value=True), inc="dve")
                        sepv()
                        P.op("vector", lambda v, tj=tj: v.tensor_scalar(
                            out=qsc[:, tj:tj + 1], in0=col(tj, RMX), scalar1=1e-9, scalar2=None,
                            op0=OP.max), inc="dve")
                    pbar(1)
                    a2_in = P.mark("dve")
                    P.waitt("scalar", a2_in)
                    P.op("scalar", lambda s: s.activation(
                        out=qsc[:, 8:16], in_=qsc[:, 0:8], func=AF.Ln), inc="act")
                    P.op("scalar", lambda s: s.activation(
                        out=qsc[:, 16:20], in_=qsc[:, 8:12], func=AF.Exp,
                        scale=-1.0, bias=ln127[:, 0:1]), inc="act")
                    a2tok = P.op("scalar", lambda s: s.activation(
                        out=qsc[:, 20:24], in_=qsc[:, 12:16], func=AF.Exp,
                        scale=-0.5), inc="act")
                    P.waitt("vector", a2tok)
                    for tj in range(NTJ):
                        P.op("vector", lambda v, tj=tj: v.tensor_tensor(
                            out=col(tj, T1), in0=qsc[:, tj:tj + 1], in1=qsc[:, 16 + tj:17 + tj],
                            op=OP.mult), inc="dve")
                        P.op("vector", lambda v, tj=tj: v.tensor_scalar(
                            out=col(tj, T1), in0=col(tj, T1), scalar1=-1.0 / 127, scalar2=2.0,
                            op0=OP.mult, op1=OP.add), inc="dve")
                        sxtok = P.op("vector", lambda v, tj=tj: v.tensor_tensor(
                            out=col(tj, SXQ), in0=qsc[:, 16 + tj:17 + tj], in1=col(tj, T1),
                            op=OP.mult), inc="dve")
                        P.waitt("scalar", sxtok)
                        P.waitt("scalar", qvx_free[tj % 2])
                        q1tok = P.op("scalar", lambda s, tj=tj: s.activation(
                            out=qv[:, (tj % 2) * 1024:(tj % 2) * 1024 + 1024], in_=xc[tj][:],
                            func=AF.Identity, bias=magic[:, 0:1], scale=col(tj, SXQ)), inc="act")
                        xc_free[tj] = q1tok
                        P.waitt("vector", q1tok)
                        xq_tok = P.op("vector", lambda v, tj=tj: v.tensor_scalar(
                            out=xq[tj % 2][:], in0=qv[:, (tj % 2) * 1024:(tj % 2) * 1024 + 1024],
                            scalar1=MAGIC, scalar2=None, op0=OP.subtract), inc="dve")
                        qvx_free[tj % 2] = xq_tok
                        if dbg_dump and tg == 0 and tj == 0:
                            P.op("vector", lambda v: v.tensor_copy(dbg_xq[:], xq[0][:, 0:512]), inc="dve")
                            P.op("vector", lambda v: v.tensor_copy(dbg_sc[:, 0:24], qsc[:]), inc="dve")
                            qvx_free[0] = P.mark("dve")
                        for k in range(8):
                            xq_rdy[tj] = transpose_tile(
                                xq[tj % 2][:, k * 128:(k + 1) * 128],
                                xqT[:, (tj * 8 + k) * 128:(tj * 8 + k) * 128 + 128],
                                [6, 7], xq_tok)
                    pbar(2)
                    for tj in range(NTJ):
                        P.op("vector", lambda v, tj=tj: v.tensor_tensor(
                            out=col(tj, T1), in0=qsc[:, 20 + tj:21 + tj], in1=qsc[:, 20 + tj:21 + tj],
                            op=OP.mult), inc="dve")
                        P.op("vector", lambda v, tj=tj: v.tensor_tensor(
                            out=col(tj, T1), in0=col(tj, T1), in1=qsc[:, 4 + tj:5 + tj], op=OP.mult), inc="dve")
                        P.op("vector", lambda v, tj=tj: v.tensor_scalar(
                            out=col(tj, T1), in0=col(tj, T1), scalar1=-0.5, scalar2=1.5,
                            op0=OP.mult, op1=OP.add), inc="dve")
                        P.op("vector", lambda v, tj=tj: v.tensor_tensor(
                            out=col(tj, RSTD), in0=qsc[:, 20 + tj:21 + tj], in1=col(tj, T1), op=OP.mult), inc="dve")
                        P.op("vector", lambda v, tj=tj: v.tensor_tensor(
                            out=col(tj, T0), in0=col(tj, RMX), in1=col(tj, RSTD), op=OP.mult), inc="dve")
                        P.op("vector", lambda v, tj=tj: v.tensor_scalar(
                            out=col(tj, T0), in0=col(tj, T0), scalar1=1e-5, scalar2=None,
                            op0=OP.max), inc="dve")
                        P.op("vector", lambda v, tj=tj: v.tensor_tensor(
                            out=col(tj, GSC), in0=col(tj, T0), in1=wmean[:, 0:1], op=OP.mult), inc="dve")
                        P.op("vector", lambda v, tj=tj: v.tensor_scalar(
                            out=col(tj, GSC), in0=col(tj, GSC), scalar1=1.0 / 127, scalar2=None,
                            op0=OP.mult), inc="dve")
                        P.op("vector", lambda v, tj=tj: v.tensor_tensor(
                            out=col(tj, USC), in0=col(tj, T0), in1=wmean[:, 1:2], op=OP.mult), inc="dve")
                        P.op("vector", lambda v, tj=tj: v.tensor_scalar(
                            out=col(tj, USC), in0=col(tj, USC), scalar1=1.0 / 127, scalar2=None,
                            op0=OP.mult), inc="dve")
                    scl_tok = P.mark("dve")
                    if paranoid == 10:
                        act_guard = P.mark("act")
                        pe_guard = P.mark("pe")
                        continue
                    pbar(3)
                    prod_last = [None] * NTJ
                    mmtok = None
                    for n2 in range(16):
                        gsl = n2 % 2
                        P.waitt("sync", wbg_free[gsl])
                        gtok = P.op("sync", lambda s, gsl=gsl, n2=n2: s.dma_start(
                            out=wbg[gsl][:], in_=sg[n2]), inc=f"dw{gsl}", k=16)
                        P.waitt("sync", wbu_free[gsl])
                        utok = P.op("sync", lambda s, gsl=gsl, n2=n2: s.dma_start(
                            out=wbu[gsl][:], in_=su[n2]), inc=f"du{gsl}", k=16)
                        for tj in range(NTJ):
                            bg, bu = 2 * tj, 2 * tj + 1
                            P.waitt("tensor", bankfree[bg])
                            P.waitt("tensor", bankfree[bu])
                            P.waitt("tensor", xq_rdy[tj])
                            P.waitt("tensor", gtok)
                            P.waitt("tensor", utok)
                            for k in range(8):
                                lhs = xqT[:, (tj * 8 + k) * 128:(tj * 8 + k) * 128 + 128]
                                P.op("tensor", lambda t, lhs=lhs, bg=bg, gsl=gsl, k=k: t.matmul(
                                    PS[bg][:, 0:256], lhs, wbg[gsl][:, k * 256:(k + 1) * 256],
                                    start=(k == 0), stop=(k == 7)))
                                fn = (lambda t, lhs=lhs, bu=bu, gsl=gsl, k=k: t.matmul(
                                    PS[bu][:, 0:256], lhs, wbu[gsl][:, k * 256:(k + 1) * 256],
                                    start=(k == 0), stop=(k == 7)))
                                if k == 7:
                                    mmtok = P.op("tensor", fn, inc="pe")
                                else:
                                    P.op("tensor", fn)
                            wbg_free[gsl] = mmtok
                            wbu_free[gsl] = mmtok
                            P.waitt("scalar", mmtok)
                            P.waitt("scalar", scl_tok)
                            P.waitt("scalar", sgb_free[tj % 2])
                            sltok = P.op("scalar", lambda s, bg=bg, tj=tj: s.activation(
                                out=sgb[:, (tj % 2) * 256:(tj % 2) * 256 + 256],
                                in_=PS[bg][:, 0:256], func=AF.Silu, scale=col(tj, GSC)), inc="act")
                            P.waitt("vector", sltok)
                            ptok = P.op("vector", lambda v, bu=bu, tj=tj, n2=n2: v.scalar_tensor_tensor(
                                out=prod[:, tj * 4096 + n2 * 256: tj * 4096 + n2 * 256 + 256],
                                in0=PS[bu][:, 0:256], scalar=col(tj, USC),
                                in1=sgb[:, (tj % 2) * 256:(tj % 2) * 256 + 256],
                                op0=OP.mult, op1=OP.mult), inc="dve")
                            bankfree[bg] = ptok
                            bankfree[bu] = ptok
                            sgb_free[tj % 2] = ptok
                            prod_last[tj] = ptok
                            if dbg_dump and tg == 0 and tj == 0 and n2 == 0:
                                P.op("vector", lambda v: v.tensor_copy(dbg_prod[:], prod[:, 0:256]), inc="dve")
                                prod_last[0] = P.mark("dve")
                    pbar(4)
                    vqT_rdy = [None] * NTJ
                    qvv_free = qvx_free[0]
                    for tj in range(NTJ):
                        P.waitt("vector", prod_last[tj])
                        P.op("vector", lambda v, tj=tj: v.tensor_reduce(
                            col(tj, VMX), prod[:, tj * 4096:(tj + 1) * 4096], axis=AX,
                            op=OP.max, apply_absolute_value=True), inc="dve")
                        sepv()
                        P.op("vector", lambda v, tj=tj: v.tensor_scalar(
                            out=col(tj, VMX), in0=col(tj, VMX), scalar1=1e-5, scalar2=None,
                            op0=OP.max), inc="dve")
                        vmx_tok = P.mark("dve")
                        P.waitt("scalar", vmx_tok)
                        P.op("scalar", lambda s, tj=tj: s.activation(
                            out=col(tj, SCR), in_=col(tj, VMX), func=AF.Ln), inc="act")
                        sv0 = P.op("scalar", lambda s, tj=tj: s.activation(
                            out=col(tj, T0), in_=col(tj, SCR), func=AF.Exp,
                            scale=-1.0, bias=ln127[:, 0:1]), inc="act")
                        P.waitt("vector", sv0)
                        P.op("vector", lambda v, tj=tj: v.tensor_tensor(
                            out=col(tj, T1), in0=col(tj, VMX), in1=col(tj, T0), op=OP.mult), inc="dve")
                        P.op("vector", lambda v, tj=tj: v.tensor_scalar(
                            out=col(tj, T1), in0=col(tj, T1), scalar1=-1.0 / 127, scalar2=2.0,
                            op0=OP.mult, op1=OP.add), inc="dve")
                        svqtok = P.op("vector", lambda v, tj=tj: v.tensor_tensor(
                            out=col(tj, SVQ), in0=col(tj, T0), in1=col(tj, T1), op=OP.mult), inc="dve")
                        P.op("vector", lambda v, tj=tj: v.tensor_tensor(
                            out=col(tj, DSC), in0=col(tj, VMX), in1=wmean[:, 2:3], op=OP.mult), inc="dve")
                        P.op("vector", lambda v, tj=tj: v.tensor_scalar(
                            out=col(tj, DSC), in0=col(tj, DSC), scalar1=0.5 / 127, scalar2=None,
                            op0=OP.mult), inc="dve")
                        for h2 in range(2):
                            P.waitt("scalar", svqtok)
                            P.waitt("scalar", qvv_free)
                            P.waitt("scalar", qvx_free[1])
                            q2tok = P.op("scalar", lambda s, tj=tj, h2=h2: s.activation(
                                out=qv[:], in_=prod[:, tj * 4096 + h2 * 2048: tj * 4096 + (h2 + 1) * 2048],
                                func=AF.Identity, bias=magic[:, 0:1], scale=col(tj, SVQ)), inc="act")
                            P.waitt("vector", q2tok)
                            vq_tok = P.op("vector", lambda v, h2=h2: v.tensor_scalar(
                                out=vq[:, h2 * 2048:(h2 + 1) * 2048], in0=qv[:],
                                scalar1=MAGIC, scalar2=None, op0=OP.subtract), inc="dve")
                            qvv_free = vq_tok
                            qvx_free[0] = vq_tok
                            qvx_free[1] = vq_tok

                            for k in range(16):
                                kc = h2 * 16 + k
                                vqT_rdy[tj] = transpose_tile(
                                    vq[:, kc * 128:(kc + 1) * 128],
                                    vqT[:, (tj * 32 + kc) * 128:(tj * 32 + kc) * 128 + 128],
                                    [6, 7], vq_tok)
                    pbar(5)
                    dtok = None
                    for u8 in range(8):
                        dsl = u8 % 2
                        P.waitt("sync", wbd_free[dsl])
                        sdv = sd.rearrange("r p (kc m) -> p kc r m", kc=32)
                        for kc_ in range(4):
                            src3 = sdv[:, u8 * 4 + kc_, :, :]
                            dst3 = wbd[dsl][:, kc_ * 1024:(kc_ + 1) * 1024].rearrange(
                                "p (b m) -> p b m", b=8)
                            dltok = P.op("sync", lambda s, src3=src3, dst3=dst3: s.dma_start(
                                out=dst3, in_=src3), inc=f"dd{dsl}", k=16)
                        P.waitt("tensor", dltok)
                        if u8 == 0:
                            for tj in range(NTJ):
                                P.waitt("tensor", bankfree[2 * tj])
                                P.waitt("tensor", bankfree[2 * tj + 1])
                                P.waitt("tensor", vqT_rdy[tj])
                        for kc in range(4):
                            kca = u8 * 4 + kc
                            for tj in range(NTJ):
                                lhs = vqT[:, (tj * 32 + kca) * 128:(tj * 32 + kca) * 128 + 128]
                                P.op("tensor", lambda t, lhs=lhs, tj=tj, dsl=dsl, kc=kc, kca=kca: t.matmul(
                                    PS[2 * tj][:, 0:512], lhs, wbd[dsl][:, kc * 1024:kc * 1024 + 512],
                                    start=(kca == 0), stop=(kca == 31)))
                                fn = (lambda t, lhs=lhs, tj=tj, dsl=dsl, kc=kc, kca=kca: t.matmul(
                                    PS[2 * tj + 1][:, 0:512], lhs,
                                    wbd[dsl][:, kc * 1024 + 512:kc * 1024 + 1024],
                                    start=(kca == 0), stop=(kca == 31)))
                                if kc == 3 and tj == NTJ - 1:
                                    dtok = P.op("tensor", fn, inc="pe")
                                else:
                                    P.op("tensor", fn)
                        wbd_free[dsl] = dtok
                    dfin = P.mark("pe")
                    pbar(6)
                    for tj in range(NTJ):
                        P.waitt("vector", dfin)
                        etok = None
                        for half in range(2):
                            b = 2 * tj + half
                            etok = P.op("vector", lambda v, tj=tj, b=b, half=half: v.scalar_tensor_tensor(
                                out=xtile[tj][:, half * 512:(half + 1) * 512],
                                in0=PS[b][:, 0:512], scalar=col(tj, DSC),
                                in1=xtile[tj][:, half * 512:(half + 1) * 512],
                                op0=OP.mult, op1=OP.add), inc="dve")
                            bankfree[b] = etok
                        P.waitt("sync", etok)
                        stok = P.op("sync", lambda g, tj=tj, r=row0 + tj * 128, dst=dst: g.dma_start(
                            out=dst[r:r + 128, :], in_=xtile[tj][:]), inc=f"dx{tj}", k=16)
                        xfree[tj] = stok
                    act_guard = P.mark("act")
                    pe_guard = P.mark("pe")
            P.barrier()
            bankfree.update({i: None for i in range(8)})

        def emit_attn():
            with ExitStack() as as_:
                AE = as_.enter_context
                wlat_sb = AE(nc.sbuf_tensor("wlat_sb", [128, 8 * 768], BF16))
                wqc_sb = AE(nc.sbuf_tensor("wqc_sb", [128, 2 * 896], BF16))
                wkc_sb = AE(nc.sbuf_tensor("wkc_sb", [128, 2 * 896], BF16))
                wv_sb = AE(nc.sbuf_tensor("wv_sb", [128, 2 * 1024], BF16))
                blat = AE(nc.sbuf_tensor("blat", [128, 6], F32))
                bqcp = AE(nc.sbuf_tensor("bqcp", [128, 7], F32))
                bkcp = AE(nc.sbuf_tensor("bkcp", [128, 7], F32))
                bvb = AE(nc.sbuf_tensor("bvb", [128, 1024], F32))
                bob = AE(nc.sbuf_tensor("bob", [128, 1024], F32))
                wstage = AE(nc.sbuf_tensor("wstage", [128, 1024], F32))

                def load_cast(dst_sb, hbm, nch, width):
                    last = None
                    for k in range(nch):
                        src = rowtiles(hbm, k, k + 1, width)
                        lt = P.op("sync", lambda s, src=src, w=width: s.dma_start(
                            out=sb3(wstage[:, 0:w], 1, w), in_=src), inc="dm", k=16)
                        P.waitt("vector", lt)
                        last = P.op("vector", lambda v, dst=dst_sb, k=k, w=width: v.tensor_copy(
                            dst[:, k * w:(k + 1) * w], wstage[:, 0:w]), inc="dve")
                        P.waitt("sync", last)
                    return last

                load_cast(wlat_sb, d_in["wlat"], 8, 768)
                load_cast(wqc_sb, d_in["wqc"], 2, 896)
                load_cast(wkc_sb, d_in["wkc"], 2, 896)
                load_cast(wv_sb, d_in["wv"], 2, 1024)
                for dstb, nm in [(blat, "blat_p"), (bqcp, "bqc_pp"), (bkcp, "bkc_pp"),
                                 (bvb, "bv_b"), (bob, "bo_b")]:
                    P.op("sync", lambda s, dstb=dstb, nm=nm: s.dma_start(
                        out=dstb[:], in_=d_in[nm]), inc="dm", k=16)
                wload_ld = P.mark("dm")
                wload_cast = P.mark("dve")

                def do_batch(b):
                    brow = b * S
                    with ExitStack() as bs:
                        BE = bs.enter_context
                        QT = BE(nc.sbuf_tensor(f"QT{b}", [128, 8 * 2048], BF16))
                        KT = BE(nc.sbuf_tensor(f"KT{b}", [128, 8 * 2048], BF16))
                        VB2 = BE(nc.sbuf_tensor(f"VB{b}", [128, 16 * 16 * 65], BF16))
                        VB2r = VB2.rearrange("p (k c) -> p k c", c=65)
                        P.op("gpsimd", lambda g: g.memset(VB2r[:, :, 64:65], 1.0))

                        def do_scope_a():
                          with ExitStack() as sa:
                              SA = sa.enter_context
                              xtile = [SA(nc.sbuf_tensor(f"a{b}xt{i}", [128, 1024], F32)) for i in range(4)]
                              sqtr = SA(nc.sbuf_tensor(f"a{b}sqtr", [128, 1024], BF16))
                              xn = [SA(nc.sbuf_tensor(f"a{b}xn{i}", [128, 1024], F32)) for i in range(2)]
                              xlnT = SA(nc.sbuf_tensor(f"a{b}xlnT", [128, 8 * 512], BF16))
                              cqg = SA(nc.sbuf_tensor(f"a{b}cqg", [128, 2 * 2048], BF16))
                              ckvg = SA(nc.sbuf_tensor(f"a{b}ckvg", [128, 2 * 2048], BF16))
                              cosb = SA(nc.sbuf_tensor(f"a{b}cosb", [128, 512], F32))
                              sinb = SA(nc.sbuf_tensor(f"a{b}sinb", [128, 512], F32))
                              rotb = SA(nc.sbuf_tensor(f"a{b}rotb", [128, 512], F32))
                              rawq = SA(nc.sbuf_tensor(f"a{b}rawq", [128, 512], F32))
                              rawk = SA(nc.sbuf_tensor(f"a{b}rawk", [128, 512], F32))
                              sc = SA(nc.sbuf_tensor(f"a{b}sc", [128, 64], F32))
                              vep = SA(nc.sbuf_tensor(f"a{b}vep", [128, 12], F32))

                              SSQ, SX, MEAN, NMEAN, RSTD, NMR, T0, T1 = range(8)

                              def acol(tj, c):
                                  return sc[:, tj * 8 + c: tj * 8 + c + 1]

                              lat_free = {m: None for m in range(6)}
                              lat_prev = None
                              xn_free = [None, None]
                              raw_free = [None, None]
                              rot_free = None
                              tab_free = None
                              a_act_guard = None
                              for tg in range(4):
                                  P.waitt("vector", a_act_guard)
                                  for tj in range(4):
                                      r = brow + tg * 512 + tj * 128
                                      ltok = P.op("sync", lambda s, tj=tj, r=r: s.dma_start(
                                          out=xtile[tj][:], in_=x1[r:r + 128, :]), inc=f"dx{tj}", k=16)
                                      P.waitt("scalar", ltok)
                                      stok = P.op("scalar", lambda s, tj=tj: s.activation(
                                          out=sqtr[:], in_=xtile[tj][:], func=AF.Square,
                                          accum_out=acol(tj, SSQ)), inc="act")
                                      P.waitt("vector", ltok)
                                      P.op("vector", lambda v, tj=tj: v.tensor_reduce(
                                          acol(tj, SX), xtile[tj][:], axis=AX, op=OP.add), inc="dve")
                                      P.waitt("vector", stok)
                                      sepv()
                                      P.op("vector", lambda v, tj=tj: v.tensor_scalar(
                                          out=acol(tj, MEAN), in0=acol(tj, SX), scalar1=1.0 / 1024,
                                          scalar2=None, op0=OP.mult), inc="dve")
                                      P.op("vector", lambda v, tj=tj: v.tensor_scalar(
                                          out=acol(tj, T0), in0=acol(tj, SSQ), scalar1=1.0 / 1024,
                                          scalar2=None, op0=OP.mult), inc="dve")
                                      P.op("vector", lambda v, tj=tj: v.tensor_tensor(
                                          out=acol(tj, T1), in0=acol(tj, MEAN), in1=acol(tj, MEAN),
                                          op=OP.mult), inc="dve")
                                      P.op("vector", lambda v, tj=tj: v.tensor_tensor(
                                          out=acol(tj, T0), in0=acol(tj, T0), in1=acol(tj, T1),
                                          op=OP.subtract), inc="dve")
                                      P.op("vector", lambda v, tj=tj: v.tensor_scalar(
                                          out=vep[:, tj:tj + 1], in0=acol(tj, T0), scalar1=1e-5,
                                          scalar2=None, op0=OP.add), inc="dve")
                                  sq_in = P.mark("dve")
                                  P.waitt("scalar", sq_in)
                                  P.op("scalar", lambda s: s.activation(
                                      out=vep[:, 4:8], in_=vep[:, 0:4], func=AF.Ln), inc="act")
                                  sqtok = P.op("scalar", lambda s: s.activation(
                                      out=vep[:, 8:12], in_=vep[:, 4:8], func=AF.Exp,
                                      scale=-0.5), inc="act")
                                  P.waitt("vector", sqtok)
                                  for tj in range(4):
                                      P.op("vector", lambda v, tj=tj: v.tensor_tensor(
                                          out=acol(tj, T1), in0=vep[:, 8 + tj:9 + tj], in1=vep[:, 8 + tj:9 + tj],
                                          op=OP.mult), inc="dve")
                                      P.op("vector", lambda v, tj=tj: v.tensor_tensor(
                                          out=acol(tj, T1), in0=acol(tj, T1), in1=vep[:, tj:tj + 1],
                                          op=OP.mult), inc="dve")
                                      P.op("vector", lambda v, tj=tj: v.tensor_scalar(
                                          out=acol(tj, T1), in0=acol(tj, T1), scalar1=-0.5, scalar2=1.5,
                                          op0=OP.mult, op1=OP.add), inc="dve")
                                      P.op("vector", lambda v, tj=tj: v.tensor_tensor(
                                          out=acol(tj, RSTD), in0=vep[:, 8 + tj:9 + tj], in1=acol(tj, T1),
                                          op=OP.mult), inc="dve")
                                      P.op("vector", lambda v, tj=tj: v.tensor_tensor(
                                          out=acol(tj, T1), in0=acol(tj, MEAN), in1=acol(tj, RSTD),
                                          op=OP.mult), inc="dve")
                                      nmrtok = P.op("vector", lambda v, tj=tj: v.tensor_scalar(
                                          out=acol(tj, NMR), in0=acol(tj, T1), scalar1=-1.0,
                                          scalar2=None, op0=OP.mult), inc="dve")
                                      P.waitt("scalar", nmrtok)
                                      P.waitt("scalar", xn_free[tj % 2])
                                      xntok = P.op("scalar", lambda s, tj=tj: s.activation(
                                          out=xn[tj % 2][:], in_=xtile[tj][:], func=AF.Identity,
                                          bias=acol(tj, NMR), scale=acol(tj, RSTD)), inc="act")
                                      P.waitt("vector", lat_prev)
                                      for k in range(8):
                                          xn_free[tj % 2] = transpose_tile(
                                              xn[tj % 2][:, k * 128:(k + 1) * 128],
                                              xlnT[:, k * 512 + tj * 128: k * 512 + tj * 128 + 128],
                                              [6, 7], xntok)
                                  xlnT_tok = P.mark("dve")
                                  P.waitt("tensor", xlnT_tok)
                                  P.wait("tensor", "dm", wload_ld[1])
                                  P.waitt("tensor", wload_cast)
                                  for m in range(6):
                                      P.waitt("tensor", lat_free[m])
                                  lattok = None
                                  for k in range(8):
                                      for m in range(6):
                                          lhs = wlat_sb[:, k * 768 + m * 128: k * 768 + m * 128 + 128]
                                          fn = (lambda t, lhs=lhs, m=m, k=k: t.matmul(
                                              PS[m][:, 0:512], lhs, xlnT[:, k * 512:(k + 1) * 512],
                                              start=(k == 0), stop=(k == 7)))
                                          if k == 7 and m == 5:
                                              lattok = P.op("tensor", fn, inc="pe")
                                          else:
                                              P.op("tensor", fn)
                                  lat_prev = lattok
                                  P.waitt("scalar", lattok)
                                  for m in range(4):
                                      dstb = cqg if m < 2 else ckvg
                                      evt = P.op("scalar", lambda s, m=m, dstb=dstb, tg=tg: s.activation(
                                          out=dstb[:, (m % 2) * 2048 + tg * 512:(m % 2) * 2048 + tg * 512 + 512],
                                          in_=PS[m][:, 0:512], func=AF.Gelu,
                                          bias=blat[:, m:m + 1], scale=1.0), inc="act")
                                      lat_free[m] = evt
                                  P.waitt("scalar", raw_free[0])
                                  r4 = P.op("scalar", lambda s: s.activation(
                                      out=rawq[:], in_=PS[4][:, 0:512], func=AF.Identity,
                                      bias=blat[:, 4:5], scale=1.0), inc="act")
                                  lat_free[4] = r4
                                  P.waitt("scalar", raw_free[1])
                                  r5 = P.op("scalar", lambda s: s.activation(
                                      out=rawk[:], in_=PS[5][:, 0:512], func=AF.Identity,
                                      bias=blat[:, 5:6], scale=1.0), inc="act")
                                  lat_free[5] = r5
                                  P.waitt("sync", tab_free)
                                  P.op("sync", lambda s, tg=tg: s.dma_start(
                                      out=cosb[:], in_=d_in["cosT"][:, tg * 512:(tg + 1) * 512]), inc="dc", k=16)
                                  ctok = P.op("sync", lambda s, tg=tg: s.dma_start(
                                      out=sinb[:], in_=d_in["sinT"][:, tg * 512:(tg + 1) * 512]), inc="dc", k=16)
                                  for idx, (raw, rtok, dstT) in enumerate(((rawq, r4, QT), (rawk, r5, KT))):
                                      P.waitt("scalar", rtok)
                                      P.waitt("scalar", rot_free)
                                      P.op("scalar", lambda g, raw=raw: g.dma_start(
                                          out=rotb[0:64, :], in_=raw[64:128, :]), inc="dr", k=16)
                                      shtok = P.op("scalar", lambda g, raw=raw: g.dma_start(
                                          out=rotb[64:128, :], in_=raw[0:64, :]), inc="dr", k=16)
                                      P.waitt("vector", shtok)
                                      P.op("vector", lambda v: v.tensor_scalar(
                                          out=rotb[0:64, :], in0=rotb[0:64, :], scalar1=-1.0,
                                          scalar2=None, op0=OP.mult), inc="dve")
                                      P.waitt("vector", ctok)
                                      P.op("vector", lambda v: v.tensor_tensor(
                                          out=rotb[:], in0=rotb[:], in1=sinb[:], op=OP.mult), inc="dve")
                                      P.op("vector", lambda v, raw=raw: v.tensor_tensor(
                                          out=raw[:], in0=raw[:], in1=cosb[:], op=OP.mult), inc="dve")
                                      addtok = P.op("vector", lambda v, raw=raw: v.tensor_tensor(
                                          out=raw[:], in0=raw[:], in1=rotb[:], op=OP.add), inc="dve")
                                      raw_free[idx] = addtok
                                      rot_free = addtok
                                      P.waitt("scalar", addtok)
                                      gtok = P.op("scalar", lambda s, raw=raw, dstT=dstT, tg=tg: s.activation(
                                          out=dstT[:, 7 * 2048 + tg * 512: 7 * 2048 + tg * 512 + 512],
                                          in_=raw[:], func=AF.Gelu), inc="act")
                                      tab_free = gtok
                                  a_act_guard = P.mark("act")
                                  P.wait("sync", "act", a_act_guard[1])

                              if paranoid == 30 and b == 0:
                                  cg0 = P.mark("act")
                                  P.waitt("vector", cg0)
                                  dq2 = P.op("vector", lambda v: v.tensor_copy(dbg_vq[:, 0:128], cqg[:, 0:128]), inc="dve")
                                  P.waitt("sync", dq2)
                                  P.op("sync", lambda s_: s_.dma_start(out=y_out[1664:1792, 0:128], in_=dbg_vq[:, 0:128]), inc="dm", k=16)
                              qkv_guard = P.mark("act")
                              P.waitt("tensor", qkv_guard)
                              for (wsb, bpp, dstT, srcT) in ((wqc_sb, bqcp, QT, cqg),
                                                             (wkc_sb, bkcp, KT, ckvg)):
                                  for j in range(7):
                                      for tg in range(4):
                                          bk = tg % 2
                                          P.waitt("tensor", bankfree[bk])
                                          mm = None
                                          for k in range(2):
                                              lhs = wsb[:, k * 896 + j * 128: j * 128 + k * 896 + 128]
                                              fn = (lambda t, lhs=lhs, k=k, bk=bk, tg=tg, srcT=srcT: t.matmul(
                                                  PS[bk][:, 0:512], lhs,
                                                  srcT[:, k * 2048 + tg * 512: k * 2048 + tg * 512 + 512],
                                                  start=(k == 0), stop=(k == 1)))
                                              if k == 1:
                                                  mm = P.op("tensor", fn, inc="pe")
                                              else:
                                                  P.op("tensor", fn)
                                          P.waitt("scalar", mm)
                                          ev = P.op("scalar", lambda s, bk=bk, j=j, tg=tg, bpp=bpp, dstT=dstT: s.activation(
                                              out=dstT[:, j * 2048 + tg * 512: j * 2048 + tg * 512 + 512],
                                              in_=PS[bk][:, 0:512], func=AF.Gelu,
                                              bias=bpp[:, j:j + 1], scale=1.0), inc="act")
                                          bankfree[bk] = ev
                              for tj in range(16):
                                  b0, b1 = 2, 3
                                  P.waitt("tensor", bankfree[b0])
                                  P.waitt("tensor", bankfree[b1])
                                  mm = None
                                  for k in range(2):
                                      lhs = ckvg[:, k * 2048 + tj * 128: k * 2048 + tj * 128 + 128]
                                      P.op("tensor", lambda t, lhs=lhs, k=k, b0=b0: t.matmul(
                                          PS[b0][:, 0:512], lhs, wv_sb[:, k * 1024:k * 1024 + 512],
                                          start=(k == 0), stop=(k == 1)))
                                      fn = (lambda t, lhs=lhs, k=k, b1=b1: t.matmul(
                                          PS[b1][:, 0:512], lhs, wv_sb[:, k * 1024 + 512:(k + 1) * 1024],
                                          start=(k == 0), stop=(k == 1)))
                                      if k == 1:
                                          mm = P.op("tensor", fn, inc="pe")
                                      else:
                                          P.op("tensor", fn)
                                  P.waitt("vector", mm)
                                  e0 = P.op("vector", lambda v, tj=tj, b0=b0: v.tensor_tensor(
                                      out=VB2r[:, tj * 16: tj * 16 + 8, 0:64],
                                      in0=PS[b0][:, 0:512].rearrange("p (h c) -> p h c", c=64),
                                      in1=bvb[:, 0:512].rearrange("p (h c) -> p h c", c=64),
                                      op=OP.add), inc="dve")
                                  e1 = P.op("vector", lambda v, tj=tj, b1=b1: v.tensor_tensor(
                                      out=VB2r[:, tj * 16 + 8: tj * 16 + 16, 0:64],
                                      in0=PS[b1][:, 0:512].rearrange("p (h c) -> p h c", c=64),
                                      in1=bvb[:, 512:1024].rearrange("p (h c) -> p h c", c=64),
                                      op=OP.add), inc="dve")
                                  bankfree[b0] = e0
                                  bankfree[b1] = e1

                        do_scope_a()
                        P.barrier()
                        bankfree.update({i: None for i in range(8)})

                        def do_scope_b():
                          with ExitStack() as sb:
                              SB = sb.enter_context
                              wo_sb = SB(nc.sbuf_tensor(f"wo_sb{b}", [128, 8 * 1024], BF16))
                              OT = SB(nc.sbuf_tensor(f"OT{b}", [128, 8 * 2048], BF16))
                              xtile = [SB(nc.sbuf_tensor(f"b{b}xt{i}", [128, 1024], F32)) for i in range(2)]
                              scb = SB(nc.sbuf_tensor(f"b{b}sc", [128, 16], F32))

                              wo_tok = None
                              for k in range(8):
                                  src = rowtiles(d_in["wo"], k, k + 1, 1024)
                                  lt = P.op("sync", lambda s, src=src: s.dma_start(
                                      out=sb3(wstage[:, 0:1024], 1, 1024), in_=src), inc="dm", k=16)
                                  P.waitt("vector", lt)
                                  wo_tok = P.op("vector", lambda v, k=k: v.tensor_copy(
                                      wo_sb[:, k * 1024:(k + 1) * 1024], wstage[:, 0:1024]), inc="dve")
                                  P.waitt("sync", wo_tok)

                              ebf = [SB(nc.sbuf_tensor(f"eb{b}{i}", [128, 512], BF16)) for i in range(3)]
                              Oq = [SB(nc.sbuf_tensor(f"Oq{b}{i}", [128, 64], BF16)) for i in range(2)]
                              OT7 = PS[7][:].bitcast(BF16)
                              ebf_free = [None, None, None]
                              Oq_free = [None, None]
                              for h in range(16):
                                  hr0 = (h % 2) * 64
                                  hch = h // 2
                                  for g4 in range(4):
                                      q0 = g4 * 512
                                      nkc = 4 * g4 + 4
                                      rhsQ = QT[hr0:hr0 + 64, hch * 2048 + q0: hch * 2048 + q0 + 512]
                                      pvtok = [None] * 4
                                      extok = [None] * nkc

                                      def emit_st(kc):
                                          sb_ = (0, 1, 6)[kc % 3]
                                          a = kc - 4 * g4
                                          P.waitt("tensor", bankfree[sb_])
                                          lhsK = KT[hr0:hr0 + 64, hch * 2048 + kc * 128: hch * 2048 + kc * 128 + 128]
                                          if a < 0:
                                              mm = P.op("tensor", lambda t, lhsK=lhsK, rhsQ=rhsQ, sb_=sb_: t.matmul(
                                                  PS[sb_][:, 0:512], lhsK, rhsQ, start=True, stop=True), inc="pe")
                                          else:
                                              P.op("tensor", lambda t, lhsK=lhsK, rhsQ=rhsQ, sb_=sb_: t.matmul(
                                                  PS[sb_][:, 0:512], lhsK, rhsQ, start=True, stop=False))
                                              mm = P.op("tensor", lambda t, sb_=sb_, a=a: t.matmul(
                                                  PS[sb_][:, 0:512], identb[:], maskTw[a][:, 0:512],
                                                  start=False, stop=True), inc="pe")
                                          P.waitt("scalar", mm)
                                          P.waitt("scalar", ebf_free[kc % 3])
                                          ex = P.op("scalar", lambda s, sb_=sb_, kc=kc: s.activation(
                                              out=ebf[kc % 3][:], in_=PS[sb_][:, 0:512], func=AF.Exp,
                                              scale=0.125), inc="act")
                                          bankfree[sb_] = ex
                                          extok[kc] = ex

                                      def emit_pv(kc):
                                          sb_ = kc % 3
                                          for j in range(4):
                                              if j == 0:
                                                  P.waitt("tensor", extok[kc])
                                              if kc == 0:
                                                  P.waitt("tensor", bankfree[2 + j])
                                              rhsV = VB2r[:, kc * 16 + h, :]
                                              pvtok[j] = P.op("tensor", lambda t, sb_=sb_, j=j, rhsV=rhsV, kc=kc, nkc=nkc: t.matmul(
                                                  PS[2 + j][:, 0:65], ebf[sb_][:, j * 128:(j + 1) * 128], rhsV,
                                                  start=(kc == 0), stop=(kc == nkc - 1)), inc="pe")
                                          ebf_free[sb_] = pvtok[3]

                                      emit_st(0)
                                      if nkc > 1:
                                          emit_st(1)
                                      for kc in range(2, nkc):
                                          emit_st(kc)
                                          emit_pv(kc - 2)
                                      emit_pv(nkc - 2) if nkc > 1 else None
                                      emit_pv(nkc - 1)
                                      for j in range(4):
                                          qt = 4 * g4 + j
                                          P.waitt("scalar", pvtok[j])
                                          P.op("scalar", lambda s, j=j: s.activation(
                                              out=scb[:, 8 + j:9 + j], in_=PS[2 + j][:, 64:65],
                                              func=AF.Ln), inc="act")
                                          li = P.op("scalar", lambda s, j=j: s.activation(
                                              out=scb[:, 12 + j:13 + j], in_=scb[:, 8 + j:9 + j],
                                              func=AF.Exp, scale=-1.0), inc="act")
                                          seps()
                                          P.waitt("vector", li)
                                          P.waitt("vector", Oq_free[j % 2])
                                          nr = P.op("vector", lambda v, j=j: v.tensor_scalar(
                                              out=Oq[j % 2][:], in0=PS[2 + j][:, 0:64],
                                              scalar1=scb[:, 12 + j:13 + j], scalar2=None,
                                              op0=OP.mult), inc="dve")
                                          bankfree[2 + j] = nr
                                          ob = 7
                                          obv = OT7
                                          P.waitt("tensor", bankfree[ob])
                                          P.waitt("tensor", nr)
                                          tt = P.op("tensor", lambda t, obv=obv, j=j, hr0=hr0: t.transpose(
                                              obv[hr0:hr0 + 64, 0:128], Oq[j % 2][:], identb[:]), inc="pe")
                                          Oq_free[j % 2] = tt
                                          if j % 2 == 0:
                                              P.waitt("scalar", tt)
                                              oev = P.op("scalar", lambda s, obv=obv, hr0=hr0, hch=hch, qt=qt: s.activation(
                                                  out=OT[hr0:hr0 + 64, hch * 2048 + qt * 128: hch * 2048 + qt * 128 + 128],
                                                  in_=obv[hr0:hr0 + 64, 0:128], func=AF.Copy), inc="act")
                                          else:
                                              P.waitt("vector", tt)
                                              oev = P.op("vector", lambda v, obv=obv, hr0=hr0, hch=hch, qt=qt: v.tensor_copy(
                                                  OT[hr0:hr0 + 64, hch * 2048 + qt * 128: hch * 2048 + qt * 128 + 128],
                                                  obv[hr0:hr0 + 64, 0:128]), inc="dve")
                                          bankfree[ob] = oev
                              o_done = P.mark("act")
                              if paranoid == 30 and b == 0:
                                  P.waitt("vector", o_done)
                                  P.waitt("vector", P.mark("dm"))
                                  do_ = P.op("vector", lambda v: v.tensor_copy(dbg_xq[:, 0:256], OT[:, 0:256]), inc="dve")
                                  P.waitt("sync", do_)
                                  P.op("sync", lambda s_: s_.dma_start(out=y_out[1792:1920, 0:256], in_=dbg_xq[:, 0:256]), inc="dm", k=16)
                                  do2 = P.op("vector", lambda v: v.tensor_copy(dbg_vq[:, 0:128], OT[:, 7 * 2048:7 * 2048 + 128]), inc="dve")
                                  P.waitt("sync", do2)
                                  P.op("sync", lambda s_: s_.dma_start(out=y_out[1920:2048, 0:128], in_=dbg_vq[:, 0:128]), inc="dm", k=16)
                              xf = [None, None]
                              for tj in range(16):
                                  r = brow + tj * 128
                                  sl = tj % 2
                                  P.waitt("sync", xf[sl])
                                  lt = P.op("sync", lambda s, sl=sl, r=r: s.dma_start(
                                      out=xtile[sl][:], in_=x1[r:r + 128, :]), inc="dm", k=16)
                                  P.waitt("vector", lt)
                                  botok = P.op("vector", lambda v, sl=sl: v.tensor_tensor(
                                      out=xtile[sl][:], in0=xtile[sl][:], in1=bob[:], op=OP.add), inc="dve")
                                  bk0, bk1 = 2 * (tj % 2), 2 * (tj % 2) + 1
                                  P.waitt("tensor", bankfree[bk0])
                                  P.waitt("tensor", bankfree[bk1])
                                  P.waitt("tensor", o_done)
                                  P.waitt("tensor", wo_tok)
                                  mm = None
                                  for k in range(8):
                                      lhs = OT[:, k * 2048 + tj * 128: k * 2048 + tj * 128 + 128]
                                      P.op("tensor", lambda t, lhs=lhs, k=k, bk0=bk0: t.matmul(
                                          PS[bk0][:, 0:512], lhs, wo_sb[:, k * 1024:k * 1024 + 512],
                                          start=(k == 0), stop=(k == 7)))
                                      fn = (lambda t, lhs=lhs, k=k, bk1=bk1: t.matmul(
                                          PS[bk1][:, 0:512], lhs, wo_sb[:, k * 1024 + 512:(k + 1) * 1024],
                                          start=(k == 0), stop=(k == 7)))
                                      if k == 7:
                                          mm = P.op("tensor", fn, inc="pe")
                                      else:
                                          P.op("tensor", fn)
                                  P.waitt("vector", mm)
                                  P.waitt("vector", botok)
                                  P.op("vector", lambda v, sl=sl, bk0=bk0: v.tensor_tensor(
                                      out=xtile[sl][:, 0:512], in0=PS[bk0][:, 0:512],
                                      in1=xtile[sl][:, 0:512], op=OP.add), inc="dve")
                                  ev = P.op("vector", lambda v, sl=sl, bk1=bk1: v.tensor_tensor(
                                      out=xtile[sl][:, 512:1024], in0=PS[bk1][:, 0:512],
                                      in1=xtile[sl][:, 512:1024], op=OP.add), inc="dve")
                                  bankfree[bk0] = ev
                                  bankfree[bk1] = ev
                                  if paranoid == 30 and b == 0 and tj == 0:
                                      P.waitt("vector", ev)
                                      P.waitt("vector", P.mark("dm"))
                                      dw_ = P.op("vector", lambda v, sl=sl: v.tensor_copy(dbg_xq[:, 0:256], xtile[sl][:, 0:256]), inc="dve")
                                      P.waitt("sync", dw_)
                                      P.op("sync", lambda s_: s_.dma_start(out=y_out[2048:2176, 0:256], in_=dbg_xq[:, 0:256]), inc="dm", k=16)
                                  P.waitt("sync", ev)
                                  stok = P.op("sync", lambda g, sl=sl, r=r: g.dma_start(
                                      out=x1[r:r + 128, :], in_=xtile[sl][:]), inc=f"dx{sl}", k=16)
                                  xf[sl] = stok

                        do_scope_b()
                        P.barrier()
                        bankfree.update({i: None for i in range(8)})


                for b in range(2):
                    do_batch(b)

        def emit_dump_prep():
            with ExitStack() as cs:
                ctb = cs.enter_context(nc.sbuf_tensor("dmpb", [128, 2048], BF16))
                ctf = cs.enter_context(nc.sbuf_tensor("dmpf", [128, 2048], F32))
                free = None
                def dump(srcap, row):
                    nonlocal free
                    lt = P.op("sync", lambda s, srcap=srcap: s.dma_start(
                        out=ctb[:], in_=srcap), inc="dm", k=16)
                    P.waitt("vector", lt)
                    P.waitt("vector", free)
                    ct2 = P.op("vector", lambda v: v.tensor_copy(ctf[:], ctb[:]), inc="dve")
                    P.waitt("gpsimd", ct2)
                    f1 = P.op("gpsimd", lambda g, row=row: g.dma_start(
                        out=y_out[row:row + 128, :], in_=ctf[:, 0:1024]), inc="dm", k=16)
                    f2 = P.op("gpsimd", lambda g, row=row: g.dma_start(
                        out=y_out[row + 128:row + 256, :], in_=ctf[:, 1024:2048]), inc="dm", k=16)
                    free = f2
                    P.wait("sync", "dm", f2[1])
                dump(sg[0], 0)
                dump(sg[15], 256)
                dump(su[0], 512)
                dump(sd[0][:, 0:2048], 768)
                dump(sd[7][:, 2048:4096], 1024)
                wv_ = P.op("gpsimd", lambda g: g.dma_start(
                    out=y_out[1280:1408, 0:8], in_=wmean[:]), inc="dm", k=16)
                wv2 = P.op("gpsimd", lambda g: g.dma_start(
                    out=y_out[1408:1536, 0:16], in_=dbg[:]), inc="dm", k=16)
            P.barrier()

        def emit_copy_out():
            with ExitStack() as cs:
                ct = cs.enter_context(nc.sbuf_tensor("cpyt", [128, 1024], F32))
                free = None
                for tj in range(32):
                    P.waitt("sync", free)
                    lt = P.op("sync", lambda s, tj=tj: s.dma_start(
                        out=ct[:], in_=x1[tj * 128:(tj + 1) * 128, :]), inc="dm", k=16)
                    P.waitt("sync", lt)
                    free = P.op("sync", lambda g, tj=tj: g.dma_start(
                        out=y_out[tj * 128:(tj + 1) * 128, :], in_=ct[:]), inc="dm", k=16)
            P.barrier()

        if stage == 9:
            with ExitStack() as ms:
                mt = ms.enter_context(nc.sbuf_tensor("rb9", [128, 2048], F32))
                mb = ms.enter_context(nc.sbuf_tensor("rb9b", [128, 2048], BF16))
                for rep in range(8):
                    lt = P.op("sync", lambda s: s.dma_start(out=mb[:], in_=sd[7][:, 0:2048]), inc="dm", k=16)
                    P.waitt("vector", lt)
                    c1 = P.op("vector", lambda v: v.tensor_copy(mt[:], mb[:]), inc="dve")
                    P.waitt("scalar", c1)
                    P.op("scalar", lambda g, rep=rep: g.dma_start(
                        out=y_out[rep * 256:rep * 256 + 128, :], in_=mt[:, 0:1024]), inc="dm", k=16)
                    st9t = P.op("scalar", lambda g, rep=rep: g.dma_start(
                        out=y_out[rep * 256 + 128:rep * 256 + 256, :], in_=mt[:, 1024:2048]), inc="dm", k=16)
                    P.wait("sync", "dm", st9t[1])
                    P.op("sync", lambda s: s.nop(cycle_cnt=50000))
            P.barrier()
            with nc.Block() as block:
                block.tensor(lambda e: [f(e) for f in P.stream["tensor"]])
                block.vector(lambda e: [f(e) for f in P.stream["vector"]])
                block.scalar(lambda e: [f(e) for f in P.stream["scalar"]])
                block.gpsimd(lambda e: [f(e) for f in P.stream["gpsimd"]])
                block.sync(lambda e: [f(e) for f in P.stream["sync"]])
            return nc
        if stage == 7:
            lvl = paranoid
            with ExitStack() as ms:
                xt4 = [ms.enter_context(nc.sbuf_tensor(f"m7x{i}", [128, 1024], F32)) for i in range(4)]
                sq7 = ms.enter_context(nc.sbuf_tensor("m7sq", [128, 1024], BF16))
                sc7 = ms.enter_context(nc.sbuf_tensor("m7sc", [128, 16], F32))
                xc7 = [ms.enter_context(nc.sbuf_tensor(f"m7c{i}", [128, 1024], F32)) for i in range(2)]
                dmp = ms.enter_context(nc.sbuf_tensor("m7d", [128, 1024], F32))
                dmp2 = ms.enter_context(nc.sbuf_tensor("m7d2", [128, 16], F32))
                wdum = ms.enter_context(nc.sbuf_tensor("m7w", [128, 4096], BF16))
                snap = None
                for tj in range(4):
                    lt = P.op("sync", lambda s, tj=tj: s.dma_start(
                        out=xt4[tj][:], in_=d_in["x"][tj * 128:(tj + 1) * 128, :]), inc="dm", k=16)
                    if lvl >= 3:
                        for w in range(4):
                            P.op("sync", lambda s, tj=tj, w=w: s.dma_start(
                                out=wdum[:, 0:2048], in_=sg[(4 * tj + w) % 16]), inc="dm", k=16)
                    P.waitt("scalar", lt)
                    stok = P.op("scalar", lambda s, tj=tj: s.activation(
                        out=sq7[:], in_=xt4[tj][:], func=AF.Square,
                        accum_out=sc7[:, tj * 4:tj * 4 + 1]), inc="act")
                    P.waitt("vector", lt)
                    P.op("vector", lambda v, tj=tj: v.tensor_reduce(
                        sc7[:, tj * 4 + 1:tj * 4 + 2], xt4[tj][:], axis=AX, op=OP.add), inc="dve")
                    if lvl >= 2:
                        P.waitt("vector", stok)
                        sepv()
                        P.op("vector", lambda v, tj=tj: v.tensor_scalar(
                            out=sc7[:, tj * 4 + 2:tj * 4 + 3], in0=sc7[:, tj * 4 + 1:tj * 4 + 2],
                            scalar1=-1.0 / 1024, scalar2=None, op0=OP.mult), inc="dve")
                        P.waitt("scalar", P.mark("dve"))
                        P.op("scalar", lambda s, tj=tj: s.activation(
                            out=xc7[tj % 2][:], in_=xt4[tj][:], func=AF.Identity,
                            bias=sc7[:, tj * 4 + 2:tj * 4 + 3], scale=1.0), inc="act")
                    if tj == 0:
                        P.op("vector", lambda v: v.tensor_copy(dmp[:], xt4[0][:]), inc="dve")
                        snap = P.op("vector", lambda v: v.tensor_copy(dmp2[:], sc7[:]), inc="dve")
                P.waitt("gpsimd", snap)
                P.wait("gpsimd", "dve")
                P.op("gpsimd", lambda g: g.dma_start(out=y_out[0:128, :], in_=dmp[:]), inc="dm", k=16)
                P.op("gpsimd", lambda g: g.dma_start(out=y_out[128:256, 0:16], in_=dmp2[:]), inc="dm", k=16)
                P.op("gpsimd", lambda g: g.dma_start(out=y_out[256:384, :], in_=xt4[0][:]), inc="dm", k=16)
                P.wait("gpsimd", "dm")
            P.barrier()
            with nc.Block() as block:
                block.tensor(lambda e: [f(e) for f in P.stream["tensor"]])
                block.vector(lambda e: [f(e) for f in P.stream["vector"]])
                block.scalar(lambda e: [f(e) for f in P.stream["scalar"]])
                block.gpsimd(lambda e: [f(e) for f in P.stream["gpsimd"]])
                block.sync(lambda e: [f(e) for f in P.stream["sync"]])
            return nc
        if stage == 6:
            with ExitStack() as ms:
                mt = ms.enter_context(nc.sbuf_tensor("minitile", [128, 1024], F32))
                mdbg = ms.enter_context(nc.sbuf_tensor("minidbg", [128, 512], F32))
                lt = P.op("sync", lambda s: s.dma_start(out=mt[:], in_=d_in["x"][0:128, :]), inc="dm", k=16)
                P.waitt("vector", lt)
                ct = P.op("vector", lambda v: v.tensor_copy(mdbg[:], mt[:, 0:512]), inc="dve")
                P.waitt("gpsimd", ct)
                st1 = P.op("gpsimd", lambda g: g.dma_start(out=y_out[0:128, 0:512], in_=mdbg[:]), inc="dm", k=16)
                st2 = P.op("gpsimd", lambda g: g.dma_start(out=y_out[128:256, :], in_=mt[:]), inc="dm", k=16)
                P.wait("gpsimd", "dm")
            P.barrier()
            with nc.Block() as block:
                block.tensor(lambda e: [f(e) for f in P.stream["tensor"]])
                block.vector(lambda e: [f(e) for f in P.stream["vector"]])
                block.scalar(lambda e: [f(e) for f in P.stream["scalar"]])
                block.gpsimd(lambda e: [f(e) for f in P.stream["gpsimd"]])
                block.sync(lambda e: [f(e) for f in P.stream["sync"]])
            return nc
        if stage == 5:
            emit_ffn(5, dbg_dump=True)
            with ExitStack() as cs:
                relx = cs.enter_context(nc.sbuf_tensor("relx", [128, 1024], F32))
                rl = P.op("sync", lambda s: s.dma_start(out=relx[:], in_=d_in["x"][0:128, :]), inc="dm", k=16)
                P.waitt("gpsimd", rl)
                P.op("gpsimd", lambda g: g.dma_start(out=y_out[640:768, :], in_=relx[:]), inc="dm", k=16)
                free = None
                for (buf, row) in [(dbg_xq, 0), (dbg_vq, 128), (dbg_prod, 256), (dbg_wg, 384), (dbg_sc, 512)]:
                    w = buf.shape[1] if hasattr(buf, 'shape') else 1024
                    f1 = P.op("gpsimd", lambda g, buf=buf, row=row: g.dma_start(
                        out=y_out[row:row + 128, 0:buf.shape[1]], in_=buf[:]), inc="dm", k=16)
                P.wait("gpsimd", "dm")
            P.barrier()
            with nc.Block() as block:
                block.tensor(lambda e: [f(e) for f in P.stream["tensor"]])
                block.vector(lambda e: [f(e) for f in P.stream["vector"]])
                block.scalar(lambda e: [f(e) for f in P.stream["scalar"]])
                block.gpsimd(lambda e: [f(e) for f in P.stream["gpsimd"]])
                block.sync(lambda e: [f(e) for f in P.stream["sync"]])
            return nc
        if stage == 0:
            emit_dump_prep()
            with nc.Block() as block:
                block.tensor(lambda e: [f(e) for f in P.stream["tensor"]])
                block.vector(lambda e: [f(e) for f in P.stream["vector"]])
                block.scalar(lambda e: [f(e) for f in P.stream["scalar"]])
                block.gpsimd(lambda e: [f(e) for f in P.stream["gpsimd"]])
                block.sync(lambda e: [f(e) for f in P.stream["sync"]])
            return nc
        emit_ffn(0 if stage > 1 else 2)
        if stage >= 2:
            emit_attn()
            if stage == 2:
                if paranoid != 30:
                    emit_copy_out()
                else:
                    P.barrier()
            else:
                emit_ffn(1)
        P.barrier()

        with nc.Block() as block:
            block.tensor(lambda e: [f(e) for f in P.stream["tensor"]])
            block.vector(lambda e: [f(e) for f in P.stream["vector"]])
            block.scalar(lambda e: [f(e) for f in P.stream["scalar"]])
            block.gpsimd(lambda e: [f(e) for f in P.stream["gpsimd"]])
            block.sync(lambda e: [f(e) for f in P.stream["sync"]])
    return nc


def kernel(L0_attn_x, L0_attn_wq, L0_attn_bq, L0_attn_wqc, L0_attn_bqc,
           L0_attn_wqr, L0_attn_bqr, L0_attn_wkv, L0_attn_bkv, L0_attn_wkr,
           L0_attn_bkr, L0_attn_wkc, L0_attn_bkc, L0_attn_wv, L0_attn_bv,
           L0_attn_wo, L0_attn_bo, L0_ffn_w_gate, L0_ffn_w_up, L0_ffn_w_down):
    f32 = lambda a: np.ascontiguousarray(np.asarray(a, dtype=np.float32))
    x = f32(L0_attn_x).reshape(16, 2048, 1024)
    wq, bq = f32(L0_attn_wq), f32(L0_attn_bq)
    wqc, bqc = f32(L0_attn_wqc), f32(L0_attn_bqc)
    wqr, bqr = f32(L0_attn_wqr), f32(L0_attn_bqr)
    wkv, bkv = f32(L0_attn_wkv), f32(L0_attn_bkv)
    wkr, bkr = f32(L0_attn_wkr), f32(L0_attn_bkr)
    wkc, bkc = f32(L0_attn_wkc), f32(L0_attn_bkc)
    wv, bv = f32(L0_attn_wv), f32(L0_attn_bv)
    wo, bo = f32(L0_attn_wo), f32(L0_attn_bo)
    wg, wu, wd = f32(L0_ffn_w_gate), f32(L0_ffn_w_up), f32(L0_ffn_w_down)

    wlat = np.ascontiguousarray(np.concatenate([wq, wkv, wqr, wkr], axis=1))
    blat_vec = np.concatenate([bq, bkv, bqr, bkr])
    blat_p = np.ascontiguousarray(blat_vec.reshape(6, 128).T)
    bqc_pp = np.ascontiguousarray(bqc.reshape(7, 128).T)
    bkc_pp = np.ascontiguousarray(bkc.reshape(7, 128).T)
    bv_b = np.ascontiguousarray(np.broadcast_to(bv, (128, 1024)))
    bo_b = np.ascontiguousarray(np.broadcast_to(bo, (128, 1024)))

    inv = 1.0 / (10000.0 ** (np.arange(0, DR, 2, dtype=np.float64) / DR))
    t = np.arange(S, dtype=np.float64)
    freqs = t[:, None] * inv[None, :]
    emb = np.concatenate([freqs, freqs], axis=1)
    cosT = np.ascontiguousarray(np.cos(emb).T.astype(np.float32))
    sinT = np.ascontiguousarray(np.sin(emb).T.astype(np.float32))

    nc = build_program()
    common = dict(wg=wg, wu=wu, wd=wd, wlat=wlat, wqc=wqc, wkc=wkc, wv=wv, wo=wo,
                  blat_p=blat_p, bqc_pp=bqc_pp, bkc_pp=bkc_pp, bv_b=bv_b, bo_b=bo_b,
                  cosT=cosT, sinT=sinT)
    in_maps = []
    for c in range(N_CORES):
        m = dict(common)
        m["x"] = np.ascontiguousarray(x[2 * c:2 * c + 2].reshape(T, E))
        in_maps.append(m)
    res = run_bass_kernel_spmd(nc, in_maps, core_ids=list(range(N_CORES)))
    out = np.empty((16, 2048, 1024), dtype=np.float32)
    for c in range(N_CORES):
        out[2 * c:2 * c + 2] = np.asarray(res.results[c]["y"]).reshape(2, 2048, 1024)
    return out
